# version 11
# baseline (speedup 1.0000x reference)
import numpy as np
import ml_dtypes
import concourse.bass as bass
import concourse.mybir as mybir
from concourse.bass_utils import run_bass_kernel_spmd
from contextlib import ExitStack

F32 = mybir.dt.float32
BF16 = mybir.dt.bfloat16
AF = mybir.ActivationFunctionType
ALU = mybir.AluOpType
AX = mybir.AxisListType

LN_EPS = 1e-5
Q_EPS = 1e-5
RACE_CHECK = False


class CFG:
    def __init__(self, dm=1024, df=4096, di=2048, ds=16, dr=64, nb=2, nq=4,
                 Lq=4096, CH=512, TC=1024):
        self.dm, self.df, self.di, self.ds, self.dr = dm, df, di, ds, dr
        self.nb, self.nq = nb, nq
        self.nc = nb * nq
        self.Lq, self.CH, self.TC = Lq, CH, TC
        self.L = Lq * nq
        self.dpc = di // self.nc
        assert self.dpc % 128 == 0
        self.ndblk = self.dpc // 128


FULL = CFG()
MINI = CFG(dm=256, df=512, di=256, ds=16, dr=64, nb=1, nq=2, Lq=128, CH=64, TC=64)


class Sem:
    __slots__ = ("h", "v")

    def __init__(self, h):
        self.h = h
        self.v = 0

    def inc(self, instr, k=1):
        instr.then_inc(self.h, k)
        self.v += k


def W(eng, *sems):
    for s in sems:
        if s.v > 0:
            eng.wait_ge(s.h, s.v)


def chn(p, e, eng, ins):
    s = p.sem(_ENG_SEM[eng])
    s.inc(ins)
    e.wait_ge(s.h, s.v)


class Prog:
    def __init__(self, name):
        self.nc = bass.Bass("TRN2", target_bir_lowering=False,
                            detect_race_conditions=RACE_CHECK)
        self.ctx = ExitStack()
        self.name = name
        self.sems = {}

    def sem(self, name):
        if name not in self.sems:
            self.sems[name] = Sem(self.ctx.enter_context(self.nc.semaphore(name)))
        return self.sems[name]

    def sb(self, name, shape, dt, st=None):
        return (st or self.ctx).enter_context(
            self.nc.sbuf_tensor(name, list(shape), dt))

    def ps(self, name, shape, st=None):
        return (st or self.ctx).enter_context(
            self.nc.psum_tensor(name, list(shape), F32))


_ENG_SEM = {"vector": "VC", "scalar": "SC", "tensor": "TC", "gpsimd": "GC",
            "sync": None}


def _mkbody(p, blk, eng, fn):
    nm = _ENG_SEM[eng]
    s = p.sem(nm) if nm else None

    def wrapped(e):
        if s is not None and s.v > 0:
            e.wait_ge(s.h, s.v)
        fn(e)
    getattr(blk, eng)(wrapped)



_QCNT = 0


def emit_load_weight_cast(p, blk, dram_ap, dst_tiles, scratch, rows, cols, sLD):
    nk = rows // 128
    for k in range(nk):
        par = k % len(scratch)

        def fy(e, k=k, par=par):
            W(e, p.sem("VC"))
            ins = e.dma_start(out=scratch[par][:, :cols],
                              in_=dram_ap[k * 128:(k + 1) * 128, :])
            sLD.inc(ins, 16)
        _mkbody(p, blk, "sync", fy)

        def fv(e, k=k, par=par):
            W(e, sLD)
            ins = e.tensor_copy(out=dst_tiles[k][:, :cols], in_=scratch[par][:, :cols])
            p.sem("VC").inc(ins)
        _mkbody(p, blk, "vector", fv)


def emit_quantize(p, blk, dram_ap, dst_tiles, scratch, rows, cols, sLD, consts,
                  tag):
    nk = rows // 128
    
    redsum = consts["redsum"]
    gsb = consts["gsb"]
    ginv = consts["ginv_" + tag]
    onesall = consts["onesall"]
    ps_s = consts["ps_small"]
    global _QCNT
    _QCNT += 1
    qst = ExitStack()
    t1 = p.sb("qt1_%d" % _QCNT, [128, cols], F32, qst)
    t2 = p.sb("qt2_%d" % _QCNT, [128, cols], F32, qst)

    def fvz(e):
        ins = e.memset(redsum[:, :], 0.0)
        p.sem("VC").inc(ins)
    _mkbody(p, blk, "vector", fvz)

    for k in range(nk):
        par = k % len(scratch)

        def fy(e, k=k, par=par):
            W(e, p.sem("VC"))
            ins = e.dma_start(out=scratch[par][:, :cols],
                              in_=dram_ap[k * 128:(k + 1) * 128, :])
            sLD.inc(ins, 16)
        _mkbody(p, blk, "sync", fy)

        def fv(e, k=k, par=par):
            W(e, sLD)
            ins = e.tensor_reduce(out=redsum[:, k:k + 1], in_=scratch[par][:, :cols],
                                  axis=AX.X, op=ALU.add, apply_absolute_value=True)
            p.sem("VC").inc(ins)
        _mkbody(p, blk, "vector", fv)

    ncols = redsum.shape[1]
    rhi = consts["x2"][0]
    rlo = consts["x2"][1]
    rdf = consts["vscr"]

    def fvhl(e):
        ins = e.tensor_copy(out=rhi[:, :ncols], in_=redsum[:, :ncols])
        chn(p, e, "vector", ins)
        ins = e.tensor_tensor(out=rdf[:, :ncols], in0=redsum[:, :ncols],
                              in1=rhi[:, :ncols], op=ALU.subtract)
        chn(p, e, "vector", ins)
        ins = e.tensor_copy(out=rlo[:, :ncols], in_=rdf[:, :ncols])
        p.sem("VC").inc(ins)
    _mkbody(p, blk, "vector", fvhl)

    def ft(e):
        W(e, p.sem("VC"))
        e.matmul(ps_s[:, :ncols], onesall[:, :], rhi[:, :ncols],
                 start=True, stop=False)
        ins = e.matmul(ps_s[:, :ncols], onesall[:, :], rlo[:, :ncols],
                       start=False, stop=True)
        p.sem("TC").inc(ins)
    _mkbody(p, blk, "tensor", ft)

    def fv4(e):
        W(e, p.sem("TC"))
        ins = e.tensor_reduce(out=gsb[:, :], in_=ps_s[:, :ncols], axis=AX.X,
                              op=ALU.add)
        chn(p, e, "vector", ins)
        ins = e.tensor_scalar(out=gsb[:, :], in0=gsb[:, :],
                              scalar1=1.0 / (rows * cols),
                              scalar2=Q_EPS, op0=ALU.mult, op1=ALU.add)
        chn(p, e, "vector", ins)
        ins = e.reciprocal(out=ginv[:, :], in_=gsb[:, :])
        p.sem("VC").inc(ins)
    _mkbody(p, blk, "vector", fv4)

    for k in range(nk):
        par = k % len(scratch)

        def fy(e, k=k, par=par):
            W(e, p.sem("VC"))
            ins = e.dma_start(out=scratch[par][:, :cols],
                              in_=dram_ap[k * 128:(k + 1) * 128, :])
            sLD.inc(ins, 16)
        _mkbody(p, blk, "sync", fy)

        def fv(e, k=k, par=par):
            W(e, sLD)
            ins = e.tensor_scalar(out=scratch[par][:, :cols],
                                  in0=scratch[par][:, :cols],
                                  scalar1=ginv[:, 0:1], scalar2=None, op0=ALU.mult)
            chn(p, e, "vector", ins)
            e.tensor_scalar(out=t1[:, :cols], in0=scratch[par][:, :cols],
                            scalar1=0.5, scalar2=None, op0=ALU.is_gt)
            ins = e.tensor_scalar(out=t2[:, :cols], in0=scratch[par][:, :cols],
                                  scalar1=-0.5, scalar2=None, op0=ALU.is_lt)
            chn(p, e, "vector", ins)
            ins = e.tensor_tensor(out=dst_tiles[k][:, :cols], in0=t1[:, :cols],
                                  in1=t2[:, :cols], op=ALU.subtract)
            p.sem("VC").inc(ins)
        _mkbody(p, blk, "vector", fv)
    qst.close()


def Wv(e, sem, val):
    if val > 0:
        e.wait_ge(sem.h, val)


def emit_ln(p, blk, cfg, xb_tiles, out_tiles, N, consts, ready=()):
    nk = cfg.dm // 128
    VC, SC, TCs = p.sem("VC"), p.sem("SC"), p.sem("TC")
    ps_s1 = consts["lnps_s1"]
    ps_s2 = consts["lnps_s2"]
    onesall = consts["onesall"]
    x2 = consts["x2"]
    xbb = consts["xbb"]
    m_rep = consts["m_rep"]
    r_rep = consts["r_rep"]
    ckb = consts["ckb"]

    def fvc(e):
        for s, v in ready:
            Wv(e, s, v)
        for k in range(nk):
            ins = e.tensor_copy(out=xbb[k][:, :N], in_=xb_tiles[k][:, :N])
        VC.inc(ins)
    _mkbody(p, blk, "vector", fvc)
    tick = {"cast": p.sem("VC").v}

    lntmp = consts["lntmp"]
    vscr = consts["vscr"]

    def fsq(e):
        for s, v in ready:
            Wv(e, s, v)
        for k in range(nk):
            ins = e.activation(out=x2[k][:, :N], in_=xb_tiles[k][:, :N],
                               func=AF.Square)
        SC.inc(ins)
    _mkbody(p, blk, "scalar", fsq)
    tick["sq"] = SC.v

    def ft0(e, tc=tick["cast"], ts=tick["sq"]):
        Wv(e, VC, tc)
        Wv(e, SC, ts)
        for k in range(nk):
            e.matmul(ps_s1[:, :N], onesall[:, :], xbb[k][:, :N],
                     start=(k == 0), stop=(k == nk - 1))
        for k in range(nk):
            ins = e.matmul(ps_s2[:, :N], onesall[:, :], x2[k][:, :N],
                           start=(k == 0), stop=(k == nk - 1))
        TCs.inc(ins)
    _mkbody(p, blk, "tensor", ft0)

    def fv1(e):
        W(e, TCs)
        e.tensor_scalar(out=m_rep[:, :N], in0=ps_s1[:, :N], scalar1=1.0 / cfg.dm,
                        scalar2=None, op0=ALU.mult)
        ins = e.tensor_scalar(out=lntmp[:, :N], in0=ps_s2[:, :N],
                              scalar1=1.0 / cfg.dm, scalar2=None, op0=ALU.mult)
        chn(p, e, "vector", ins)
        ins = e.tensor_tensor(out=vscr[:, :N], in0=m_rep[:, :N],
                              in1=m_rep[:, :N], op=ALU.mult)
        chn(p, e, "vector", ins)
        ins = e.tensor_tensor(out=lntmp[:, :N], in0=lntmp[:, :N],
                              in1=vscr[:, :N], op=ALU.subtract)
        chn(p, e, "vector", ins)
        ins = e.tensor_scalar(out=lntmp[:, :N], in0=lntmp[:, :N], scalar1=0.0,
                              scalar2=None, op0=ALU.max)
        VC.inc(ins)
    _mkbody(p, blk, "vector", fv1)

    def fs2(e):
        W(e, VC)
        ins = e.activation(out=r_rep[:, :N], in_=lntmp[:, :N], func=AF.Ln,
                            bias=ckb[:, CK_EPS:CK_EPS + 1])
        chn(p, e, "scalar", ins)
        ins = e.activation(out=r_rep[:, :N], in_=r_rep[:, :N], func=AF.Exp,
                           scale=ckb[:, CK_NHALF:CK_NHALF + 1])
        SC.inc(ins)
    _mkbody(p, blk, "scalar", fs2)

    def fv3(e):
        W(e, SC)
        tmp = consts["lntmp"]
        for k in range(nk):
            ins = e.tensor_tensor(out=tmp[:, :N], in0=xb_tiles[k][:, :N],
                                  in1=m_rep[:, :N], op=ALU.subtract)
            chn(p, e, "vector", ins)
            ins = e.tensor_tensor(out=out_tiles[k][:, :N], in0=tmp[:, :N],
                                  in1=r_rep[:, :N], op=ALU.mult)
            chn(p, e, "vector", ins)
    _mkbody(p, blk, "vector", fv3)
    tick["apply"] = p.sem("VC").v
    return tick


def emit_ffn(p, blk, cfg, x_dram, out_dram, h_dram, w1t_in, w2t_in, chunks,
             consts, tag, resid_scale=0.5, wq_src=None, wq_out=None):
    nc = p.nc
    dm, df = cfg.dm, cfg.df
    CH = cfg.CH
    nkm, nkf = dm // 128, df // 128
    VC, SC, TCs = p.sem("VC"), p.sem("SC"), p.sem("TC")
    sLDx = p.sem("sLDx" + tag)
    sLDh = p.sem("sLDh" + tag)
    sSThp = [p.sem("sSTh%s_%d" % (tag, i)) for i in range(3)]
    sSTop = [p.sem("sSTo%s_%d" % (tag, i)) for i in range(3)]
    sLDw = p.sem("sLDw")
    b1sb, b2h = consts["b1sb"], consts["b2h"]

    sSTq = p.sem("sSTq" + tag) if wq_out is not None else None
    with ExitStack() as sta:
        w1q = [p.sb("w1qA%s_%d" % (tag, k), [128, df], BF16, sta) for k in range(nkm)]
        if wq_src is None:
            wscr = [p.sb("wscA%s_%d" % (tag, i), [128, df], F32, sta)
                    for i in range(2)]
            emit_quantize(p, blk, w1t_in, w1q, wscr, dm, df, sLDw, consts, "w1")
        else:
            def fyq(e):
                W(e, VC)
                for k in range(nkm):
                    ins = e.dma_start(out=w1q[k][:, :],
                                      in_=wq_src[0][k * 128:(k + 1) * 128, :])
                    sLDw.inc(ins, 16)
            _mkbody(p, blk, "sync", fyq)

            def fvq(e):
                W(e, sLDw)
                VC.inc(e.nop())
            _mkbody(p, blk, "vector", fvq)
        qdA = VC.v
        if wq_out is not None:
            def fgq(e, qd=qdA):
                Wv(e, VC, qd)
                for k in range(nkm):
                    ins = e.dma_start(out=wq_out[0][k * 128:(k + 1) * 128, :],
                                      in_=w1q[k][:, :])
                    sSTq.inc(ins, 16)
            _mkbody(p, blk, "gpsimd", fgq)
        xb = [p.sb("xbA%s_%d" % (tag, k), [128, CH], F32, sta) for k in range(nkm)]
        lnb = [p.sb("lnA%s_%d" % (tag, k), [128, CH], BF16, sta) for k in range(nkm)]
        hb = [p.sb("hbA%s_%d" % (tag, i), [128, CH], BF16, sta) for i in range(3)]
        ps_f = [p.ps("psA%s_%d" % (tag, i), [128, CH], sta) for i in range(3)]

        evbufA = {0: 0, 1: 0, 2: 0}
        stbufA = {0: 0, 1: 0, 2: 0}
        prev = None
        for (off, N) in chunks:
            def fy(e, off=off, N=N, prev=prev):
                if prev is not None:
                    Wv(e, VC, prev["apply"])
                    Wv(e, SC, prev["sq"])
                else:
                    Wv(e, VC, qdA)
                for k in range(nkm):
                    ins = e.dma_start(out=xb[k][:, :N],
                                      in_=x_dram[k * 128:(k + 1) * 128, off:off + N])
                    sLDx.inc(ins, 16)
            _mkbody(p, blk, "sync", fy)
            ldt = sLDx.v

            prev = emit_ln(p, blk, cfg, xb, lnb, N, consts,
                           ready=((sLDx, ldt),))

            for m in range(nkf):
                def ft(e, m=m, N=N, at=prev["apply"], ev=evbufA[m % 3]):
                    Wv(e, VC, max(at, ev))
                    for k in range(nkm):
                        ins = e.matmul(ps_f[m % 3][:, :N],
                                       w1q[k][:, m * 128:(m + 1) * 128],
                                       lnb[k][:, :N], start=(k == 0),
                                       stop=(k == nkm - 1))
                    TCs.inc(ins)
                _mkbody(p, blk, "tensor", ft)
                mmt = TCs.v

                def fv(e, m=m, N=N, mmt=mmt, st=stbufA[m % 3]):
                    Wv(e, TCs, mmt)
                    Wv(e, sSThp[m % 3], st)
                    ins = e.tensor_scalar(out=hb[m % 3][:, :N],
                                          in0=ps_f[m % 3][:, :N],
                                          scalar1=b1sb[:, m:m + 1], scalar2=0.0,
                                          op0=ALU.add, op1=ALU.max)
                    VC.inc(ins)
                _mkbody(p, blk, "vector", fv)
                evbufA[m % 3] = VC.v

                def fg(e, m=m, off=off, N=N, ev=evbufA[m % 3]):
                    Wv(e, VC, ev)
                    ins = e.dma_start(out=h_dram[m * 128:(m + 1) * 128, off:off + N],
                                      in_=hb[m % 3][:, :N])
                    sSThp[m % 3].inc(ins, 16)
                _mkbody(p, blk, "gpsimd", fg)
                stbufA[m % 3] = sSThp[m % 3].v

    def fvhsync(e):
        W(e, *sSThp)
        if sSTq is not None:
            W(e, sSTq)
        VC.inc(e.nop())
    _mkbody(p, blk, "vector", fvhsync)

    with ExitStack() as stb:
        w2q = [p.sb("w2qB%s_%d" % (tag, k), [128, dm], BF16, stb) for k in range(nkf)]
        if wq_src is None:
            wscr = [p.sb("wscB%s_%d" % (tag, i), [128, max(df, dm)], F32, stb)
                    for i in range(2)]
            emit_quantize(p, blk, w2t_in, w2q, wscr, df, dm, sLDw, consts, "w2")
        else:
            def fyq2(e):
                W(e, VC)
                for k in range(nkf):
                    ins = e.dma_start(out=w2q[k][:, :],
                                      in_=wq_src[1][k * 128:(k + 1) * 128, :])
                    sLDw.inc(ins, 16)
            _mkbody(p, blk, "sync", fyq2)

            def fvq2(e):
                W(e, sLDw)
                VC.inc(e.nop())
            _mkbody(p, blk, "vector", fvq2)
        qdB = VC.v
        if wq_out is not None:
            def fgq2(e, qd=qdB):
                Wv(e, VC, qd)
                for k in range(nkf):
                    ins = e.dma_start(out=wq_out[1][k * 128:(k + 1) * 128, :],
                                      in_=w2q[k][:, :])
                    sSTq.inc(ins, 16)
            _mkbody(p, blk, "gpsimd", fgq2)
        hB = [[p.sb("hBB%s_%d_%d" % (tag, j, k), [128, CH], BF16, stb)
               for k in range(nkf)] for j in range(2)]
        xb = [p.sb("xbB%s_%d" % (tag, k), [128, CH], F32, stb) for k in range(nkm)]
        ps_f = [p.ps("psB%s_%d" % (tag, i), [128, CH], stb) for i in range(3)]
        ot = [p.sb("otB%s_%d" % (tag, i), [128, CH], F32, stb) for i in range(3)]
        sLDxB = p.sem("sLDxB" + tag)
        sLDhp = [p.sem("sLDh%s_%d" % (tag, i)) for i in range(2)]

        evbufB = {0: 0, 1: 0, 2: 0}
        stbufB = {0: 0, 1: 0, 2: 0}
        mm2last = {0: 0, 1: 0}
        evlast = 0
        for ci, (off, N) in enumerate(chunks):
            cpar = ci % 2

            def fy(e, off=off, N=N, ci=ci, cpar=cpar, mml=mm2last[cpar]):
                if ci == 0:
                    Wv(e, VC, qdB)
                    for s in sSThp:
                        e.wait_ge(s.h, s.v)
                else:
                    Wv(e, TCs, mml)
                for k in range(nkf):
                    ins = e.dma_start(out=hB[cpar][k][:, :N],
                                      in_=h_dram[k * 128:(k + 1) * 128, off:off + N])
                    sLDhp[cpar].inc(ins, 16)
            _mkbody(p, blk, "sync", fy)
            ldt = sLDhp[cpar].v

            def fyx(e, off=off, N=N, ci=ci, evlast=evlast):
                if ci == 0:
                    Wv(e, VC, qdB)
                else:
                    Wv(e, VC, evlast)
                for k in range(nkm):
                    ins = e.dma_start(out=xb[k][:, :N],
                                      in_=x_dram[k * 128:(k + 1) * 128, off:off + N])
                    sLDxB.inc(ins, 16)
            _mkbody(p, blk, "sync", fyx)
            ldtx = sLDxB.v

            for m in range(nkm):
                def ft(e, m=m, N=N, ldt=ldt, cpar=cpar, ev=evbufB[m % 3]):
                    Wv(e, sLDhp[cpar], ldt)
                    Wv(e, VC, max(ev, qdB))
                    for k in range(nkf):
                        ins = e.matmul(ps_f[m % 3][:, :N],
                                       w2q[k][:, m * 128:(m + 1) * 128],
                                       hB[cpar][k][:, :N], start=(k == 0),
                                       stop=(k == nkf - 1))
                    TCs.inc(ins)
                _mkbody(p, blk, "tensor", ft)
                mmt = TCs.v

                def fv(e, m=m, N=N, mmt=mmt, ldt=ldtx, st=stbufB[m % 3]):
                    Wv(e, TCs, mmt)
                    Wv(e, sLDxB, ldt)
                    Wv(e, sSTop[m % 3], st)
                    ins = e.tensor_scalar(out=ot[m % 3][:, :N],
                                          in0=ps_f[m % 3][:, :N],
                                          scalar1=resid_scale,
                                          scalar2=b2h[:, m:m + 1],
                                          op0=ALU.mult, op1=ALU.add)
                    chn(p, e, "vector", ins)
                    ins = e.tensor_tensor(out=ot[m % 3][:, :N], in0=ot[m % 3][:, :N],
                                          in1=xb[m][:, :N], op=ALU.add)
                    VC.inc(ins)
                _mkbody(p, blk, "vector", fv)
                evbufB[m % 3] = VC.v

                def fg(e, m=m, off=off, N=N, ev=evbufB[m % 3]):
                    Wv(e, VC, ev)
                    ins = e.dma_start(out=out_dram[m * 128:(m + 1) * 128, off:off + N],
                                      in_=ot[m % 3][:, :N])
                    sSTop[m % 3].inc(ins, 16)
                _mkbody(p, blk, "gpsimd", fg)
                stbufB[m % 3] = sSTop[m % 3].v
            mm2last[cpar] = TCs.v
            evlast = VC.v

    if wq_out is not None:
        def fvqend(e):
            W(e, sSTq)
            VC.inc(e.nop())
        _mkbody(p, blk, "vector", fvqend)


CK_EPS, CK_NHALF, CK_HALF, CK_NONE, CK_ONE = 0, 1, 2, 3, 4


def mk_consts(p, cfg, nkm, nkf):
    consts = {}
    CH = cfg.CH
    consts["ckb"] = p.sb("ckb", [128, 8], F32)
    consts["onesall"] = p.sb("onesall", [128, 128], BF16)
    consts["redsum"] = p.sb("redsum", [128, max(nkm, nkf)], F32)
    consts["gsb"] = p.sb("gsb", [128, 1], F32)
    consts["ginv_w1"] = p.sb("ginv_w1", [128, 1], F32)
    consts["ginv_w2"] = p.sb("ginv_w2", [128, 1], F32)
    consts["ps_small"] = p.ps("ps_small", [128, 128])
    consts["b1sb"] = p.sb("b1sb", [128, nkf], F32)
    consts["b2h"] = p.sb("b2h", [128, nkm], F32)
    consts["x2"] = [p.sb("x2_%d" % i, [128, CH], BF16)
                    for i in range(cfg.dm // 128)]
    consts["xbb"] = [p.sb("xbb_%d" % k, [128, CH], BF16) for k in range(nkm)]
    consts["vscr"] = p.sb("vscr", [128, CH], F32)
    consts["m_rep"] = p.sb("m_rep", [128, CH], F32)
    consts["r_rep"] = p.sb("r_rep", [128, CH], F32)
    consts["lntmp"] = p.sb("lntmp", [128, CH], F32)
    consts["lnps_s1"] = p.ps("lnps_s1", [128, CH])
    consts["lnps_s2"] = p.ps("lnps_s2", [128, CH])
    consts["CHmax"] = CH
    return consts


def emit_const_init(p, blk, consts):
    VC, SC, TCs = p.sem("VC"), p.sem("SC"), p.sem("TC")
    ckb = consts["ckb"]

    def f_init(e):
        e.memset(consts["onesall"][:, :], 1.0)
        e.memset(ckb[:, CK_EPS:CK_EPS + 1], LN_EPS)
        e.memset(ckb[:, CK_NHALF:CK_NHALF + 1], -0.5)
        e.memset(ckb[:, CK_HALF:CK_HALF + 1], 0.5)
        e.memset(ckb[:, CK_NONE:CK_NONE + 1], -1.0)
        ins = e.memset(ckb[:, CK_ONE:CK_ONE + 1], 1.0)
        VC.inc(ins)
    _mkbody(p, blk, "vector", f_init)

    def f_s(e):
        W(e, VC)
        SC.inc(e.nop())
    _mkbody(p, blk, "scalar", f_s)

    def f_t(e):
        W(e, VC)
        TCs.inc(e.nop())
    _mkbody(p, blk, "tensor", f_t)



def build_l1(cfg):
    p = Prog("l1")
    nc = p.nc
    dm, df, di, ds, dr = cfg.dm, cfg.df, cfg.di, cfg.ds, cfg.dr
    CH, Lq = cfg.CH, cfg.Lq
    Ls = Lq + 3
    nkm, nkf, nki = dm // 128, df // 128, di // 128
    ncH = Lq // CH

    x_in = nc.dram_tensor("x_in", [dm, Ls], F32, kind="ExternalInput").ap()
    hmask_in = nc.dram_tensor("hmask_in", [128, 1], F32, kind="ExternalInput").ap()
    w1t_in = nc.dram_tensor("w1t_in", [dm, df], F32, kind="ExternalInput").ap()
    w2t_in = nc.dram_tensor("w2t_in", [df, dm], F32, kind="ExternalInput").ap()
    b1_in = nc.dram_tensor("b1_in", [128, nkf], F32, kind="ExternalInput").ap()
    b2_in = nc.dram_tensor("b2_in", [128, nkm], F32, kind="ExternalInput").ap()
    wint_in = nc.dram_tensor("wint_in", [dm, 2 * di], F32, kind="ExternalInput").ap()
    cw_in = nc.dram_tensor("cw_in", [128, 4 * nki], F32, kind="ExternalInput").ap()
    cb_in = nc.dram_tensor("cb_in", [128, nki], F32, kind="ExternalInput").ap()
    wxt_in = nc.dram_tensor("wxt_in", [di, 128], F32, kind="ExternalInput").ap()
    wdtt_in = nc.dram_tensor("wdtt_in", [dr, di], F32, kind="ExternalInput").ap()
    dtb_in = nc.dram_tensor("dtb_in", [128, nki], F32, kind="ExternalInput").ap()
    eye_in = nc.dram_tensor("eye_in", [128, 128], F32, kind="ExternalInput").ap()

    xp_out = nc.dram_tensor("xp_out", [dm, Ls], F32, kind="ExternalOutput").ap()
    u_out = nc.dram_tensor("u_out", [di, Lq], BF16, kind="ExternalOutput").ap()
    sz_out = nc.dram_tensor("sz_out", [di, Lq], BF16, kind="ExternalOutput").ap()
    delta_out = nc.dram_tensor("delta_out", [di, Lq], F32, kind="ExternalOutput").ap()
    b_out = nc.dram_tensor("b_out", [ds, Lq], F32, kind="ExternalOutput").ap()
    c_out = nc.dram_tensor("c_out", [ds, Lq], F32, kind="ExternalOutput").ap()
    bmin_out = nc.dram_tensor("bmin_out", [ds, 1], F32, kind="ExternalOutput").ap()
    w1q_out = nc.dram_tensor("w1q_out", [dm, df], BF16, kind="ExternalOutput").ap()
    w2q_out = nc.dram_tensor("w2q_out", [df, dm], BF16, kind="ExternalOutput").ap()

    h_sp = nc.dram_tensor("h_sp", [df, Ls], BF16, kind="Internal").ap()
    x_sp = nc.dram_tensor("x_sp", [di, Ls], BF16, kind="Internal").ap()

    chunks = [(0, 3)] + [(3 + i * CH, CH) for i in range(ncH)]

    with p.ctx:
        blk = p.ctx.enter_context(nc.Block())
        VC, SC, TCs = p.sem("VC"), p.sem("SC"), p.sem("TC")
        GC = p.sem("GC")
        sLDw = p.sem("sLDw")
        sLDc = p.sem("sLDc")

        consts = mk_consts(p, cfg, nkm, nkf)
        cwsb = p.sb("cwsb", [128, 4 * nki], F32)
        cbsb = p.sb("cbsb", [128, nki], F32)
        cbh = p.sb("cbh", [128, nki], F32)
        dtbsb = p.sb("dtbsb", [128, nki], F32)
        eyesb = p.sb("eyesb", [128, 128], F32)
        hmsb = p.sb("hmsb", [128, 1], F32)

        emit_const_init(p, blk, consts)

        def f_ldc(e):
            for src, dst in ((b1_in, consts["b1sb"]), (b2_in, consts["b2h"]),
                             (cw_in, cwsb), (cb_in, cbsb), (dtb_in, dtbsb),
                             (eye_in, eyesb), (hmask_in, hmsb)):
                ins = e.dma_start(out=dst[:, :], in_=src)
                sLDc.inc(ins, 16)
        _mkbody(p, blk, "sync", f_ldc)

        def f_small(e):
            W(e, sLDc)
            e.tensor_scalar(out=consts["b2h"][:, :], in0=consts["b2h"][:, :],
                            scalar1=0.5, scalar2=None, op0=ALU.mult)
            ins = e.tensor_scalar(out=cbh[:, :], in0=cbsb[:, :], scalar1=0.5,
                                  scalar2=None, op0=ALU.mult)
            VC.inc(ins)
        _mkbody(p, blk, "vector", f_small)

        emit_ffn(p, blk, cfg, x_in, xp_out, h_sp, w1t_in, w2t_in, chunks,
                 consts, "f1", 0.5, wq_out=(w1q_out, w2q_out))

        sSTof1p = [p.sem("sSTof1_%d" % i) for i in range(3)]
        sLDx2 = p.sem("sLDx2")
        sSTx_p = [p.sem("sSTx__%d" % i) for i in range(3)]
        sSTsz = p.sem("sSTsz")
        with ExitStack() as st2:
            winb = [p.sb("winb_%d" % k, [128, 2 * di], BF16, st2) for k in range(nkm)]
            wscr = [p.sb("w2sc_0", [128, 2 * di], F32, st2)]
            emit_load_weight_cast(p, blk, wint_in, winb, wscr, dm, 2 * di, sLDw)
            xb = [p.sb("xb2_%d" % k, [128, CH], F32, st2) for k in range(nkm)]
            lnb = [p.sb("ln2_%d" % k, [128, CH], BF16, st2) for k in range(nkm)]
            ps_f = [p.ps("ps_w%d" % i, [128, CH], st2) for i in range(3)]
            xt = [p.sb("xt_%d" % i, [128, CH], BF16, st2) for i in range(3)]
            nki_l = cfg.di // 128
            zal = p.sb("zal", [128, nki_l * CH], BF16, st2)
            thal = p.sb("thal", [128, nki_l * CH], BF16, st2)
            szal = p.sb("szal", [128, nki_l * CH], BF16, st2)

            evb2 = {0: 0, 1: 0, 2: 0}
            stx2 = {0: 0, 1: 0, 2: 0}
            ztanh_prev = 0
            stsz_prev = 0
            zsil_prev = 0
            prev2 = None
            for ci, (off, N) in enumerate(chunks):
                def fy(e, off=off, N=N, ci=ci, prev=prev2):
                    if prev is not None:
                        Wv(e, VC, prev["apply"])
                        Wv(e, SC, prev["sq"])
                    else:
                        W(e, SC, VC)
                        for s in sSTof1p:
                            e.wait_ge(s.h, s.v)
                    for k in range(nkm):
                        ins = e.dma_start(out=xb[k][:, :N],
                                          in_=xp_out[k * 128:(k + 1) * 128, off:off + N])
                        sLDx2.inc(ins, 16)
                _mkbody(p, blk, "sync", fy)
                ldt2 = sLDx2.v

                prev2 = emit_ln(p, blk, cfg, xb, lnb, N, consts,
                                ready=((sLDx2, ldt2),))

                m_tiles = nki if off == 0 else 2 * nki
                for m in range(m_tiles):
                    def ft(e, m=m, N=N, at=prev2["apply"], ev=evb2[m % 3]):
                        Wv(e, VC, max(at, ev))
                        for k in range(nkm):
                            ins = e.matmul(ps_f[m % 3][:, :N],
                                           winb[k][:, m * 128:(m + 1) * 128],
                                           lnb[k][:, :N],
                                           start=(k == 0), stop=(k == nkm - 1))
                        TCs.inc(ins)
                    _mkbody(p, blk, "tensor", ft)
                    mmt = TCs.v

                    if m < nki:
                        def fv(e, m=m, N=N, mmt=mmt, st=stx2[m % 3]):
                            Wv(e, TCs, mmt)
                            Wv(e, sSTx_p[m % 3], st)
                            ins = e.tensor_copy(out=xt[m % 3][:, :N],
                                                in_=ps_f[m % 3][:, :N])
                            VC.inc(ins)
                        _mkbody(p, blk, "vector", fv)
                        evb2[m % 3] = VC.v

                        def fg(e, m=m, off=off, N=N, ev=VC.v):
                            Wv(e, VC, ev)
                            ins = e.dma_start(
                                out=x_sp[m * 128:(m + 1) * 128, off:off + N],
                                in_=xt[m % 3][:, :N])
                            sSTx_p[m % 3].inc(ins, 16)
                        _mkbody(p, blk, "gpsimd", fg)
                        stx2[m % 3] = sSTx_p[m % 3].v
                    else:
                        def fv(e, m=m, N=N, mmt=mmt, zt=ztanh_prev,
                               gz=zsil_prev):
                            Wv(e, TCs, mmt)
                            Wv(e, SC, zt)
                            Wv(e, GC, gz)
                            mi = m - nki
                            ins = e.tensor_copy(out=zal[:, mi * CH:mi * CH + N],
                                                in_=ps_f[m % 3][:, :N])
                            VC.inc(ins)
                        _mkbody(p, blk, "vector", fv)
                        evb2[m % 3] = VC.v

                if m_tiles > nki:
                    def fs(e, N=N, ev=VC.v, gz=zsil_prev):
                        Wv(e, VC, ev)
                        Wv(e, GC, gz)
                        ins = e.activation(out=thal[:, :], in_=zal[:, :],
                                           func=AF.Tanh,
                                           scale=consts["ckb"][:, CK_HALF:CK_HALF + 1])
                        SC.inc(ins)
                    _mkbody(p, blk, "scalar", fs)
                    ztanh_prev = SC.v

                    def fgz(e, off=off, N=N, sct=ztanh_prev, st=stsz_prev):
                        Wv(e, SC, sct)
                        Wv(e, sSTsz, st)
                        ins = e.tensor_scalar(out=thal[:, :], in0=thal[:, :],
                                              scalar1=0.5, scalar2=0.5,
                                              op0=ALU.mult, op1=ALU.add)
                        chn(p, e, "gpsimd", ins)
                        ins = e.tensor_tensor(out=szal[:, :], in0=zal[:, :],
                                              in1=thal[:, :], op=ALU.mult)
                        chn(p, e, "gpsimd", ins)
                        for mi in range(nki):
                            ins = e.dma_start(
                                out=sz_out[mi * 128:(mi + 1) * 128,
                                           off - 3:off - 3 + N],
                                in_=szal[:, mi * CH:mi * CH + N])
                            sSTsz.inc(ins, 16)
                        GC.inc(e.nop())
                    _mkbody(p, blk, "gpsimd", fgz)
                    stsz_prev = sSTsz.v
                    zsil_prev = p.sem("GC").v

        sLDxw = p.sem("sLDxw")
        sSTu = p.sem("sSTu")
        st2_sc, st2_gc = SC.v, GC.v
        st2_sts = p.sem("sSTsz").v

        def fvt23(e, sc=st2_sc, gc=st2_gc, sts=st2_sts):
            Wv(e, SC, sc)
            Wv(e, GC, gc)
            Wv(e, p.sem("sSTsz"), sts)
            VC.inc(e.nop())
        _mkbody(p, blk, "vector", fvt23)

        with ExitStack() as st3:
            diag = [[p.sb("dg_%d_%d" % (j, k), [128, 128], BF16, st3)
                     for k in range(nki)] for j in range(4)]
            xw = [p.sb("xw_%d" % k, [128, CH + 3], BF16, st3) for k in range(nki)]
            ps_c = [p.ps("ps_c%d" % i, [128, CH], st3) for i in range(2)]
            ual = p.sb("ual", [128, nki * CH], BF16, st3)
            thal3 = p.sb("thal3", [128, nki * CH], BF16, st3)
            utal = p.sb("utal", [128, nki * CH], BF16, st3)

            def fdg(e):
                W(e, sLDc)
                for j in range(4):
                    for k in range(nki):
                        ins = e.tensor_scalar(
                            out=diag[j][k][:, :], in0=eyesb[:, :],
                            scalar1=cwsb[:, j * nki + k:j * nki + k + 1],
                            scalar2=None, op0=ALU.mult)
                VC.inc(ins)
            _mkbody(p, blk, "vector", fdg)

            evc3 = {0: 0, 1: 0}
            mm3_prev = 0
            tanh3_prev = 0
            stu_prev = 0
            usil_prev = 0
            dgt = VC.v
            for c in range(ncH):
                off = c * CH

                def fy(e, off=off, c=c, mmp=mm3_prev):
                    if c == 0:
                        W(e, TCs, VC)
                        for s in sSTx_p:
                            e.wait_ge(s.h, s.v)
                    else:
                        Wv(e, TCs, mmp)
                    for k in range(nki):
                        ins = e.dma_start(
                            out=xw[k][:, :],
                            in_=x_sp[k * 128:(k + 1) * 128, off:off + CH + 3])
                        sLDxw.inc(ins, 16)
                _mkbody(p, blk, "sync", fy)
                ldt3 = sLDxw.v

                if c == 0:
                    def fvm(e):
                        W(e, sLDxw)
                        for k in range(nki):
                            ins = e.tensor_scalar(out=xw[k][:, 0:3], in0=xw[k][:, 0:3],
                                                  scalar1=hmsb[:, 0:1], scalar2=None,
                                                  op0=ALU.mult)
                        VC.inc(ins)
                    _mkbody(p, blk, "vector", fvm)
                    dgt = VC.v

                for k in range(nki):
                    def ft(e, k=k, ldt=ldt3, wt=max(dgt, evc3[k % 2])):
                        Wv(e, sLDxw, ldt)
                        Wv(e, VC, wt)
                        for j in range(4):
                            ins = e.matmul(ps_c[k % 2][:, :CH], diag[j][k][:, :],
                                           xw[k][:, j:j + CH],
                                           start=(j == 0), stop=(j == 3))
                        TCs.inc(ins)
                    _mkbody(p, blk, "tensor", ft)
                    mmt = TCs.v

                    def fv(e, k=k, mmt=mmt, zt=tanh3_prev, gz=usil_prev):
                        Wv(e, TCs, mmt)
                        Wv(e, SC, zt)
                        Wv(e, GC, gz)
                        ins = e.tensor_scalar(out=ual[:, k * CH:(k + 1) * CH],
                                              in0=ps_c[k % 2][:, :CH],
                                              scalar1=cbsb[:, k:k + 1],
                                              scalar2=None, op0=ALU.add)
                        VC.inc(ins)
                    _mkbody(p, blk, "vector", fv)
                    evc3[k % 2] = VC.v
                mm3_prev = TCs.v

                def fs(e, ev=VC.v, gz=usil_prev):
                    Wv(e, VC, ev)
                    Wv(e, GC, gz)
                    ins = e.activation(out=thal3[:, :], in_=ual[:, :],
                                       func=AF.Tanh,
                                       scale=consts["ckb"][:, CK_HALF:CK_HALF + 1])
                    SC.inc(ins)
                _mkbody(p, blk, "scalar", fs)
                tanh3_prev = SC.v

                def fg(e, off=off, sct=tanh3_prev, st=stu_prev):
                    Wv(e, SC, sct)
                    Wv(e, sSTu, st)
                    ins = e.tensor_scalar(out=thal3[:, :], in0=thal3[:, :],
                                          scalar1=0.5, scalar2=0.5,
                                          op0=ALU.mult, op1=ALU.add)
                    chn(p, e, "gpsimd", ins)
                    ins = e.tensor_tensor(out=utal[:, :], in0=ual[:, :],
                                          in1=thal3[:, :], op=ALU.mult)
                    chn(p, e, "gpsimd", ins)
                    for k in range(nki):
                        ins = e.dma_start(
                            out=u_out[k * 128:(k + 1) * 128, off:off + CH],
                            in_=utal[:, k * CH:(k + 1) * CH])
                        sSTu.inc(ins, 16)
                    GC.inc(e.nop())
                _mkbody(p, blk, "gpsimd", fg)
                stu_prev = sSTu.v
                usil_prev = p.sem("GC").v

        sLDu = p.sem("sLDu")
        sSTd = p.sem("sSTd")
        sSTbc = p.sem("sSTbc")
        dxw = 128
        st3_sc, st3_gc = SC.v, GC.v
        st3_stu = p.sem("sSTu").v

        def fvt34(e, sc=st3_sc, gc=st3_gc, stu=st3_stu):
            Wv(e, SC, sc)
            Wv(e, GC, gc)
            Wv(e, p.sem("sSTu"), stu)
            VC.inc(e.nop())
        _mkbody(p, blk, "vector", fvt34)

        with ExitStack() as st4:
            wxb = [p.sb("wxb_%d" % k, [128, dxw], BF16, st4) for k in range(nki)]
            wdtb = p.sb("wdtb", [128, di], BF16, st4)
            wscr4 = [p.sb("w4sc_%d" % i, [128, max(dxw, di)], F32, st4)
                     for i in range(2)]
            emit_load_weight_cast(p, blk, wxt_in, wxb, wscr4, di, dxw, sLDw)

        
            def fyw(e):
                W(e, p.sem("VC"))
                ins = e.dma_start(out=wscr4[0][0:dr, :di], in_=wdtt_in)
                sLDw.inc(ins, 16)
            _mkbody(p, blk, "sync", fyw)

            def fvw(e):
                W(e, sLDw)
                ins = e.memset(wdtb[:, :], 0.0)
                chn(p, e, "vector", ins)
                ins = e.tensor_copy(out=wdtb[0:dr, :], in_=wscr4[0][0:dr, :di])
                p.sem("VC").inc(ins)
            _mkbody(p, blk, "vector", fvw)

            ub = [p.sb("ub_%d" % k, [128, CH], BF16, st4) for k in range(nki)]
            ps_x = p.ps("ps_x", [dxw, CH], st4)
            ps_d = [p.ps("ps_d%d" % i, [128, CH], st4) for i in range(2)]
            dtr = p.sb("dtr", [128, CH], BF16, st4)
            Bt = p.sb("Bt", [ds, CH], F32, st4)
            Ct = p.sb("Ct", [ds, CH], F32, st4)
            chmin = p.sb("chmin", [ds, 1], F32, st4)
            bmin_run = p.sb("bmin_run", [ds, 1], F32, st4)
            dtal = p.sb("dtal", [128, nki * CH], F32, st4)
            spl = p.sb("spl", [128, nki * CH], F32, st4)

            def fv0(e):
                e.memset(bmin_run[:, :], 3.0e38)
                ins = e.memset(dtr[:, :], 0.0)
                VC.inc(ins)
            _mkbody(p, blk, "vector", fv0)

            mm4_prev = 0
            dtr_prev = 0
            bc_prev = 0
            stbc_prev = 0
            spl_prev = 0
            std_prev = 0
            dsil_prev = 0
            evd4 = {0: 0, 1: 0}
            wrdy = VC.v
            for c in range(ncH):
                off = c * CH

                def fy(e, off=off, c=c, mmp=mm4_prev):
                    if c == 0:
                        W(e, TCs)
                        e.wait_ge(sSTu.h, sSTu.v)
                    else:
                        Wv(e, TCs, mmp)
                    for k in range(nki):
                        ins = e.dma_start(out=ub[k][:, :],
                                          in_=u_out[k * 128:(k + 1) * 128, off:off + CH])
                        sLDu.inc(ins, 16)
                _mkbody(p, blk, "sync", fy)
                ldt4 = sLDu.v

                def ft(e, ldt=ldt4, sct=dtr_prev, vct=max(wrdy, bc_prev)):
                    Wv(e, sLDu, ldt)
                    Wv(e, SC, sct)
                    Wv(e, VC, vct)
                    for k in range(nki):
                        ins = e.matmul(ps_x[:, :CH], wxb[k][:, :], ub[k][:, :],
                                       start=(k == 0), stop=(k == nki - 1))
                    TCs.inc(ins)
                _mkbody(p, blk, "tensor", ft)
                mmx = TCs.v
                mm4_prev = mmx

                def fs(e, mmt=mmx):
                    Wv(e, TCs, mmt)
                    ins = e.activation(out=dtr[0:dr, :], in_=ps_x[0:dr, :CH],
                                       func=AF.Identity)
                    SC.inc(ins)
                _mkbody(p, blk, "scalar", fs)
                dtr_prev = SC.v

                def fv(e, mmt=mmx, st=stbc_prev):
                    Wv(e, TCs, mmt)
                    Wv(e, sSTbc, st)
                    e.tensor_copy(out=Bt[:, :], in_=ps_x[64:64 + ds, :CH])
                    ins = e.tensor_copy(out=Ct[:, :], in_=ps_x[96:96 + ds, :CH])
                    chn(p, e, "vector", ins)
                    ins = e.tensor_reduce(out=chmin[:, :], in_=Bt[:, :], axis=AX.X,
                                          op=ALU.min)
                    chn(p, e, "vector", ins)
                    ins = e.tensor_tensor(out=bmin_run[:, :], in0=bmin_run[:, :],
                                          in1=chmin[:, :], op=ALU.min)
                    VC.inc(ins)
                _mkbody(p, blk, "vector", fv)
                bc_prev = VC.v

                def fg(e, off=off, ev=VC.v):
                    Wv(e, VC, ev)
                    ins = e.dma_start(out=b_out[:, off:off + CH], in_=Bt[:, :])
                    sSTbc.inc(ins, 16)
                    ins = e.dma_start(out=c_out[:, off:off + CH], in_=Ct[:, :])
                    sSTbc.inc(ins, 16)
                _mkbody(p, blk, "gpsimd", fg)
                stbc_prev = sSTbc.v

                for m in range(nki):
                    def ft2(e, m=m, sct=dtr_prev, ev=evd4[m % 2]):
                        Wv(e, SC, sct)
                        Wv(e, VC, max(ev, wrdy))
                        ins = e.matmul(ps_d[m % 2][:, :CH],
                                       wdtb[:, m * 128:(m + 1) * 128], dtr[:, :],
                                       start=True, stop=True)
                        TCs.inc(ins)
                    _mkbody(p, blk, "tensor", ft2)
                    mmt2 = TCs.v

                    def fv2(e, m=m, mmt=mmt2, sct=spl_prev, gz=dsil_prev,
                            st=std_prev):
                        Wv(e, TCs, mmt)
                        Wv(e, SC, sct)
                        Wv(e, GC, gz)
                        Wv(e, sSTd, st)
                        ins = e.tensor_scalar(out=dtal[:, m * CH:(m + 1) * CH],
                                              in0=ps_d[m % 2][:, :CH],
                                              scalar1=dtbsb[:, m:m + 1],
                                              scalar2=None, op0=ALU.add)
                        VC.inc(ins)
                    _mkbody(p, blk, "vector", fv2)
                    evd4[m % 2] = VC.v

                def fs2(e, ev=VC.v, gz=dsil_prev):
                    Wv(e, VC, ev)
                    Wv(e, GC, gz)
                    ckb = consts["ckb"]
                    ins = e.activation(out=spl[:, :], in_=dtal[:, :], func=AF.Abs)
                    chn(p, e, "scalar", ins)
                    ins = e.activation(out=spl[:, :], in_=spl[:, :], func=AF.Exp,
                                       scale=ckb[:, CK_NONE:CK_NONE + 1])
                    chn(p, e, "scalar", ins)
                    ins = e.activation(out=spl[:, :], in_=spl[:, :], func=AF.Ln,
                                       bias=ckb[:, CK_ONE:CK_ONE + 1])
                    chn(p, e, "scalar", ins)
                    ins = e.activation(out=dtal[:, :], in_=dtal[:, :], func=AF.Relu)
                    SC.inc(ins)
                _mkbody(p, blk, "scalar", fs2)
                spl_prev = SC.v

                def fg2(e, off=off, sct=spl_prev, st=std_prev):
                    Wv(e, SC, sct)
                    Wv(e, sSTd, st)
                    ins = e.tensor_tensor(out=dtal[:, :], in0=dtal[:, :],
                                          in1=spl[:, :], op=ALU.add)
                    chn(p, e, "gpsimd", ins)
                    ins = e.tensor_scalar(out=dtal[:, :], in0=dtal[:, :],
                                          scalar1=1e-4, scalar2=20.0,
                                          op0=ALU.max, op1=ALU.min)
                    chn(p, e, "gpsimd", ins)
                    for m in range(nki):
                        ins = e.dma_start(
                            out=delta_out[m * 128:(m + 1) * 128, off:off + CH],
                            in_=dtal[:, m * CH:(m + 1) * CH])
                        sSTd.inc(ins, 16)
                    GC.inc(e.nop())
                _mkbody(p, blk, "gpsimd", fg2)
                std_prev = sSTd.v
                dsil_prev = p.sem("GC").v

            def fgend(e):
                W(e, VC)
                ins = e.dma_start(out=bmin_out, in_=bmin_run[:, :])
                sSTd.inc(ins, 16)
                for nm in ("sSTd", "sSTbc", "sSTu", "sSTsz", "sSTx_",
                           "sSTof1_0", "sSTof1_1", "sSTof1_2",
                           "sSThf1_0", "sSThf1_1", "sSThf1_2"):
                    s = p.sem(nm)
                    e.wait_ge(s.h, s.v)
            _mkbody(p, blk, "gpsimd", fgend)

    return p



def build_l2(cfg):
    p = Prog("l2")
    nc = p.nc
    ds, TC_, L = cfg.ds, cfg.TC, cfg.L
    nmc = L // TC_
    SUB = min(512, TC_)
    nsub = TC_ // SUB
    ndblk, nbat = cfg.ndblk, cfg.nb
    rows = nbat * ndblk * 128

    delta_in = nc.dram_tensor("delta_in", [rows, L], F32, kind="ExternalInput").ap()
    u_in = nc.dram_tensor("u_in", [rows, L], BF16, kind="ExternalInput").ap()
    sz_in = nc.dram_tensor("sz_in", [rows, L], BF16, kind="ExternalInput").ap()
    b_in = nc.dram_tensor("b_in", [nbat * ds, L], F32, kind="ExternalInput").ap()
    c_in = nc.dram_tensor("c_in", [nbat * ds, L], F32, kind="ExternalInput").ap()
    alog_in = nc.dram_tensor("alog_in", [ndblk * 128, ds], F32, kind="ExternalInput").ap()
    d_in = nc.dram_tensor("d_in", [128, ndblk], F32, kind="ExternalInput").ap()
    bmins_in = nc.dram_tensor("bmins_in", [128, cfg.nc * ds], F32, kind="ExternalInput").ap()
    sel_in = nc.dram_tensor("sel_in", [128, ds * 128], F32, kind="ExternalInput").ap()
    y_out = nc.dram_tensor("y_out", [rows, L], BF16, kind="ExternalOutput").ap()

    with p.ctx:
        blk = p.ctx.enter_context(nc.Block())
        VC, SC, TCe, GC = p.sem("VC"), p.sem("SC"), p.sem("TC"), p.sem("GC")
        sLD = p.sem("sLD")
        sLDc = p.sem("sLDc")
        sSTy = p.sem("sSTy")

        negA = [p.sb("negA_%d" % d, [128, ds], F32) for d in range(ndblk)]
        Dsb = p.sb("Dsb", [128, ndblk], F32)
        bmrow = p.sb("bmrow", [128, cfg.nc * ds], F32)
        beta_sb = p.sb("beta_sb", [128, 1], F32)
        sel = p.sb("sel", [128, ds * 128], BF16)
        self_f = p.sb("self_f", [128, ds * 128], F32)
        ps_bc = [p.ps("ps_bc%d" % i, [128, 2 * TC_]) for i in range(2)]

        dw = [[p.sb("dw_%d_%d" % (d, i), [128, TC_], F32) for i in range(2)]
              for d in range(ndblk)]
        uw = [[p.sb("uw_%d_%d" % (d, i), [128, TC_], BF16) for i in range(2)]
              for d in range(ndblk)]
        szw = [[p.sb("szw_%d_%d" % (d, i), [128, TC_], BF16) for i in range(2)]
               for d in range(ndblk)]
        Bwf = p.sb("Bwf", [ds, TC_], F32)
        Cwf = p.sb("Cwf", [ds, TC_], F32)
        Bw = [p.sb("Bw_%d" % i, [128, TC_], BF16) for i in range(2)]
        Cw = [p.sb("Cw_%d" % i, [128, TC_], BF16) for i in range(2)]
        du = [p.sb("du_%d" % d, [128, TC_], F32) for d in range(ndblk)]
        AB = 8
        a_b2 = [p.sb("a_b%d" % i, [128, AB * TC_], F32) for i in range(2)]
        b_b = p.sb("b_b", [128, TC_], F32)
        xc = [p.sb("xc%d" % d, [128, TC_], F32) for d in range(ndblk)]
        wvb = [p.sb("wvb_%d" % i, [128, TC_], BF16) for i in range(2)]
        yac = [p.sb("yac_%d" % d, [128, TC_], F32) for d in range(ndblk)]
        ygt = [p.sb("ygt_%d" % i, [128, TC_], BF16) for i in range(2)]
        state = p.sb("state", [128, nbat * ndblk * ds], F32)

        def fy0(e):
            for d in range(ndblk):
                ins = e.dma_start(out=negA[d][:, :],
                                  in_=alog_in[d * 128:(d + 1) * 128, :])
                sLDc.inc(ins, 16)
            ins = e.dma_start(out=Dsb[:, :], in_=d_in)
            sLDc.inc(ins, 16)
            ins = e.dma_start(out=bmrow[:, :], in_=bmins_in)
            sLDc.inc(ins, 16)
            ins = e.dma_start(out=self_f[:, :], in_=sel_in)
            sLDc.inc(ins, 16)
        _mkbody(p, blk, "sync", fy0)

        def fs0(e):
            W(e, sLDc)
            for d in range(ndblk):
                ins = e.activation(out=negA[d][:, :], in_=negA[d][:, :], func=AF.Exp)
            SC.inc(ins)
        _mkbody(p, blk, "scalar", fs0)

        def fv0(e):
            W(e, SC, sLDc)
            for d in range(ndblk):
                e.tensor_scalar(out=negA[d][:, :], in0=negA[d][:, :],
                                scalar1=-1.0, scalar2=None, op0=ALU.mult)
            e.tensor_copy(out=sel[:, :], in_=self_f[:, :])
            e.memset(Bw[0][:, :], 0.0)
            e.memset(Bw[1][:, :], 0.0)
            e.memset(Cw[0][:, :], 0.0)
            e.memset(Cw[1][:, :], 0.0)
            ins = e.tensor_reduce(out=beta_sb[:, :], in_=bmrow[:, :], axis=AX.X,
                                  op=ALU.min)
            chn(p, e, "vector", ins)
            e.tensor_scalar(out=beta_sb[:, :], in0=beta_sb[:, :], scalar1=-1.0,
                            scalar2=0.0, op0=ALU.mult, op1=ALU.max)
            ins = e.memset(state[:, :], 0.0)
            VC.inc(ins)
        _mkbody(p, blk, "vector", fv0)

        fv0t = VC.v
        gsz = 8 // ndblk
        sSTyp = [p.sem("sSTy_%d" % i) for i in range(2)]
        sc_last = {0: 0, 1: 0}
        vc2_last = {0: 0, 1: 0}
        tc_last = {0: 0, 1: 0}
        fv1_prev = 0
        gc_prev = 0
        ab_tick = {0: fv0t, 1: fv0t}
        psbc_tick = {0: fv0t, 1: fv0t}
        gc_tick = {0: 0, 1: 0}
        gcd_tick = {d: 0 for d in range(ndblk)}
        sty_tick = {0: 0, 1: 0}

        for b in range(nbat):
            for mc in range(nmc):
                par = mc % 2
                t0 = mc * TC_

                def fy(e, b=b, par=par, t0=t0, sct=sc_last[par],
                       vct=max(fv1_prev, vc2_last[par])):
                    Wv(e, SC, sct)
                    Wv(e, VC, vct)
                    for d in range(ndblk):
                        r0 = (b * ndblk + d) * 128
                        ins = e.dma_start(out=dw[d][par][:, :],
                                          in_=delta_in[r0:r0 + 128, t0:t0 + TC_])
                        sLD.inc(ins, 16)
                        ins = e.dma_start(out=uw[d][par][:, :],
                                          in_=u_in[r0:r0 + 128, t0:t0 + TC_])
                        sLD.inc(ins, 16)
                        ins = e.dma_start(out=szw[d][par][:, :],
                                          in_=sz_in[r0:r0 + 128, t0:t0 + TC_])
                        sLD.inc(ins, 16)
                    ins = e.dma_start(out=Bwf[:, :],
                                      in_=b_in[b * ds:(b + 1) * ds, t0:t0 + TC_])
                    sLD.inc(ins, 16)
                    ins = e.dma_start(out=Cwf[:, :],
                                      in_=c_in[b * ds:(b + 1) * ds, t0:t0 + TC_])
                    sLD.inc(ins, 16)
                _mkbody(p, blk, "sync", fy)
                ldt = sLD.v

                def fv1(e, par=par, ldt=ldt, gct=gc_prev, tct=tc_last[par]):
                    Wv(e, sLD, ldt)
                    Wv(e, GC, gct)
                    Wv(e, TCe, tct)
                    e.tensor_scalar(out=Bw[par][0:ds, :], in0=Bwf[:, :],
                                    scalar1=beta_sb[0:ds, 0:1], scalar2=None,
                                    op0=ALU.add)
                    e.tensor_copy(out=Cw[par][0:ds, :], in_=Cwf[:, :])
                    for d in range(ndblk):
                        e.tensor_tensor(out=du[d][:, :], in0=dw[d][par][:, :],
                                        in1=uw[d][par][:, :], op=ALU.mult)
                        ins = e.memset(yac[d][:, :], 0.0)
                    VC.inc(ins)
                _mkbody(p, blk, "vector", fv1)
                fv1t = VC.v

                for n in range(ds):
                    if n % gsz == 0:
                        g = n // gsz
                        ab = a_b2[g % 2]

                        def fsb(e, n=n, par=par, ab=ab, wt=ab_tick[g % 2],
                                ldt=ldt):
                            Wv(e, VC, wt)
                            Wv(e, sLD, ldt)
                            i0 = n * ndblk
                            for j in range(8):
                                i = i0 + j
                                nn, dd = divmod(i, ndblk)
                                ins = e.activation(
                                    out=ab[:, j * TC_:(j + 1) * TC_],
                                    in_=dw[dd][par][:, :], func=AF.Exp,
                                    scale=negA[dd][:, nn:nn + 1])
                            SC.inc(ins)
                        _mkbody(p, blk, "scalar", fsb)
                        sbt = SC.v

                    def ft(e, n=n, par=par, wt=max(fv1t, psbc_tick[n % 2])):
                        Wv(e, VC, wt)
                        pb = ps_bc[n % 2]
                        for s in range(nsub):
                            e.matmul(pb[:, s * SUB:(s + 1) * SUB],
                                     sel[:, n * 128:(n + 1) * 128],
                                     Bw[par][:, s * SUB:(s + 1) * SUB],
                                     start=True, stop=True)
                            ins = e.matmul(
                                pb[:, TC_ + s * SUB:TC_ + (s + 1) * SUB],
                                sel[:, n * 128:(n + 1) * 128],
                                Cw[par][:, s * SUB:(s + 1) * SUB],
                                start=True, stop=True)
                        TCe.inc(ins)
                    _mkbody(p, blk, "tensor", ft)
                    mmt = TCe.v

                    for d in range(ndblk):
                        i = n * ndblk + d
                        ji = i % 8
                        ab = a_b2[(n // gsz) % 2]

                        def fv(e, n=n, d=d, b=b, i=i, ji=ji, ab=ab, mmt=mmt,
                               scw=sbt, gct=max(gc_tick[i % 2], gcd_tick[d])):
                            if d == 0:
                                Wv(e, TCe, mmt)
                            Wv(e, SC, scw)
                            Wv(e, GC, gct)
                            pb = ps_bc[n % 2]
                            ins = e.tensor_tensor(out=b_b[:, :], in0=pb[:, :TC_],
                                                  in1=du[d][:, :], op=ALU.mult)
                            chn(p, e, "vector", ins)
                            sidx = (b * ndblk + d) * ds + n
                            ins = e.tensor_tensor_scan(out=xc[d][:, :],
                                                       data0=ab[:, ji * TC_:(ji + 1) * TC_],
                                                       data1=b_b[:, :],
                                                       initial=state[:, sidx:sidx + 1],
                                                       op0=ALU.mult, op1=ALU.add)
                            chn(p, e, "vector", ins)
                            ins = e.tensor_tensor(out=wvb[i % 2][:, :],
                                                  in0=xc[d][:, :],
                                                  in1=pb[:, TC_:], op=ALU.mult)
                            VC.inc(ins)
                        _mkbody(p, blk, "vector", fv)
                        if d == ndblk - 1:
                            psbc_tick[n % 2] = VC.v
                        if i == (n // gsz) * gsz * ndblk + 8 - 1:
                            ab_tick[(n // gsz) % 2] = VC.v

                        def fg(e, d=d, i=i, b=b, n=n, ev=VC.v):
                            Wv(e, VC, ev)
                            sidx = (b * ndblk + d) * ds + n
                            e.tensor_copy(out=state[:, sidx:sidx + 1],
                                          in_=xc[d][:, TC_ - 1:TC_])
                            ins = e.tensor_tensor(out=yac[d][:, :], in0=yac[d][:, :],
                                                  in1=wvb[i % 2][:, :], op=ALU.add)
                            GC.inc(ins)
                        _mkbody(p, blk, "gpsimd", fg)
                        gc_tick[i % 2] = GC.v
                        gcd_tick[d] = GC.v

                sc_last[par] = SC.v
                tc_last[par] = TCe.v
                fv1_prev = fv1t
                gc_prev = GC.v

                for d in range(ndblk):
                    def fv2(e, d=d, par=par, gct=gc_prev, st=sty_tick[d % 2]):
                        Wv(e, GC, gct)
                        Wv(e, sSTyp[d % 2], st)
                        ins = e.scalar_tensor_tensor(out=yac[d][:, :],
                                                     in0=uw[d][par][:, :],
                                                     scalar=Dsb[:, d:d + 1],
                                                     in1=yac[d][:, :],
                                                     op0=ALU.mult, op1=ALU.add)
                        chn(p, e, "vector", ins)
                        ins = e.tensor_tensor(out=ygt[d % 2][:, :], in0=yac[d][:, :],
                                              in1=szw[d][par][:, :], op=ALU.mult)
                        VC.inc(ins)
                    _mkbody(p, blk, "vector", fv2)

                    def fg2(e, d=d, b=b, t0=t0, ev=VC.v):
                        Wv(e, VC, ev)
                        r0 = (b * ndblk + d) * 128
                        ins = e.dma_start(out=y_out[r0:r0 + 128, t0:t0 + TC_],
                                          in_=ygt[d % 2][:, :])
                        sSTyp[d % 2].inc(ins, 16)
                    _mkbody(p, blk, "gpsimd", fg2)
                    sty_tick[d % 2] = sSTyp[d % 2].v
                vc2_last[par] = VC.v

        def fgend(e):
            for s in sSTyp:
                e.wait_ge(s.h, s.v)
        _mkbody(p, blk, "gpsimd", fgend)

    return p



def build_l3(cfg):
    p = Prog("l3")
    nc = p.nc
    dm, df, di = cfg.dm, cfg.df, cfg.di
    CH, Lq = cfg.CH, cfg.Lq
    nkm, nkf, nki = dm // 128, df // 128, di // 128
    ncH = Lq // CH

    y_in = nc.dram_tensor("y_in", [di, Lq], BF16, kind="ExternalInput").ap()
    xp_in = nc.dram_tensor("xp_in", [dm, Lq], F32, kind="ExternalInput").ap()
    woutt_in = nc.dram_tensor("woutt_in", [di, dm], F32, kind="ExternalInput").ap()
    w1q_in = nc.dram_tensor("w1q_in", [dm, df], BF16, kind="ExternalInput").ap()
    w2q_in = nc.dram_tensor("w2q_in", [df, dm], BF16, kind="ExternalInput").ap()
    b1_in = nc.dram_tensor("b1_in", [128, nkf], F32, kind="ExternalInput").ap()
    b2_in = nc.dram_tensor("b2_in", [128, nkm], F32, kind="ExternalInput").ap()

    x3_out = nc.dram_tensor("x3_out", [dm, Lq], F32, kind="ExternalOutput").ap()
    xpp_sp = nc.dram_tensor("xpp_sp", [dm, Lq], F32, kind="Internal").ap()
    h_sp = nc.dram_tensor("h_sp3", [df, Lq], BF16, kind="Internal").ap()

    chunks = [(i * CH, CH) for i in range(ncH)]

    with p.ctx:
        blk = p.ctx.enter_context(nc.Block())
        VC, SC, TCs = p.sem("VC"), p.sem("SC"), p.sem("TC")
        sLDw, sLDc = p.sem("sLDw"), p.sem("sLDc")
        sLDy = p.sem("sLDy")

        consts = mk_consts(p, cfg, nkm, nkf)
        emit_const_init(p, blk, consts)

        def f_ldc(e):
            ins = e.dma_start(out=consts["b1sb"][:, :], in_=b1_in)
            sLDc.inc(ins, 16)
            ins = e.dma_start(out=consts["b2h"][:, :], in_=b2_in)
            sLDc.inc(ins, 16)
        _mkbody(p, blk, "sync", f_ldc)

        def f_small(e):
            W(e, sLDc)
            ins = e.tensor_scalar(out=consts["b2h"][:, :], in0=consts["b2h"][:, :],
                                  scalar1=0.5, scalar2=None, op0=ALU.mult)
            VC.inc(ins)
        _mkbody(p, blk, "vector", f_small)

        sSTxppp = [p.sem("sSTxpp_%d" % i) for i in range(3)]
        with ExitStack() as st1:
            wob = [p.sb("wob_%d" % k, [128, dm], BF16, st1) for k in range(nki)]
            wscr = [p.sb("wsc1_%d" % i, [128, dm], F32, st1) for i in range(2)]
            emit_load_weight_cast(p, blk, woutt_in, wob, wscr, di, dm, sLDw)
            yb = [p.sb("yb_%d" % k, [128, CH], BF16, st1) for k in range(nki)]
            xpb = [p.sb("xpb_%d" % k, [128, CH], F32, st1) for k in range(nkm)]
            ps_o = [p.ps("ps_o%d" % i, [128, CH], st1) for i in range(3)]
            xppt = [p.sb("xppt_%d" % i, [128, CH], F32, st1) for i in range(3)]

            evo = {0: 0, 1: 0, 2: 0}
            sto = {0: 0, 1: 0, 2: 0}
            mmo_prev = 0
            evo_prev = 0
            wrdy1 = p.sem("VC").v
            for (off, N) in chunks:
                def fy(e, off=off, N=N, mmp=mmo_prev, evp=evo_prev):
                    if mmp == 0:
                        W(e, TCs, VC)
                    else:
                        Wv(e, TCs, mmp)
                        Wv(e, VC, evp)
                    for k in range(nki):
                        ins = e.dma_start(out=yb[k][:, :N],
                                          in_=y_in[k * 128:(k + 1) * 128, off:off + N])
                        sLDy.inc(ins, 16)
                    for k in range(nkm):
                        ins = e.dma_start(out=xpb[k][:, :N],
                                          in_=xp_in[k * 128:(k + 1) * 128, off:off + N])
                        sLDy.inc(ins, 16)
                _mkbody(p, blk, "sync", fy)
                ldt1 = sLDy.v

                for m in range(nkm):
                    def ft(e, m=m, N=N, ldt=ldt1, wt=max(wrdy1, evo[m % 3])):
                        Wv(e, sLDy, ldt)
                        Wv(e, VC, wt)
                        for k in range(nki):
                            ins = e.matmul(ps_o[m % 3][:, :N],
                                           wob[k][:, m * 128:(m + 1) * 128],
                                           yb[k][:, :N],
                                           start=(k == 0), stop=(k == nki - 1))
                        TCs.inc(ins)
                    _mkbody(p, blk, "tensor", ft)
                    mmt = TCs.v

                    def fv(e, m=m, N=N, mmt=mmt, st=sto[m % 3], ldt=ldt1):
                        Wv(e, TCs, mmt)
                        Wv(e, sLDy, ldt)
                        Wv(e, sSTxppp[m % 3], st)
                        ins = e.tensor_tensor(out=xppt[m % 3][:, :N],
                                              in0=ps_o[m % 3][:, :N],
                                              in1=xpb[m][:, :N], op=ALU.add)
                        VC.inc(ins)
                    _mkbody(p, blk, "vector", fv)
                    evo[m % 3] = p.sem("VC").v

                    def fg(e, m=m, off=off, N=N, ev=evo[m % 3]):
                        Wv(e, VC, ev)
                        ins = e.dma_start(
                            out=xpp_sp[m * 128:(m + 1) * 128, off:off + N],
                            in_=xppt[m % 3][:, :N])
                        sSTxppp[m % 3].inc(ins, 16)
                    _mkbody(p, blk, "gpsimd", fg)
                    sto[m % 3] = sSTxppp[m % 3].v
                mmo_prev = TCs.v
                evo_prev = p.sem("VC").v

        def fybar(e):
            for s in sSTxppp:
                e.wait_ge(s.h, s.v)
        _mkbody(p, blk, "sync", fybar)

        emit_ffn(p, blk, cfg, xpp_sp, x3_out, h_sp, None, None, chunks,
                 consts, "f2", 0.5, wq_src=(w1q_in, w2q_in))

        def fgend(e):
            for nm in ("sSTof2_0", "sSTof2_1", "sSTof2_2",
                       "sSThf2_0", "sSThf2_1", "sSThf2_2",
                       "sSTxpp_0", "sSTxpp_1", "sSTxpp_2"):
                s = p.sem(nm)
                e.wait_ge(s.h, s.v)
        _mkbody(p, blk, "gpsimd", fgend)

    return p



def _t(a):
    return np.ascontiguousarray(np.asarray(a).T)


def stage_l1(cfg, inputs):
    x = np.asarray(inputs["L0_attn_hidden_states"])
    W1 = np.asarray(inputs["L0_ffn_W1"])
    W2 = np.asarray(inputs["L0_ffn_W2"])
    b1 = np.asarray(inputs["L0_ffn_b1"])
    b2 = np.asarray(inputs["L0_ffn_b2"])
    Win = np.asarray(inputs["L0_attn_W_in"])
    cw = np.asarray(inputs["L0_attn_conv_w"])[:, 0, :]
    cb = np.asarray(inputs["L0_attn_conv_b"])
    Wx = np.asarray(inputs["L0_attn_W_x"])
    Wdt = np.asarray(inputs["L0_attn_W_dt"])
    dtb = np.asarray(inputs["L0_attn_dt_bias"])

    dm, df, di = cfg.dm, cfg.df, cfg.di
    nkf, nkm, nki = df // 128, dm // 128, di // 128
    w1t, w2t, wint, wdtt = _t(W1), _t(W2), _t(Win), _t(Wdt)
    wxt_raw = _t(Wx)
    wxt = np.zeros((di, 128), np.float32)
    wxt[:, 0:cfg.dr] = wxt_raw[:, 0:cfg.dr]
    wxt[:, 64:64 + cfg.ds] = wxt_raw[:, cfg.dr:cfg.dr + cfg.ds]
    wxt[:, 96:96 + cfg.ds] = wxt_raw[:, cfg.dr + cfg.ds:cfg.dr + 2 * cfg.ds]
    b1m = np.ascontiguousarray(b1.reshape(nkf, 128).T)
    b2m = np.ascontiguousarray(b2.reshape(nkm, 128).T)
    cwm = np.ascontiguousarray(
        cw.reshape(nki, 128, 4).transpose(1, 2, 0).reshape(128, 4 * nki))
    cbm = np.ascontiguousarray(cb.reshape(nki, 128).T)
    dtbm = np.ascontiguousarray(dtb.reshape(nki, 128).T)
    eye = np.eye(128, dtype=np.float32)

    maps = []
    for c in range(cfg.nc):
        b, q = divmod(c, cfg.nq)
        xt = _t(x[b])
        qs = q * cfg.Lq
        slab = np.zeros((dm, cfg.Lq + 3), np.float32)
        slab[:, 3:] = xt[:, qs:qs + cfg.Lq]
        if q > 0:
            slab[:, 0:3] = xt[:, qs - 3:qs]
        hmask = np.full((128, 1), 0.0 if q == 0 else 1.0, np.float32)
        maps.append(dict(x_in=slab, hmask_in=hmask, w1t_in=w1t, w2t_in=w2t,
                         b1_in=b1m, b2_in=b2m, wint_in=wint, cw_in=cwm,
                         cb_in=cbm, wxt_in=wxt, wdtt_in=wdtt, dtb_in=dtbm,
                         eye_in=eye))
    return maps


def stage_l2(cfg, inputs, l1res):
    A_log = np.asarray(inputs["L0_attn_A_log"])
    Dv = np.asarray(inputs["L0_attn_D"])

    def seq(key):
        return [np.concatenate([np.asarray(l1res[b * cfg.nq + q][key])
                                for q in range(cfg.nq)], axis=1)
                for b in range(cfg.nb)]

    delta_b, u_b, sz_b = seq("delta_out"), seq("u_out"), seq("sz_out")
    B_b, C_b = seq("b_out"), seq("c_out")
    bmins = np.concatenate([np.asarray(l1res[c]["bmin_out"]).reshape(-1)
                            for c in range(cfg.nc)]).reshape(1, -1)
    sel_pat = np.zeros((128, cfg.ds * 128), np.float32)
    for n in range(cfg.ds):
        sel_pat[n, n * 128:(n + 1) * 128] = 1.0
    bmins = np.ascontiguousarray(np.broadcast_to(bmins, (128, bmins.shape[1])))
    b_cat = np.ascontiguousarray(np.concatenate(B_b, axis=0))
    c_cat = np.ascontiguousarray(np.concatenate(C_b, axis=0))

    maps = []
    for s in range(cfg.nc):
        r0 = s * cfg.dpc

        def sl(arrs):
            return np.ascontiguousarray(
                np.concatenate([arrs[b][r0:r0 + cfg.dpc, :]
                                for b in range(cfg.nb)], axis=0))

        alog_s = np.ascontiguousarray(A_log[r0:r0 + cfg.dpc, :])
        d_s = np.ascontiguousarray(Dv[r0:r0 + cfg.dpc].reshape(cfg.ndblk, 128).T)
        maps.append(dict(delta_in=sl(delta_b), u_in=sl(u_b), sz_in=sl(sz_b),
                         b_in=b_cat, c_in=c_cat, alog_in=alog_s, d_in=d_s,
                         bmins_in=bmins, sel_in=sel_pat))
    return maps


def stage_l3(cfg, inputs, l1res, l2res):
    Wout = np.asarray(inputs["L0_attn_W_out"])
    b1 = np.asarray(inputs["L0_ffn_b1"])
    b2 = np.asarray(inputs["L0_ffn_b2"])
    dm, df, di = cfg.dm, cfg.df, cfg.di
    nkf, nkm = df // 128, dm // 128
    woutt = _t(Wout)
    w1q = np.asarray(l1res[0]["w1q_out"])
    w2q = np.asarray(l1res[0]["w2q_out"])
    b1m = np.ascontiguousarray(b1.reshape(nkf, 128).T)
    b2m = np.ascontiguousarray(b2.reshape(nkm, 128).T)

    y_full = []
    for b in range(cfg.nb):
        yb = np.zeros((di, cfg.L), dtype=ml_dtypes.bfloat16)
        for s in range(cfg.nc):
            ys = np.asarray(l2res[s]["y_out"])
            for d in range(cfg.ndblk):
                plane = b * cfg.ndblk + d
                yb[s * cfg.dpc + d * 128: s * cfg.dpc + (d + 1) * 128, :] = \
                    ys[plane * 128:(plane + 1) * 128, :]
        y_full.append(yb)

    maps = []
    for c in range(cfg.nc):
        b, q = divmod(c, cfg.nq)
        qs = q * cfg.Lq
        y_c = np.ascontiguousarray(y_full[b][:, qs:qs + cfg.Lq])
        xp_c = np.ascontiguousarray(np.asarray(l1res[c]["xp_out"])[:, 3:])
        maps.append(dict(y_in=y_c, xp_in=xp_c, woutt_in=woutt, w1q_in=w1q,
                         w2q_in=w2q, b1_in=b1m, b2_in=b2m))
    return maps


_CACHE = {}


def _progs(cfg):
    key = (cfg.dm, cfg.Lq)
    if key not in _CACHE:
        _CACHE[key] = (build_l1(cfg), build_l2(cfg), build_l3(cfg))
    return _CACHE[key]


def kernel(**inputs):
    cfg = FULL
    p1, p2, p3 = _progs(cfg)
    cores = list(range(cfg.nc))

    m1 = stage_l1(cfg, inputs)
    r1 = run_bass_kernel_spmd(p1.nc, m1, core_ids=cores).results
    m2 = stage_l2(cfg, inputs, r1)
    r2 = run_bass_kernel_spmd(p2.nc, m2, core_ids=cores).results
    m3 = stage_l3(cfg, inputs, r1, r2)
    r3 = run_bass_kernel_spmd(p3.nc, m3, core_ids=cores).results

    out = np.empty((cfg.nb, cfg.L, cfg.dm), np.float32)
    for c in range(cfg.nc):
        b, q = divmod(c, cfg.nq)
        out[b, q * cfg.Lq:(q + 1) * cfg.Lq, :] = np.asarray(r3[c]["x3_out"]).T
    return out


# version 16
# speedup vs baseline: 1.1541x; 1.1541x over previous
import numpy as np
import ml_dtypes
import concourse.bass as bass
import concourse.mybir as mybir
from concourse.bass_utils import run_bass_kernel_spmd
from contextlib import ExitStack

F32 = mybir.dt.float32
BF16 = mybir.dt.bfloat16
AF = mybir.ActivationFunctionType
ALU = mybir.AluOpType
AX = mybir.AxisListType

LN_EPS = 1e-5
Q_EPS = 1e-5
RACE_CHECK = False


class CFG:
    def __init__(self, dm=1024, df=4096, di=2048, ds=16, dr=64, nb=2, nq=4,
                 Lq=4096, CH=512, TC=1024):
        self.dm, self.df, self.di, self.ds, self.dr = dm, df, di, ds, dr
        self.nb, self.nq = nb, nq
        self.nc = nb * nq
        self.Lq, self.CH, self.TC = Lq, CH, TC
        self.L = Lq * nq
        self.dpc = di // self.nc
        assert self.dpc % 128 == 0
        self.ndblk = self.dpc // 128


FULL = CFG()
MINI = CFG(dm=256, df=512, di=256, ds=16, dr=64, nb=1, nq=2, Lq=128, CH=64, TC=64)


class Sem:
    __slots__ = ("h", "v")

    def __init__(self, h):
        self.h = h
        self.v = 0

    def inc(self, instr, k=1):
        instr.then_inc(self.h, k)
        self.v += k


def W(eng, *sems):
    for s in sems:
        if s.v > 0:
            eng.wait_ge(s.h, s.v)


def chn(p, e, eng, ins):
    s = p.sem(_ENG_SEM[eng])
    s.inc(ins)
    e.wait_ge(s.h, s.v)


class Prog:
    def __init__(self, name):
        self.nc = bass.Bass("TRN2", target_bir_lowering=False,
                            detect_race_conditions=RACE_CHECK)
        self.ctx = ExitStack()
        self.name = name
        self.sems = {}

    def sem(self, name):
        if name not in self.sems:
            self.sems[name] = Sem(self.ctx.enter_context(self.nc.semaphore(name)))
        return self.sems[name]

    def sb(self, name, shape, dt, st=None):
        return (st or self.ctx).enter_context(
            self.nc.sbuf_tensor(name, list(shape), dt))

    def ps(self, name, shape, st=None):
        return (st or self.ctx).enter_context(
            self.nc.psum_tensor(name, list(shape), F32))


_ENG_SEM = {"vector": "VC", "scalar": "SC", "tensor": "TC", "gpsimd": "GC",
            "sync": None}


def _mkbody(p, blk, eng, fn):
    nm = _ENG_SEM[eng]
    s = p.sem(nm) if nm else None

    def wrapped(e):
        if s is not None and s.v > 0:
            e.wait_ge(s.h, s.v)
        fn(e)
    getattr(blk, eng)(wrapped)



_QCNT = 0


def emit_load_weight_cast(p, blk, dram_ap, dst_tiles, scratch, rows, cols, sLD):
    nk = rows // 128
    for k in range(nk):
        par = k % len(scratch)

        def fy(e, k=k, par=par):
            W(e, p.sem("VC"))
            ins = e.dma_start(out=scratch[par][:, :cols],
                              in_=dram_ap[k * 128:(k + 1) * 128, :])
            sLD.inc(ins, 16)
        _mkbody(p, blk, "sync", fy)

        def fv(e, k=k, par=par):
            W(e, sLD)
            ins = e.tensor_copy(out=dst_tiles[k][:, :cols], in_=scratch[par][:, :cols])
            p.sem("VC").inc(ins)
        _mkbody(p, blk, "vector", fv)


def emit_quantize(p, blk, dram_ap, dst_tiles, scratch, rows, cols, sLD, consts,
                  tag):
    nk = rows // 128
    
    redsum = consts["redsum"]
    gsb = consts["gsb"]
    ginv = consts["ginv_" + tag]
    onesall = consts["onesall"]
    ps_s = consts["ps_small"]
    global _QCNT
    _QCNT += 1
    qst = ExitStack()
    t1 = p.sb("qt1_%d" % _QCNT, [128, cols], F32, qst)

    def fvz(e):
        ins = e.memset(redsum[:, :], 0.0)
        p.sem("VC").inc(ins)
    _mkbody(p, blk, "vector", fvz)

    for k in range(nk):
        par = k % len(scratch)

        def fy(e, k=k, par=par):
            W(e, p.sem("VC"))
            ins = e.dma_start(out=scratch[par][:, :cols],
                              in_=dram_ap[k * 128:(k + 1) * 128, :])
            sLD.inc(ins, 16)
        _mkbody(p, blk, "sync", fy)

        def fv(e, k=k, par=par):
            W(e, sLD)
            ins = e.tensor_reduce(out=redsum[:, k:k + 1], in_=scratch[par][:, :cols],
                                  axis=AX.X, op=ALU.add, apply_absolute_value=True)
            p.sem("VC").inc(ins)
        _mkbody(p, blk, "vector", fv)

    ncols = redsum.shape[1]
    rhi = consts["x2"][0]
    rlo = consts["x2"][1]
    rdf = consts["vscr"]

    def fvhl(e):
        ins = e.tensor_copy(out=rhi[:, :ncols], in_=redsum[:, :ncols])
        chn(p, e, "vector", ins)
        ins = e.tensor_tensor(out=rdf[:, :ncols], in0=redsum[:, :ncols],
                              in1=rhi[:, :ncols], op=ALU.subtract)
        chn(p, e, "vector", ins)
        ins = e.tensor_copy(out=rlo[:, :ncols], in_=rdf[:, :ncols])
        p.sem("VC").inc(ins)
    _mkbody(p, blk, "vector", fvhl)

    def ft(e):
        W(e, p.sem("VC"))
        e.matmul(ps_s[:, :ncols], onesall[:, :], rhi[:, :ncols],
                 start=True, stop=False)
        ins = e.matmul(ps_s[:, :ncols], onesall[:, :], rlo[:, :ncols],
                       start=False, stop=True)
        p.sem("TC").inc(ins)
    _mkbody(p, blk, "tensor", ft)

    def fv4(e):
        W(e, p.sem("TC"))
        ins = e.tensor_reduce(out=gsb[:, :], in_=ps_s[:, :ncols], axis=AX.X,
                              op=ALU.add)
        chn(p, e, "vector", ins)
        ins = e.tensor_scalar(out=gsb[:, :], in0=gsb[:, :],
                              scalar1=1.0 / (rows * cols),
                              scalar2=Q_EPS, op0=ALU.mult, op1=ALU.add)
        chn(p, e, "vector", ins)
        ins = e.reciprocal(out=ginv[:, :], in_=gsb[:, :])
        p.sem("VC").inc(ins)
    _mkbody(p, blk, "vector", fv4)

    for k in range(nk):
        par = k % len(scratch)

        def fy(e, k=k, par=par):
            W(e, p.sem("VC"))
            ins = e.dma_start(out=scratch[par][:, :cols],
                              in_=dram_ap[k * 128:(k + 1) * 128, :])
            sLD.inc(ins, 16)
        _mkbody(p, blk, "sync", fy)

        def fv(e, k=k, par=par):
            W(e, sLD)
            ins = e.tensor_scalar(out=scratch[par][:, :cols],
                                  in0=scratch[par][:, :cols],
                                  scalar1=ginv[:, 0:1], scalar2=None, op0=ALU.mult)
            chn(p, e, "vector", ins)
            ins = e.tensor_scalar(out=t1[:, :cols], in0=scratch[par][:, :cols],
                                  scalar1=0.5, scalar2=None, op0=ALU.is_gt)
            chn(p, e, "vector", ins)
            ins = e.tensor_copy(out=dst_tiles[k][:, :cols], in_=t1[:, :cols])
            chn(p, e, "vector", ins)
            ins = e.tensor_scalar(out=t1[:, :cols], in0=scratch[par][:, :cols],
                                  scalar1=-0.5, scalar2=None, op0=ALU.is_lt)
            chn(p, e, "vector", ins)
            ins = e.tensor_tensor(out=dst_tiles[k][:, :cols],
                                  in0=dst_tiles[k][:, :cols],
                                  in1=t1[:, :cols], op=ALU.subtract)
            p.sem("VC").inc(ins)
        _mkbody(p, blk, "vector", fv)
    qst.close()


def Wv(e, sem, val):
    if val > 0:
        e.wait_ge(sem.h, val)


def mk_qstream(p, blk, dram_ap, dst_tiles, scratch, rows, cols, consts, tag,
               sta):
    nk = rows // 128
    redsum = consts["redsum"]
    gsb = consts["gsb"]
    ginv = consts["ginv_" + tag]
    onesall = consts["onesall"]
    ps_s = consts["ps_small"]
    VC, TCq = p.sem("VC"), p.sem("TC")
    sLDqp = [p.sem("sLDq%s_%d" % (tag, i)) for i in range(len(scratch))]
    global _QCNT
    _QCNT += 1
    t1 = p.sb("qs1_%d" % _QCNT, [128, cols], F32, sta)
    rhi = p.sb("qrh_%d" % _QCNT, [128, nk], BF16, sta)
    rlo = p.sb("qrl_%d" % _QCNT, [128, nk], BF16, sta)
    rdf = p.sb("qrd_%d" % _QCNT, [128, nk], F32, sta)
    base = VC.v
    state = {"red": {0: base, 1: base}, "q2": 0}

    def s_init():
        def fvz(e):
            ins = e.memset(redsum[:, :nk], 0.0)
            VC.inc(ins)
        _mkbody(p, blk, "vector", fvz)

    def s_p1(k):
        par = k % len(scratch)

        def fy(e, k=k, par=par, wt=state["red"][par]):
            Wv(e, VC, wt)
            ins = e.dma_start(out=scratch[par][:, :cols],
                              in_=dram_ap[k * 128:(k + 1) * 128, :])
            sLDqp[par].inc(ins, 16)
        _mkbody(p, blk, "sync", fy)
        ldt = sLDqp[par].v

        def fv(e, k=k, par=par, ldt=ldt):
            Wv(e, sLDqp[par], ldt)
            ins = e.tensor_reduce(out=redsum[:, k:k + 1],
                                  in_=scratch[par][:, :cols],
                                  axis=AX.X, op=ALU.add,
                                  apply_absolute_value=True)
            VC.inc(ins)
        _mkbody(p, blk, "vector", fv)
        state["red"][par] = VC.v

    def s_gsb():
        def fvhl(e):
            ins = e.tensor_copy(out=rhi[:, :], in_=redsum[:, :nk])
            chn(p, e, "vector", ins)
            ins = e.tensor_tensor(out=rdf[:, :], in0=redsum[:, :nk],
                                  in1=rhi[:, :], op=ALU.subtract)
            chn(p, e, "vector", ins)
            ins = e.tensor_copy(out=rlo[:, :], in_=rdf[:, :])
            VC.inc(ins)
        _mkbody(p, blk, "vector", fvhl)
        hlt = VC.v

        def ft(e, wt=hlt):
            Wv(e, VC, wt)
            e.matmul(ps_s[:, :nk], onesall[:, :], rhi[:, :],
                     start=True, stop=False)
            ins = e.matmul(ps_s[:, :nk], onesall[:, :], rlo[:, :],
                           start=False, stop=True)
            TCq.inc(ins)
        _mkbody(p, blk, "tensor", ft)
        mmt = TCq.v

        def fv4(e, wt=mmt):
            Wv(e, TCq, wt)
            ins = e.tensor_reduce(out=gsb[:, :], in_=ps_s[:, :nk], axis=AX.X,
                                  op=ALU.add)
            chn(p, e, "vector", ins)
            ins = e.tensor_scalar(out=gsb[:, :], in0=gsb[:, :],
                                  scalar1=1.0 / (rows * cols),
                                  scalar2=Q_EPS, op0=ALU.mult, op1=ALU.add)
            chn(p, e, "vector", ins)
            ins = e.reciprocal(out=ginv[:, :], in_=gsb[:, :])
            VC.inc(ins)
        _mkbody(p, blk, "vector", fv4)
        state["q2"] = VC.v

    def s_p2(k):
        par = k % len(scratch)

        def fy(e, k=k, par=par, wt=state["q2"]):
            Wv(e, VC, wt)
            ins = e.dma_start(out=scratch[par][:, :cols],
                              in_=dram_ap[k * 128:(k + 1) * 128, :])
            sLDqp[par].inc(ins, 16)
        _mkbody(p, blk, "sync", fy)
        ldt = sLDqp[par].v

        def fv(e, k=k, par=par, ldt=ldt):
            Wv(e, sLDqp[par], ldt)
            ins = e.tensor_scalar(out=scratch[par][:, :cols],
                                  in0=scratch[par][:, :cols],
                                  scalar1=ginv[:, 0:1], scalar2=None,
                                  op0=ALU.mult)
            chn(p, e, "vector", ins)
            ins = e.tensor_scalar(out=t1[:, :cols], in0=scratch[par][:, :cols],
                                  scalar1=0.5, scalar2=None, op0=ALU.is_gt)
            chn(p, e, "vector", ins)
            ins = e.tensor_copy(out=dst_tiles[k][:, :cols], in_=t1[:, :cols])
            chn(p, e, "vector", ins)
            ins = e.tensor_scalar(out=t1[:, :cols], in0=scratch[par][:, :cols],
                                  scalar1=-0.5, scalar2=None, op0=ALU.is_lt)
            chn(p, e, "vector", ins)
            ins = e.tensor_tensor(out=dst_tiles[k][:, :cols],
                                  in0=dst_tiles[k][:, :cols],
                                  in1=t1[:, :cols], op=ALU.subtract)
            VC.inc(ins)
        _mkbody(p, blk, "vector", fv)
        state["q2"] = VC.v

    steps = [s_init]
    for k in range(nk):
        steps.append(lambda k=k: s_p1(k))
    steps.append(s_gsb)
    for k in range(nk):
        steps.append(lambda k=k: s_p2(k))
    return steps, state


def emit_ln(p, blk, cfg, xb_tiles, out_tiles, N, consts, ready=()):
    nk = cfg.dm // 128
    VC, SC, TCs = p.sem("VC"), p.sem("SC"), p.sem("TC")
    ps_s1 = consts["lnps_s1"]
    ps_s2 = consts["lnps_s2"]
    onesall = consts["onesall"]
    x2 = consts["x2"]
    xbb = consts["xbb"]
    m_rep = consts["m_rep"]
    r_rep = consts["r_rep"]
    ckb = consts["ckb"]

    def fvc(e):
        for s, v in ready:
            Wv(e, s, v)
        for k in range(nk):
            ins = e.tensor_copy(out=xbb[k][:, :N], in_=xb_tiles[k][:, :N])
        VC.inc(ins)
    _mkbody(p, blk, "vector", fvc)
    tick = {"cast": p.sem("VC").v}

    lntmp = consts["lntmp"]
    vscr = consts["vscr"]

    def fsq(e):
        for s, v in ready:
            Wv(e, s, v)
        for k in range(nk):
            ins = e.activation(out=x2[k][:, :N], in_=xb_tiles[k][:, :N],
                               func=AF.Square)
        SC.inc(ins)
    _mkbody(p, blk, "scalar", fsq)
    tick["sq"] = SC.v

    def ft0(e, tc=tick["cast"], ts=tick["sq"]):
        Wv(e, VC, tc)
        Wv(e, SC, ts)
        for k in range(nk):
            e.matmul(ps_s1[:, :N], onesall[:, :], xbb[k][:, :N],
                     start=(k == 0), stop=(k == nk - 1))
        for k in range(nk):
            ins = e.matmul(ps_s2[:, :N], onesall[:, :], x2[k][:, :N],
                           start=(k == 0), stop=(k == nk - 1))
        TCs.inc(ins)
    _mkbody(p, blk, "tensor", ft0)

    def fv1(e):
        W(e, TCs)
        e.tensor_scalar(out=m_rep[:, :N], in0=ps_s1[:, :N], scalar1=1.0 / cfg.dm,
                        scalar2=None, op0=ALU.mult)
        ins = e.tensor_scalar(out=lntmp[:, :N], in0=ps_s2[:, :N],
                              scalar1=1.0 / cfg.dm, scalar2=None, op0=ALU.mult)
        chn(p, e, "vector", ins)
        ins = e.tensor_tensor(out=vscr[:, :N], in0=m_rep[:, :N],
                              in1=m_rep[:, :N], op=ALU.mult)
        chn(p, e, "vector", ins)
        ins = e.tensor_tensor(out=lntmp[:, :N], in0=lntmp[:, :N],
                              in1=vscr[:, :N], op=ALU.subtract)
        chn(p, e, "vector", ins)
        ins = e.tensor_scalar(out=lntmp[:, :N], in0=lntmp[:, :N], scalar1=0.0,
                              scalar2=None, op0=ALU.max)
        VC.inc(ins)
    _mkbody(p, blk, "vector", fv1)

    def fs2(e):
        W(e, VC)
        ins = e.activation(out=r_rep[:, :N], in_=lntmp[:, :N], func=AF.Ln,
                            bias=ckb[:, CK_EPS:CK_EPS + 1])
        chn(p, e, "scalar", ins)
        ins = e.activation(out=r_rep[:, :N], in_=r_rep[:, :N], func=AF.Exp,
                           scale=ckb[:, CK_NHALF:CK_NHALF + 1])
        SC.inc(ins)
    _mkbody(p, blk, "scalar", fs2)

    def fv3(e):
        W(e, SC)
        tmp = consts["lntmp"]
        for k in range(nk):
            ins = e.tensor_tensor(out=tmp[:, :N], in0=xb_tiles[k][:, :N],
                                  in1=m_rep[:, :N], op=ALU.subtract)
            chn(p, e, "vector", ins)
            ins = e.tensor_tensor(out=out_tiles[k][:, :N], in0=tmp[:, :N],
                                  in1=r_rep[:, :N], op=ALU.mult)
            chn(p, e, "vector", ins)
    _mkbody(p, blk, "vector", fv3)
    tick["apply"] = p.sem("VC").v
    return tick


def emit_ffn(p, blk, cfg, x_dram, out_dram, h_dram, w1t_in, w2t_in, chunks,
             consts, tag, resid_scale=0.5, wq_src=None, wq_out=None):
    nc = p.nc
    dm, df = cfg.dm, cfg.df
    CH = cfg.CH
    nkm, nkf = dm // 128, df // 128
    VC, SC, TCs = p.sem("VC"), p.sem("SC"), p.sem("TC")
    sLDx = p.sem("sLDx" + tag)
    sLDh = p.sem("sLDh" + tag)
    sSThp = [p.sem("sSTh%s_%d" % (tag, i)) for i in range(3)]
    sSTop = [p.sem("sSTo%s_%d" % (tag, i)) for i in range(3)]
    sLDw = p.sem("sLDw")
    b1sb, b2h = consts["b1sb"], consts["b2h"]

    sSTq1 = p.sem("sSTq1" + tag) if wq_out is not None else None
    sSTq2 = p.sem("sSTq2" + tag) if wq_out is not None else None
    stf = ExitStack()
    w2q = [p.sb("w2qB%s_%d" % (tag, k), [128, dm], BF16, stf) for k in range(nkf)]
    qstate = None
    with ExitStack() as sta:
        w1q = [p.sb("w1qA%s_%d" % (tag, k), [128, df], BF16, sta) for k in range(nkm)]
        if wq_src is None:
            stw = ExitStack()
            wscr = [p.sb("wscA%s_%d" % (tag, i), [128, df], F32, stw)
                    for i in range(2)]
            emit_quantize(p, blk, w1t_in, w1q, wscr, dm, df, sLDw, consts, "w1")
            stw.close()
        else:
            def fyq(e):
                W(e, VC)
                for k in range(nkm):
                    ins = e.dma_start(out=w1q[k][:, :],
                                      in_=wq_src[0][k * 128:(k + 1) * 128, :])
                    sLDw.inc(ins, 16)
                for k in range(nkf):
                    ins = e.dma_start(out=w2q[k][:, :],
                                      in_=wq_src[1][k * 128:(k + 1) * 128, :])
                    sLDw.inc(ins, 16)
            _mkbody(p, blk, "sync", fyq)

            def fvq(e):
                W(e, sLDw)
                VC.inc(e.nop())
            _mkbody(p, blk, "vector", fvq)
        qdA = VC.v
        if wq_out is not None:
            def fgq(e, qd=qdA):
                Wv(e, VC, qd)
                for k in range(nkm):
                    ins = e.dma_start(out=wq_out[0][k * 128:(k + 1) * 128, :],
                                      in_=w1q[k][:, :])
                    sSTq1.inc(ins, 16)
            _mkbody(p, blk, "gpsimd", fgq)
        xb = [p.sb("xbA%s_%d" % (tag, k), [128, CH], F32, sta) for k in range(nkm)]
        lnb = [p.sb("lnA%s_%d" % (tag, k), [128, CH], BF16, sta) for k in range(nkm)]
        hb = [p.sb("hbA%s_%d" % (tag, i), [128, CH], BF16, sta) for i in range(3)]
        ps_f = [p.ps("psA%s_%d" % (tag, i), [128, CH], sta) for i in range(3)]
        if wq_src is None:
            qscr = [p.sb("qsc%s_%d" % (tag, i), [128, dm], F32, sta)
                    for i in range(2)]
            qsteps, qstate = mk_qstream(p, blk, w2t_in, w2q, qscr, df, dm,
                                        consts, "w2", sta)
        else:
            qsteps = []

        evbufA = {0: 0, 1: 0, 2: 0}
        stbufA = {0: 0, 1: 0, 2: 0}
        prev = None
        for (off, N) in chunks:
            def fy(e, off=off, N=N, prev=prev):
                if prev is not None:
                    Wv(e, VC, prev["apply"])
                    Wv(e, SC, prev["sq"])
                else:
                    Wv(e, VC, qdA)
                for k in range(nkm):
                    ins = e.dma_start(out=xb[k][:, :N],
                                      in_=x_dram[k * 128:(k + 1) * 128, off:off + N])
                    sLDx.inc(ins, 16)
            _mkbody(p, blk, "sync", fy)
            ldt = sLDx.v

            prev = emit_ln(p, blk, cfg, xb, lnb, N, consts,
                           ready=((sLDx, ldt),))

            for m in range(nkf):
                def ft(e, m=m, N=N, at=prev["apply"], ev=evbufA[m % 3]):
                    Wv(e, VC, max(at, ev))
                    for k in range(nkm):
                        ins = e.matmul(ps_f[m % 3][:, :N],
                                       w1q[k][:, m * 128:(m + 1) * 128],
                                       lnb[k][:, :N], start=(k == 0),
                                       stop=(k == nkm - 1))
                    TCs.inc(ins)
                _mkbody(p, blk, "tensor", ft)
                mmt = TCs.v

                def fv(e, m=m, N=N, mmt=mmt, st=stbufA[m % 3]):
                    Wv(e, TCs, mmt)
                    Wv(e, sSThp[m % 3], st)
                    ins = e.tensor_scalar(out=hb[m % 3][:, :N],
                                          in0=ps_f[m % 3][:, :N],
                                          scalar1=b1sb[:, m:m + 1], scalar2=0.0,
                                          op0=ALU.add, op1=ALU.max)
                    VC.inc(ins)
                _mkbody(p, blk, "vector", fv)
                evbufA[m % 3] = VC.v

                def fg(e, m=m, off=off, N=N, ev=evbufA[m % 3]):
                    Wv(e, VC, ev)
                    ins = e.dma_start(out=h_dram[m * 128:(m + 1) * 128, off:off + N],
                                      in_=hb[m % 3][:, :N])
                    sSThp[m % 3].inc(ins, 16)
                _mkbody(p, blk, "gpsimd", fg)
                stbufA[m % 3] = sSThp[m % 3].v
                if m % 3 == 2 and qsteps:
                    qsteps.pop(0)()
        while qsteps:
            qsteps.pop(0)()

    def fvhsync(e):
        W(e, *sSThp)
        if sSTq1 is not None:
            W(e, sSTq1)
        VC.inc(e.nop())
    _mkbody(p, blk, "vector", fvhsync)
    hsync_t = VC.v

    with ExitStack() as stb:
        qdB = qstate["q2"] if qstate is not None else qdA
        if wq_out is not None:
            def fgq2(e, qd=qdB):
                Wv(e, VC, qd)
                for k in range(nkf):
                    ins = e.dma_start(out=wq_out[1][k * 128:(k + 1) * 128, :],
                                      in_=w2q[k][:, :])
                    sSTq2.inc(ins, 16)
            _mkbody(p, blk, "gpsimd", fgq2)
        hB = [[p.sb("hBB%s_%d_%d" % (tag, j, k), [128, CH], BF16, stb)
               for k in range(nkf)] for j in range(2)]
        xb = [p.sb("xbB%s_%d" % (tag, k), [128, CH], F32, stb) for k in range(nkm)]
        ps_f = [p.ps("psB%s_%d" % (tag, i), [128, CH], stb) for i in range(3)]
        ot = [p.sb("otB%s_%d" % (tag, i), [128, CH], F32, stb) for i in range(3)]
        sLDxB = p.sem("sLDxB" + tag)
        sLDhp = [p.sem("sLDh%s_%d" % (tag, i)) for i in range(2)]

        evbufB = {0: 0, 1: 0, 2: 0}
        stbufB = {0: 0, 1: 0, 2: 0}
        mm2last = {0: 0, 1: 0}
        evlast = 0
        for ci, (off, N) in enumerate(chunks):
            cpar = ci % 2

            def fy(e, off=off, N=N, ci=ci, cpar=cpar, mml=mm2last[cpar]):
                if ci == 0:
                    Wv(e, VC, hsync_t)
                    for s in sSThp:
                        e.wait_ge(s.h, s.v)
                else:
                    Wv(e, TCs, mml)
                for k in range(nkf):
                    ins = e.dma_start(out=hB[cpar][k][:, :N],
                                      in_=h_dram[k * 128:(k + 1) * 128, off:off + N])
                    sLDhp[cpar].inc(ins, 16)
            _mkbody(p, blk, "sync", fy)
            ldt = sLDhp[cpar].v

            def fyx(e, off=off, N=N, ci=ci, evlast=evlast):
                if ci == 0:
                    Wv(e, VC, hsync_t)
                else:
                    Wv(e, VC, evlast)
                for k in range(nkm):
                    ins = e.dma_start(out=xb[k][:, :N],
                                      in_=x_dram[k * 128:(k + 1) * 128, off:off + N])
                    sLDxB.inc(ins, 16)
            _mkbody(p, blk, "sync", fyx)
            ldtx = sLDxB.v

            for m in range(nkm):
                def ft(e, m=m, N=N, ldt=ldt, cpar=cpar, ev=evbufB[m % 3]):
                    Wv(e, sLDhp[cpar], ldt)
                    Wv(e, VC, max(ev, qdB))
                    for k in range(nkf):
                        ins = e.matmul(ps_f[m % 3][:, :N],
                                       w2q[k][:, m * 128:(m + 1) * 128],
                                       hB[cpar][k][:, :N], start=(k == 0),
                                       stop=(k == nkf - 1))
                    TCs.inc(ins)
                _mkbody(p, blk, "tensor", ft)
                mmt = TCs.v

                def fv(e, m=m, N=N, mmt=mmt, ldt=ldtx, st=stbufB[m % 3]):
                    Wv(e, TCs, mmt)
                    Wv(e, sLDxB, ldt)
                    Wv(e, sSTop[m % 3], st)
                    ins = e.tensor_scalar(out=ot[m % 3][:, :N],
                                          in0=ps_f[m % 3][:, :N],
                                          scalar1=resid_scale,
                                          scalar2=b2h[:, m:m + 1],
                                          op0=ALU.mult, op1=ALU.add)
                    chn(p, e, "vector", ins)
                    ins = e.tensor_tensor(out=ot[m % 3][:, :N], in0=ot[m % 3][:, :N],
                                          in1=xb[m][:, :N], op=ALU.add)
                    VC.inc(ins)
                _mkbody(p, blk, "vector", fv)
                evbufB[m % 3] = VC.v

                def fg(e, m=m, off=off, N=N, ev=evbufB[m % 3]):
                    Wv(e, VC, ev)
                    ins = e.dma_start(out=out_dram[m * 128:(m + 1) * 128, off:off + N],
                                      in_=ot[m % 3][:, :N])
                    sSTop[m % 3].inc(ins, 16)
                _mkbody(p, blk, "gpsimd", fg)
                stbufB[m % 3] = sSTop[m % 3].v
            mm2last[cpar] = TCs.v
            evlast = VC.v

    stf.close()
    if wq_out is not None:
        def fvqend(e):
            W(e, sSTq1, sSTq2)
            VC.inc(e.nop())
        _mkbody(p, blk, "vector", fvqend)


CK_EPS, CK_NHALF, CK_HALF, CK_NONE, CK_ONE = 0, 1, 2, 3, 4


def mk_consts(p, cfg, nkm, nkf):
    consts = {}
    CH = cfg.CH
    consts["ckb"] = p.sb("ckb", [128, 8], F32)
    consts["onesall"] = p.sb("onesall", [128, 128], BF16)
    consts["redsum"] = p.sb("redsum", [128, max(nkm, nkf)], F32)
    consts["gsb"] = p.sb("gsb", [128, 1], F32)
    consts["ginv_w1"] = p.sb("ginv_w1", [128, 1], F32)
    consts["ginv_w2"] = p.sb("ginv_w2", [128, 1], F32)
    consts["ps_small"] = p.ps("ps_small", [128, 128])
    consts["b1sb"] = p.sb("b1sb", [128, nkf], F32)
    consts["b2h"] = p.sb("b2h", [128, nkm], F32)
    consts["x2"] = [p.sb("x2_%d" % i, [128, CH], BF16)
                    for i in range(cfg.dm // 128)]
    consts["xbb"] = [p.sb("xbb_%d" % k, [128, CH], BF16) for k in range(nkm)]
    consts["vscr"] = p.sb("vscr", [128, CH], F32)
    consts["m_rep"] = p.sb("m_rep", [128, CH], F32)
    consts["r_rep"] = p.sb("r_rep", [128, CH], F32)
    consts["lntmp"] = p.sb("lntmp", [128, CH], F32)
    consts["lnps_s1"] = p.ps("lnps_s1", [128, CH])
    consts["lnps_s2"] = p.ps("lnps_s2", [128, CH])
    consts["CHmax"] = CH
    return consts


def emit_const_init(p, blk, consts):
    VC, SC, TCs = p.sem("VC"), p.sem("SC"), p.sem("TC")
    ckb = consts["ckb"]

    def f_init(e):
        e.memset(consts["onesall"][:, :], 1.0)
        e.memset(ckb[:, CK_EPS:CK_EPS + 1], LN_EPS)
        e.memset(ckb[:, CK_NHALF:CK_NHALF + 1], -0.5)
        e.memset(ckb[:, CK_HALF:CK_HALF + 1], 0.5)
        e.memset(ckb[:, CK_NONE:CK_NONE + 1], -1.0)
        ins = e.memset(ckb[:, CK_ONE:CK_ONE + 1], 1.0)
        VC.inc(ins)
    _mkbody(p, blk, "vector", f_init)

    def f_s(e):
        W(e, VC)
        SC.inc(e.nop())
    _mkbody(p, blk, "scalar", f_s)

    def f_t(e):
        W(e, VC)
        TCs.inc(e.nop())
    _mkbody(p, blk, "tensor", f_t)



def build_l1(cfg):
    p = Prog("l1")
    nc = p.nc
    dm, df, di, ds, dr = cfg.dm, cfg.df, cfg.di, cfg.ds, cfg.dr
    CH, Lq = cfg.CH, cfg.Lq
    Ls = Lq + 3
    nkm, nkf, nki = dm // 128, df // 128, di // 128
    ncH = Lq // CH

    x_in = nc.dram_tensor("x_in", [dm, Ls], F32, kind="ExternalInput").ap()
    hmask_in = nc.dram_tensor("hmask_in", [128, 1], F32, kind="ExternalInput").ap()
    w1t_in = nc.dram_tensor("w1t_in", [dm, df], F32, kind="ExternalInput").ap()
    w2t_in = nc.dram_tensor("w2t_in", [df, dm], F32, kind="ExternalInput").ap()
    b1_in = nc.dram_tensor("b1_in", [128, nkf], F32, kind="ExternalInput").ap()
    b2_in = nc.dram_tensor("b2_in", [128, nkm], F32, kind="ExternalInput").ap()
    wint_in = nc.dram_tensor("wint_in", [dm, 2 * di], F32, kind="ExternalInput").ap()
    cw_in = nc.dram_tensor("cw_in", [128, 4 * nki], F32, kind="ExternalInput").ap()
    cb_in = nc.dram_tensor("cb_in", [128, nki], F32, kind="ExternalInput").ap()
    wxt_in = nc.dram_tensor("wxt_in", [di, 128], F32, kind="ExternalInput").ap()
    wdtt_in = nc.dram_tensor("wdtt_in", [dr, di], F32, kind="ExternalInput").ap()
    dtb_in = nc.dram_tensor("dtb_in", [128, nki], F32, kind="ExternalInput").ap()
    eye_in = nc.dram_tensor("eye_in", [128, 128], F32, kind="ExternalInput").ap()

    xp_out = nc.dram_tensor("xp_out", [dm, Ls], F32, kind="ExternalOutput").ap()
    u_out = nc.dram_tensor("u_out", [di, Lq], BF16, kind="ExternalOutput").ap()
    sz_out = nc.dram_tensor("sz_out", [di, Lq], BF16, kind="ExternalOutput").ap()
    delta_out = nc.dram_tensor("delta_out", [di, Lq], F32, kind="ExternalOutput").ap()
    b_out = nc.dram_tensor("b_out", [ds, Lq], F32, kind="ExternalOutput").ap()
    c_out = nc.dram_tensor("c_out", [ds, Lq], F32, kind="ExternalOutput").ap()
    bmin_out = nc.dram_tensor("bmin_out", [ds, 1], F32, kind="ExternalOutput").ap()
    w1q_out = nc.dram_tensor("w1q_out", [dm, df], BF16, kind="ExternalOutput").ap()
    w2q_out = nc.dram_tensor("w2q_out", [df, dm], BF16, kind="ExternalOutput").ap()

    h_sp = nc.dram_tensor("h_sp", [df, Ls], BF16, kind="Internal").ap()
    x_sp = nc.dram_tensor("x_sp", [di, Ls], BF16, kind="Internal").ap()

    chunks = [(0, 3)] + [(3 + i * CH, CH) for i in range(ncH)]

    with p.ctx:
        blk = p.ctx.enter_context(nc.Block())
        VC, SC, TCs = p.sem("VC"), p.sem("SC"), p.sem("TC")
        sLDw = p.sem("sLDw")
        sLDc = p.sem("sLDc")

        consts = mk_consts(p, cfg, nkm, nkf)
        cwsb = p.sb("cwsb", [128, 4 * nki], F32)
        cbsb = p.sb("cbsb", [128, nki], F32)
        cbh = p.sb("cbh", [128, nki], F32)
        dtbsb = p.sb("dtbsb", [128, nki], F32)
        eyesb = p.sb("eyesb", [128, 128], F32)
        hmsb = p.sb("hmsb", [128, 1], F32)

        emit_const_init(p, blk, consts)

        def f_ldc(e):
            for src, dst in ((b1_in, consts["b1sb"]), (b2_in, consts["b2h"]),
                             (cw_in, cwsb), (cb_in, cbsb), (dtb_in, dtbsb),
                             (eye_in, eyesb), (hmask_in, hmsb)):
                ins = e.dma_start(out=dst[:, :], in_=src)
                sLDc.inc(ins, 16)
        _mkbody(p, blk, "sync", f_ldc)

        def f_small(e):
            W(e, sLDc)
            e.tensor_scalar(out=consts["b2h"][:, :], in0=consts["b2h"][:, :],
                            scalar1=0.5, scalar2=None, op0=ALU.mult)
            ins = e.tensor_scalar(out=cbh[:, :], in0=cbsb[:, :], scalar1=0.5,
                                  scalar2=None, op0=ALU.mult)
            VC.inc(ins)
        _mkbody(p, blk, "vector", f_small)

        emit_ffn(p, blk, cfg, x_in, xp_out, h_sp, w1t_in, w2t_in, chunks,
                 consts, "f1", 0.5, wq_out=(w1q_out, w2q_out))

        sSTof1p = [p.sem("sSTof1_%d" % i) for i in range(3)]

        def fvt12(e):
            W(e, *sSTof1p)
            VC.inc(e.nop())
        _mkbody(p, blk, "vector", fvt12)

        sLDx2 = p.sem("sLDx2")
        sSTx_p = [p.sem("sSTx__%d" % i) for i in range(3)]
        sSTsz = p.sem("sSTsz")
        with ExitStack() as st2:
            winb = [p.sb("winb_%d" % k, [128, 2 * di], BF16, st2) for k in range(nkm)]
            wscr = [p.sb("w2sc_0", [128, 2 * di], F32, st2)]
            emit_load_weight_cast(p, blk, wint_in, winb, wscr, dm, 2 * di, sLDw)
            xb = [p.sb("xb2_%d" % k, [128, CH], F32, st2) for k in range(nkm)]
            lnb = [p.sb("ln2_%d" % k, [128, CH], BF16, st2) for k in range(nkm)]
            ps_f = [p.ps("ps_w%d" % i, [128, CH], st2) for i in range(3)]
            xt = [p.sb("xt_%d" % i, [128, CH], BF16, st2) for i in range(3)]
            nki_l = cfg.di // 128
            zal = p.sb("zal", [128, nki_l * CH], BF16, st2)
            thal = p.sb("thal", [128, nki_l * CH], BF16, st2)
            szal = p.sb("szal", [128, nki_l * CH], BF16, st2)

            evb2 = {0: 0, 1: 0, 2: 0}
            stx2 = {0: 0, 1: 0, 2: 0}
            ztanh_prev = 0
            stsz_prev = 0
            prev2 = None
            for ci, (off, N) in enumerate(chunks):
                def fy(e, off=off, N=N, ci=ci, prev=prev2):
                    if prev is not None:
                        Wv(e, VC, prev["apply"])
                        Wv(e, SC, prev["sq"])
                    else:
                        W(e, SC, VC)
                        for s in sSTof1p:
                            e.wait_ge(s.h, s.v)
                    for k in range(nkm):
                        ins = e.dma_start(out=xb[k][:, :N],
                                          in_=xp_out[k * 128:(k + 1) * 128, off:off + N])
                        sLDx2.inc(ins, 16)
                _mkbody(p, blk, "sync", fy)
                ldt2 = sLDx2.v

                prev2 = emit_ln(p, blk, cfg, xb, lnb, N, consts,
                                ready=((sLDx2, ldt2),))

                m_tiles = nki if off == 0 else 2 * nki
                for m in range(m_tiles):
                    def ft(e, m=m, N=N, at=prev2["apply"], ev=evb2[m % 3]):
                        Wv(e, VC, max(at, ev))
                        for k in range(nkm):
                            ins = e.matmul(ps_f[m % 3][:, :N],
                                           winb[k][:, m * 128:(m + 1) * 128],
                                           lnb[k][:, :N],
                                           start=(k == 0), stop=(k == nkm - 1))
                        TCs.inc(ins)
                    _mkbody(p, blk, "tensor", ft)
                    mmt = TCs.v

                    if m < nki:
                        def fv(e, m=m, N=N, mmt=mmt, st=stx2[m % 3]):
                            Wv(e, TCs, mmt)
                            Wv(e, sSTx_p[m % 3], st)
                            ins = e.tensor_copy(out=xt[m % 3][:, :N],
                                                in_=ps_f[m % 3][:, :N])
                            VC.inc(ins)
                        _mkbody(p, blk, "vector", fv)
                        evb2[m % 3] = VC.v

                        def fg(e, m=m, off=off, N=N, ev=VC.v):
                            Wv(e, VC, ev)
                            ins = e.dma_start(
                                out=x_sp[m * 128:(m + 1) * 128, off:off + N],
                                in_=xt[m % 3][:, :N])
                            sSTx_p[m % 3].inc(ins, 16)
                        _mkbody(p, blk, "gpsimd", fg)
                        stx2[m % 3] = sSTx_p[m % 3].v
                    else:
                        def fv(e, m=m, N=N, mmt=mmt, zt=ztanh_prev):
                            Wv(e, TCs, mmt)
                            Wv(e, SC, zt)
                            mi = m - nki
                            ins = e.tensor_copy(out=zal[:, mi * CH:mi * CH + N],
                                                in_=ps_f[m % 3][:, :N])
                            VC.inc(ins)
                        _mkbody(p, blk, "vector", fv)
                        evb2[m % 3] = VC.v

                if m_tiles > nki:
                    def fs(e, N=N, ev=VC.v):
                        Wv(e, VC, ev)
                        ins = e.activation(out=thal[:, :], in_=zal[:, :],
                                           func=AF.Tanh,
                                           scale=consts["ckb"][:, CK_HALF:CK_HALF + 1])
                        SC.inc(ins)
                    _mkbody(p, blk, "scalar", fs)
                    ztanh_prev = SC.v

                    def fvz(e, N=N, sct=ztanh_prev, st=stsz_prev):
                        Wv(e, SC, sct)
                        Wv(e, sSTsz, st)
                        ins = e.tensor_scalar(out=thal[:, :], in0=thal[:, :],
                                              scalar1=0.5, scalar2=0.5,
                                              op0=ALU.mult, op1=ALU.add)
                        chn(p, e, "vector", ins)
                        ins = e.tensor_tensor(out=szal[:, :], in0=zal[:, :],
                                              in1=thal[:, :], op=ALU.mult)
                        VC.inc(ins)
                    _mkbody(p, blk, "vector", fvz)

                    def fgz(e, off=off, N=N, ev=VC.v):
                        Wv(e, VC, ev)
                        for mi in range(nki):
                            ins = e.dma_start(
                                out=sz_out[mi * 128:(mi + 1) * 128,
                                           off - 3:off - 3 + N],
                                in_=szal[:, mi * CH:mi * CH + N])
                            sSTsz.inc(ins, 16)
                    _mkbody(p, blk, "gpsimd", fgz)
                    stsz_prev = sSTsz.v

        sLDxw = p.sem("sLDxw")
        sSTu = p.sem("sSTu")
        with ExitStack() as st3:
            diag = [[p.sb("dg_%d_%d" % (j, k), [128, 128], BF16, st3)
                     for k in range(nki)] for j in range(4)]
            xw = [p.sb("xw_%d" % k, [128, CH + 3], BF16, st3) for k in range(nki)]
            ps_c = [p.ps("ps_c%d" % i, [128, CH], st3) for i in range(2)]
            ual = p.sb("ual", [128, nki * CH], BF16, st3)
            thal3 = p.sb("thal3", [128, nki * CH], BF16, st3)
            utal = p.sb("utal", [128, nki * CH], BF16, st3)

            def fdg(e):
                W(e, sLDc)
                for j in range(4):
                    for k in range(nki):
                        ins = e.tensor_scalar(
                            out=diag[j][k][:, :], in0=eyesb[:, :],
                            scalar1=cwsb[:, j * nki + k:j * nki + k + 1],
                            scalar2=None, op0=ALU.mult)
                VC.inc(ins)
            _mkbody(p, blk, "vector", fdg)

            evc3 = {0: 0, 1: 0}
            mm3_prev = 0
            tanh3_prev = 0
            stu_prev = 0
            dgt = VC.v
            for c in range(ncH):
                off = c * CH

                def fy(e, off=off, c=c, mmp=mm3_prev):
                    if c == 0:
                        W(e, TCs, VC)
                        for s in sSTx_p:
                            e.wait_ge(s.h, s.v)
                    else:
                        Wv(e, TCs, mmp)
                    for k in range(nki):
                        ins = e.dma_start(
                            out=xw[k][:, :],
                            in_=x_sp[k * 128:(k + 1) * 128, off:off + CH + 3])
                        sLDxw.inc(ins, 16)
                _mkbody(p, blk, "sync", fy)
                ldt3 = sLDxw.v

                if c == 0:
                    def fvm(e):
                        W(e, sLDxw)
                        for k in range(nki):
                            ins = e.tensor_scalar(out=xw[k][:, 0:3], in0=xw[k][:, 0:3],
                                                  scalar1=hmsb[:, 0:1], scalar2=None,
                                                  op0=ALU.mult)
                        VC.inc(ins)
                    _mkbody(p, blk, "vector", fvm)
                    dgt = VC.v

                for k in range(nki):
                    def ft(e, k=k, ldt=ldt3, wt=max(dgt, evc3[k % 2])):
                        Wv(e, sLDxw, ldt)
                        Wv(e, VC, wt)
                        for j in range(4):
                            ins = e.matmul(ps_c[k % 2][:, :CH], diag[j][k][:, :],
                                           xw[k][:, j:j + CH],
                                           start=(j == 0), stop=(j == 3))
                        TCs.inc(ins)
                    _mkbody(p, blk, "tensor", ft)
                    mmt = TCs.v

                    def fv(e, k=k, mmt=mmt, zt=tanh3_prev):
                        Wv(e, TCs, mmt)
                        Wv(e, SC, zt)
                        ins = e.tensor_scalar(out=ual[:, k * CH:(k + 1) * CH],
                                              in0=ps_c[k % 2][:, :CH],
                                              scalar1=cbsb[:, k:k + 1],
                                              scalar2=None, op0=ALU.add)
                        VC.inc(ins)
                    _mkbody(p, blk, "vector", fv)
                    evc3[k % 2] = VC.v
                mm3_prev = TCs.v

                def fs(e, ev=VC.v):
                    Wv(e, VC, ev)
                    ins = e.activation(out=thal3[:, :], in_=ual[:, :],
                                       func=AF.Tanh,
                                       scale=consts["ckb"][:, CK_HALF:CK_HALF + 1])
                    SC.inc(ins)
                _mkbody(p, blk, "scalar", fs)
                tanh3_prev = SC.v

                def fvu(e, sct=tanh3_prev, st=stu_prev):
                    Wv(e, SC, sct)
                    Wv(e, sSTu, st)
                    ins = e.tensor_scalar(out=thal3[:, :], in0=thal3[:, :],
                                          scalar1=0.5, scalar2=0.5,
                                          op0=ALU.mult, op1=ALU.add)
                    chn(p, e, "vector", ins)
                    ins = e.tensor_tensor(out=utal[:, :], in0=ual[:, :],
                                          in1=thal3[:, :], op=ALU.mult)
                    VC.inc(ins)
                _mkbody(p, blk, "vector", fvu)

                def fg(e, off=off, ev=VC.v):
                    Wv(e, VC, ev)
                    for k in range(nki):
                        ins = e.dma_start(
                            out=u_out[k * 128:(k + 1) * 128, off:off + CH],
                            in_=utal[:, k * CH:(k + 1) * CH])
                        sSTu.inc(ins, 16)
                _mkbody(p, blk, "gpsimd", fg)
                stu_prev = sSTu.v

        sLDu = p.sem("sLDu")
        sSTd = p.sem("sSTd")
        sSTbc = p.sem("sSTbc")
        dxw = 128
        with ExitStack() as st4:
            wxb = [p.sb("wxb_%d" % k, [128, dxw], BF16, st4) for k in range(nki)]
            wdtb = p.sb("wdtb", [128, di], BF16, st4)
            wscr4 = [p.sb("w4sc_%d" % i, [128, max(dxw, di)], F32, st4)
                     for i in range(2)]
            emit_load_weight_cast(p, blk, wxt_in, wxb, wscr4, di, dxw, sLDw)

        
            def fyw(e):
                W(e, p.sem("VC"))
                ins = e.dma_start(out=wscr4[0][0:dr, :di], in_=wdtt_in)
                sLDw.inc(ins, 16)
            _mkbody(p, blk, "sync", fyw)

            def fvw(e):
                W(e, sLDw)
                ins = e.memset(wdtb[:, :], 0.0)
                chn(p, e, "vector", ins)
                ins = e.tensor_copy(out=wdtb[0:dr, :], in_=wscr4[0][0:dr, :di])
                p.sem("VC").inc(ins)
            _mkbody(p, blk, "vector", fvw)

            ub = [p.sb("ub_%d" % k, [128, CH], BF16, st4) for k in range(nki)]
            ps_x = p.ps("ps_x", [dxw, CH], st4)
            ps_d = [p.ps("ps_d%d" % i, [128, CH], st4) for i in range(2)]
            dtr = p.sb("dtr", [128, CH], BF16, st4)
            Bt = p.sb("Bt", [ds, CH], F32, st4)
            Ct = p.sb("Ct", [ds, CH], F32, st4)
            chmin = p.sb("chmin", [ds, 1], F32, st4)
            bmin_run = p.sb("bmin_run", [ds, 1], F32, st4)
            dtal = p.sb("dtal", [128, nki * CH], F32, st4)
            spl = p.sb("spl", [128, nki * CH], F32, st4)

            def fv0(e):
                e.memset(bmin_run[:, :], 3.0e38)
                ins = e.memset(dtr[:, :], 0.0)
                VC.inc(ins)
            _mkbody(p, blk, "vector", fv0)

            mm4_prev = 0
            dtr_prev = 0
            bc_prev = 0
            stbc_prev = 0
            spl_prev = 0
            std_prev = 0
            evd4 = {0: 0, 1: 0}
            wrdy = VC.v
            for c in range(ncH):
                off = c * CH

                def fy(e, off=off, c=c, mmp=mm4_prev):
                    if c == 0:
                        W(e, TCs)
                        e.wait_ge(sSTu.h, sSTu.v)
                    else:
                        Wv(e, TCs, mmp)
                    for k in range(nki):
                        ins = e.dma_start(out=ub[k][:, :],
                                          in_=u_out[k * 128:(k + 1) * 128, off:off + CH])
                        sLDu.inc(ins, 16)
                _mkbody(p, blk, "sync", fy)
                ldt4 = sLDu.v

                def ft(e, ldt=ldt4, sct=dtr_prev, vct=max(wrdy, bc_prev)):
                    Wv(e, sLDu, ldt)
                    Wv(e, SC, sct)
                    Wv(e, VC, vct)
                    for k in range(nki):
                        ins = e.matmul(ps_x[:, :CH], wxb[k][:, :], ub[k][:, :],
                                       start=(k == 0), stop=(k == nki - 1))
                    TCs.inc(ins)
                _mkbody(p, blk, "tensor", ft)
                mmx = TCs.v
                mm4_prev = mmx

                def fs(e, mmt=mmx):
                    Wv(e, TCs, mmt)
                    ins = e.activation(out=dtr[0:dr, :], in_=ps_x[0:dr, :CH],
                                       func=AF.Identity)
                    SC.inc(ins)
                _mkbody(p, blk, "scalar", fs)
                dtr_prev = SC.v

                def fv(e, mmt=mmx, st=stbc_prev):
                    Wv(e, TCs, mmt)
                    Wv(e, sSTbc, st)
                    e.tensor_copy(out=Bt[:, :], in_=ps_x[64:64 + ds, :CH])
                    ins = e.tensor_copy(out=Ct[:, :], in_=ps_x[96:96 + ds, :CH])
                    chn(p, e, "vector", ins)
                    ins = e.tensor_reduce(out=chmin[:, :], in_=Bt[:, :], axis=AX.X,
                                          op=ALU.min)
                    chn(p, e, "vector", ins)
                    ins = e.tensor_tensor(out=bmin_run[:, :], in0=bmin_run[:, :],
                                          in1=chmin[:, :], op=ALU.min)
                    VC.inc(ins)
                _mkbody(p, blk, "vector", fv)
                bc_prev = VC.v

                def fg(e, off=off, ev=VC.v):
                    Wv(e, VC, ev)
                    ins = e.dma_start(out=b_out[:, off:off + CH], in_=Bt[:, :])
                    sSTbc.inc(ins, 16)
                    ins = e.dma_start(out=c_out[:, off:off + CH], in_=Ct[:, :])
                    sSTbc.inc(ins, 16)
                _mkbody(p, blk, "gpsimd", fg)
                stbc_prev = sSTbc.v

                for m in range(nki):
                    def ft2(e, m=m, sct=dtr_prev, ev=evd4[m % 2]):
                        Wv(e, SC, sct)
                        Wv(e, VC, max(ev, wrdy))
                        ins = e.matmul(ps_d[m % 2][:, :CH],
                                       wdtb[:, m * 128:(m + 1) * 128], dtr[:, :],
                                       start=True, stop=True)
                        TCs.inc(ins)
                    _mkbody(p, blk, "tensor", ft2)
                    mmt2 = TCs.v

                    def fv2(e, m=m, mmt=mmt2, sct=spl_prev, st=std_prev):
                        Wv(e, TCs, mmt)
                        Wv(e, SC, sct)
                        Wv(e, sSTd, st)
                        ins = e.tensor_scalar(out=dtal[:, m * CH:(m + 1) * CH],
                                              in0=ps_d[m % 2][:, :CH],
                                              scalar1=dtbsb[:, m:m + 1],
                                              scalar2=None, op0=ALU.add)
                        VC.inc(ins)
                    _mkbody(p, blk, "vector", fv2)
                    evd4[m % 2] = VC.v

                def fs2(e, ev=VC.v):
                    Wv(e, VC, ev)
                    ckb = consts["ckb"]
                    ins = e.activation(out=spl[:, :], in_=dtal[:, :], func=AF.Abs)
                    chn(p, e, "scalar", ins)
                    ins = e.activation(out=spl[:, :], in_=spl[:, :], func=AF.Exp,
                                       scale=ckb[:, CK_NONE:CK_NONE + 1])
                    chn(p, e, "scalar", ins)
                    ins = e.activation(out=spl[:, :], in_=spl[:, :], func=AF.Ln,
                                       bias=ckb[:, CK_ONE:CK_ONE + 1])
                    chn(p, e, "scalar", ins)
                    ins = e.activation(out=dtal[:, :], in_=dtal[:, :], func=AF.Relu)
                    SC.inc(ins)
                _mkbody(p, blk, "scalar", fs2)
                spl_prev = SC.v

                def fv3(e, sct=spl_prev, st=std_prev):
                    Wv(e, SC, sct)
                    Wv(e, sSTd, st)
                    ins = e.tensor_tensor(out=dtal[:, :], in0=dtal[:, :],
                                          in1=spl[:, :], op=ALU.add)
                    chn(p, e, "vector", ins)
                    ins = e.tensor_scalar(out=dtal[:, :], in0=dtal[:, :],
                                          scalar1=1e-4, scalar2=20.0,
                                          op0=ALU.max, op1=ALU.min)
                    VC.inc(ins)
                _mkbody(p, blk, "vector", fv3)

                def fg2(e, off=off, ev=VC.v):
                    Wv(e, VC, ev)
                    for m in range(nki):
                        ins = e.dma_start(
                            out=delta_out[m * 128:(m + 1) * 128, off:off + CH],
                            in_=dtal[:, m * CH:(m + 1) * CH])
                        sSTd.inc(ins, 16)
                _mkbody(p, blk, "gpsimd", fg2)
                std_prev = sSTd.v

            def fgend(e):
                W(e, VC)
                ins = e.dma_start(out=bmin_out, in_=bmin_run[:, :])
                sSTd.inc(ins, 16)
                for nm in ("sSTd", "sSTbc", "sSTu", "sSTsz", "sSTx_",
                           "sSTof1_0", "sSTof1_1", "sSTof1_2",
                           "sSThf1_0", "sSThf1_1", "sSThf1_2"):
                    s = p.sem(nm)
                    e.wait_ge(s.h, s.v)
            _mkbody(p, blk, "gpsimd", fgend)

    return p



def build_l2(cfg):
    p = Prog("l2")
    nc = p.nc
    ds, TC_, L = cfg.ds, cfg.TC, cfg.L
    nmc = L // TC_
    SUB = min(512, TC_)
    nsub = TC_ // SUB
    ndblk, nbat = cfg.ndblk, cfg.nb
    rows = nbat * ndblk * 128

    delta_in = nc.dram_tensor("delta_in", [rows, L], F32, kind="ExternalInput").ap()
    u_in = nc.dram_tensor("u_in", [rows, L], BF16, kind="ExternalInput").ap()
    sz_in = nc.dram_tensor("sz_in", [rows, L], BF16, kind="ExternalInput").ap()
    b_in = nc.dram_tensor("b_in", [nbat * ds, L], F32, kind="ExternalInput").ap()
    c_in = nc.dram_tensor("c_in", [nbat * ds, L], F32, kind="ExternalInput").ap()
    alog_in = nc.dram_tensor("alog_in", [ndblk * 128, ds], F32, kind="ExternalInput").ap()
    d_in = nc.dram_tensor("d_in", [128, ndblk], F32, kind="ExternalInput").ap()
    bmins_in = nc.dram_tensor("bmins_in", [128, cfg.nc * ds], F32, kind="ExternalInput").ap()
    sel_in = nc.dram_tensor("sel_in", [128, ds * 128], F32, kind="ExternalInput").ap()
    y_out = nc.dram_tensor("y_out", [rows, L], BF16, kind="ExternalOutput").ap()

    with p.ctx:
        blk = p.ctx.enter_context(nc.Block())
        VC, SC, TCe, GC = p.sem("VC"), p.sem("SC"), p.sem("TC"), p.sem("GC")
        sLD = p.sem("sLD")
        sLDc = p.sem("sLDc")
        sSTy = p.sem("sSTy")

        negA = [p.sb("negA_%d" % d, [128, ds], F32) for d in range(ndblk)]
        Dsb = p.sb("Dsb", [128, ndblk], F32)
        bmrow = p.sb("bmrow", [128, cfg.nc * ds], F32)
        beta_sb = p.sb("beta_sb", [128, 1], F32)
        sel = p.sb("sel", [128, ds * 128], BF16)
        self_f = p.sb("self_f", [128, ds * 128], F32)
        ps_bc = [p.ps("ps_bc%d" % i, [128, 2 * TC_]) for i in range(2)]

        dw = [[p.sb("dw_%d_%d" % (d, i), [128, TC_], F32) for i in range(2)]
              for d in range(ndblk)]
        uw = [[p.sb("uw_%d_%d" % (d, i), [128, TC_], BF16) for i in range(2)]
              for d in range(ndblk)]
        szw = [[p.sb("szw_%d_%d" % (d, i), [128, TC_], BF16) for i in range(2)]
               for d in range(ndblk)]
        Bwf = p.sb("Bwf", [ds, TC_], F32)
        Cwf = p.sb("Cwf", [ds, TC_], F32)
        Bw = [p.sb("Bw_%d" % i, [128, TC_], BF16) for i in range(2)]
        Cw = [p.sb("Cw_%d" % i, [128, TC_], BF16) for i in range(2)]
        du = [p.sb("du_%d" % d, [128, TC_], F32) for d in range(ndblk)]
        AB = 8
        a_b2 = [p.sb("a_b%d" % i, [128, AB * TC_], F32) for i in range(2)]
        b_b = p.sb("b_b", [128, TC_], F32)
        xc = p.sb("xc", [128, TC_], F32)
        wvb = [p.sb("wvb_%d" % i, [128, ndblk * TC_], BF16) for i in range(2)]
        yac = [p.sb("yac_%d" % d, [128, TC_], F32) for d in range(ndblk)]
        ygt = [p.sb("ygt_%d" % i, [128, TC_], BF16) for i in range(2)]
        state = p.sb("state", [128, nbat * ndblk * ds], F32)

        def fy0(e):
            for d in range(ndblk):
                ins = e.dma_start(out=negA[d][:, :],
                                  in_=alog_in[d * 128:(d + 1) * 128, :])
                sLDc.inc(ins, 16)
            ins = e.dma_start(out=Dsb[:, :], in_=d_in)
            sLDc.inc(ins, 16)
            ins = e.dma_start(out=bmrow[:, :], in_=bmins_in)
            sLDc.inc(ins, 16)
            ins = e.dma_start(out=self_f[:, :], in_=sel_in)
            sLDc.inc(ins, 16)
        _mkbody(p, blk, "sync", fy0)

        def fs0(e):
            W(e, sLDc)
            for d in range(ndblk):
                ins = e.activation(out=negA[d][:, :], in_=negA[d][:, :], func=AF.Exp)
            SC.inc(ins)
        _mkbody(p, blk, "scalar", fs0)

        def fv0(e):
            W(e, SC, sLDc)
            for d in range(ndblk):
                e.tensor_scalar(out=negA[d][:, :], in0=negA[d][:, :],
                                scalar1=-1.0, scalar2=None, op0=ALU.mult)
            e.tensor_copy(out=sel[:, :], in_=self_f[:, :])
            e.memset(Bw[0][:, :], 0.0)
            e.memset(Bw[1][:, :], 0.0)
            e.memset(Cw[0][:, :], 0.0)
            e.memset(Cw[1][:, :], 0.0)
            ins = e.tensor_reduce(out=beta_sb[:, :], in_=bmrow[:, :], axis=AX.X,
                                  op=ALU.min)
            chn(p, e, "vector", ins)
            e.tensor_scalar(out=beta_sb[:, :], in0=beta_sb[:, :], scalar1=-1.0,
                            scalar2=0.0, op0=ALU.mult, op1=ALU.max)
            ins = e.memset(state[:, :], 0.0)
            VC.inc(ins)
        _mkbody(p, blk, "vector", fv0)

        fv0t = VC.v
        gsz = 8 // ndblk
        sSTyp = [p.sem("sSTy_%d" % i) for i in range(2)]
        sc_last = {0: 0, 1: 0}
        vc2_last = {0: 0, 1: 0}
        tc_last = {0: 0, 1: 0}
        fv1_prev = 0
        gc_prev = 0
        ab_tick = {0: fv0t, 1: fv0t}
        psbc_tick = {0: fv0t, 1: fv0t}
        gc_tick = {0: 0, 1: 0}
        sty_tick = {0: 0, 1: 0}

        for b in range(nbat):
            for mc in range(nmc):
                par = mc % 2
                t0 = mc * TC_

                def fy(e, b=b, par=par, t0=t0, sct=sc_last[par],
                       vct=max(fv1_prev, vc2_last[par])):
                    Wv(e, SC, sct)
                    Wv(e, VC, vct)
                    for d in range(ndblk):
                        r0 = (b * ndblk + d) * 128
                        ins = e.dma_start(out=dw[d][par][:, :],
                                          in_=delta_in[r0:r0 + 128, t0:t0 + TC_])
                        sLD.inc(ins, 16)
                        ins = e.dma_start(out=uw[d][par][:, :],
                                          in_=u_in[r0:r0 + 128, t0:t0 + TC_])
                        sLD.inc(ins, 16)
                        ins = e.dma_start(out=szw[d][par][:, :],
                                          in_=sz_in[r0:r0 + 128, t0:t0 + TC_])
                        sLD.inc(ins, 16)
                    ins = e.dma_start(out=Bwf[:, :],
                                      in_=b_in[b * ds:(b + 1) * ds, t0:t0 + TC_])
                    sLD.inc(ins, 16)
                    ins = e.dma_start(out=Cwf[:, :],
                                      in_=c_in[b * ds:(b + 1) * ds, t0:t0 + TC_])
                    sLD.inc(ins, 16)
                _mkbody(p, blk, "sync", fy)
                ldt = sLD.v

                def fv1(e, par=par, ldt=ldt, gct=gc_prev, tct=tc_last[par]):
                    Wv(e, sLD, ldt)
                    Wv(e, GC, gct)
                    Wv(e, TCe, tct)
                    e.tensor_scalar(out=Bw[par][0:ds, :], in0=Bwf[:, :],
                                    scalar1=beta_sb[0:ds, 0:1], scalar2=None,
                                    op0=ALU.add)
                    e.tensor_copy(out=Cw[par][0:ds, :], in_=Cwf[:, :])
                    for d in range(ndblk):
                        e.tensor_tensor(out=du[d][:, :], in0=dw[d][par][:, :],
                                        in1=uw[d][par][:, :], op=ALU.mult)
                        ins = e.memset(yac[d][:, :], 0.0)
                    VC.inc(ins)
                _mkbody(p, blk, "vector", fv1)
                fv1t = VC.v

                for n in range(ds):
                    if n % gsz == 0:
                        g = n // gsz
                        ab = a_b2[g % 2]

                        def fsb(e, n=n, par=par, ab=ab, wt=ab_tick[g % 2],
                                ldt=ldt):
                            Wv(e, VC, wt)
                            Wv(e, sLD, ldt)
                            i0 = n * ndblk
                            for j in range(8):
                                i = i0 + j
                                nn, dd = divmod(i, ndblk)
                                ins = e.activation(
                                    out=ab[:, j * TC_:(j + 1) * TC_],
                                    in_=dw[dd][par][:, :], func=AF.Exp,
                                    scale=negA[dd][:, nn:nn + 1])
                            SC.inc(ins)
                        _mkbody(p, blk, "scalar", fsb)
                        sbt = SC.v

                    def ft(e, n=n, par=par, wt=max(fv1t, psbc_tick[n % 2])):
                        Wv(e, VC, wt)
                        pb = ps_bc[n % 2]
                        for s in range(nsub):
                            e.matmul(pb[:, s * SUB:(s + 1) * SUB],
                                     sel[:, n * 128:(n + 1) * 128],
                                     Bw[par][:, s * SUB:(s + 1) * SUB],
                                     start=True, stop=True)
                            ins = e.matmul(
                                pb[:, TC_ + s * SUB:TC_ + (s + 1) * SUB],
                                sel[:, n * 128:(n + 1) * 128],
                                Cw[par][:, s * SUB:(s + 1) * SUB],
                                start=True, stop=True)
                        TCe.inc(ins)
                    _mkbody(p, blk, "tensor", ft)
                    mmt = TCe.v

                    ab = a_b2[(n // gsz) % 2]

                    def fv(e, n=n, b=b, ab=ab, mmt=mmt, sbt=sbt,
                           gct=gc_tick[n % 2]):
                        Wv(e, TCe, mmt)
                        Wv(e, SC, sbt)
                        Wv(e, GC, gct)
                        pb = ps_bc[n % 2]
                        for d in range(ndblk):
                            ji = (n * ndblk + d) % 8
                            ins = e.tensor_tensor(out=b_b[:, :], in0=pb[:, :TC_],
                                                  in1=du[d][:, :], op=ALU.mult)
                            chn(p, e, "vector", ins)
                            sidx = (b * ndblk + d) * ds + n
                            ins = e.tensor_tensor_scan(out=xc[:, :],
                                                       data0=ab[:, ji * TC_:(ji + 1) * TC_],
                                                       data1=b_b[:, :],
                                                       initial=state[:, sidx:sidx + 1],
                                                       op0=ALU.mult, op1=ALU.add)
                            chn(p, e, "vector", ins)
                            e.tensor_copy(out=state[:, sidx:sidx + 1],
                                          in_=xc[:, TC_ - 1:TC_])
                            ins = e.tensor_tensor(
                                out=wvb[n % 2][:, d * TC_:(d + 1) * TC_],
                                in0=xc[:, :], in1=pb[:, TC_:], op=ALU.mult)
                            if d == ndblk - 1:
                                VC.inc(ins)
                            else:
                                chn(p, e, "vector", ins)
                    _mkbody(p, blk, "vector", fv)
                    psbc_tick[n % 2] = VC.v
                    if (n + 1) % gsz == 0:
                        ab_tick[(n // gsz) % 2] = VC.v

                    def fg(e, n=n, ev=VC.v):
                        Wv(e, VC, ev)
                        for d in range(ndblk):
                            ins = e.tensor_tensor(
                                out=yac[d][:, :], in0=yac[d][:, :],
                                in1=wvb[n % 2][:, d * TC_:(d + 1) * TC_],
                                op=ALU.add)
                        GC.inc(ins)
                    _mkbody(p, blk, "gpsimd", fg)
                    gc_tick[n % 2] = GC.v

                sc_last[par] = SC.v
                tc_last[par] = TCe.v
                fv1_prev = fv1t
                gc_prev = GC.v

                for d in range(ndblk):
                    def fv2(e, d=d, par=par, gct=gc_prev, st=sty_tick[d % 2]):
                        Wv(e, GC, gct)
                        Wv(e, sSTyp[d % 2], st)
                        ins = e.scalar_tensor_tensor(out=yac[d][:, :],
                                                     in0=uw[d][par][:, :],
                                                     scalar=Dsb[:, d:d + 1],
                                                     in1=yac[d][:, :],
                                                     op0=ALU.mult, op1=ALU.add)
                        chn(p, e, "vector", ins)
                        ins = e.tensor_tensor(out=ygt[d % 2][:, :], in0=yac[d][:, :],
                                              in1=szw[d][par][:, :], op=ALU.mult)
                        VC.inc(ins)
                    _mkbody(p, blk, "vector", fv2)

                    def fg2(e, d=d, b=b, t0=t0, ev=VC.v):
                        Wv(e, VC, ev)
                        r0 = (b * ndblk + d) * 128
                        ins = e.dma_start(out=y_out[r0:r0 + 128, t0:t0 + TC_],
                                          in_=ygt[d % 2][:, :])
                        sSTyp[d % 2].inc(ins, 16)
                    _mkbody(p, blk, "gpsimd", fg2)
                    sty_tick[d % 2] = sSTyp[d % 2].v
                vc2_last[par] = VC.v

        def fgend(e):
            for s in sSTyp:
                e.wait_ge(s.h, s.v)
        _mkbody(p, blk, "gpsimd", fgend)

    return p



def build_l3(cfg):
    p = Prog("l3")
    nc = p.nc
    dm, df, di = cfg.dm, cfg.df, cfg.di
    CH, Lq = cfg.CH, cfg.Lq
    nkm, nkf, nki = dm // 128, df // 128, di // 128
    ncH = Lq // CH

    y_in = nc.dram_tensor("y_in", [di, Lq], BF16, kind="ExternalInput").ap()
    xp_in = nc.dram_tensor("xp_in", [dm, Lq], F32, kind="ExternalInput").ap()
    woutt_in = nc.dram_tensor("woutt_in", [di, dm], F32, kind="ExternalInput").ap()
    w1q_in = nc.dram_tensor("w1q_in", [dm, df], BF16, kind="ExternalInput").ap()
    w2q_in = nc.dram_tensor("w2q_in", [df, dm], BF16, kind="ExternalInput").ap()
    b1_in = nc.dram_tensor("b1_in", [128, nkf], F32, kind="ExternalInput").ap()
    b2_in = nc.dram_tensor("b2_in", [128, nkm], F32, kind="ExternalInput").ap()

    x3_out = nc.dram_tensor("x3_out", [dm, Lq], F32, kind="ExternalOutput").ap()
    xpp_sp = nc.dram_tensor("xpp_sp", [dm, Lq], F32, kind="Internal").ap()
    h_sp = nc.dram_tensor("h_sp3", [df, Lq], BF16, kind="Internal").ap()

    chunks = [(i * CH, CH) for i in range(ncH)]

    with p.ctx:
        blk = p.ctx.enter_context(nc.Block())
        VC, SC, TCs = p.sem("VC"), p.sem("SC"), p.sem("TC")
        sLDw, sLDc = p.sem("sLDw"), p.sem("sLDc")
        sLDy = p.sem("sLDy")

        consts = mk_consts(p, cfg, nkm, nkf)
        emit_const_init(p, blk, consts)

        def f_ldc(e):
            ins = e.dma_start(out=consts["b1sb"][:, :], in_=b1_in)
            sLDc.inc(ins, 16)
            ins = e.dma_start(out=consts["b2h"][:, :], in_=b2_in)
            sLDc.inc(ins, 16)
        _mkbody(p, blk, "sync", f_ldc)

        def f_small(e):
            W(e, sLDc)
            ins = e.tensor_scalar(out=consts["b2h"][:, :], in0=consts["b2h"][:, :],
                                  scalar1=0.5, scalar2=None, op0=ALU.mult)
            VC.inc(ins)
        _mkbody(p, blk, "vector", f_small)

        sSTxppp = [p.sem("sSTxpp_%d" % i) for i in range(3)]
        with ExitStack() as st1:
            wob = [p.sb("wob_%d" % k, [128, dm], BF16, st1) for k in range(nki)]
            wscr = [p.sb("wsc1_%d" % i, [128, dm], F32, st1) for i in range(2)]
            emit_load_weight_cast(p, blk, woutt_in, wob, wscr, di, dm, sLDw)
            yb = [p.sb("yb_%d" % k, [128, CH], BF16, st1) for k in range(nki)]
            xpb = [p.sb("xpb_%d" % k, [128, CH], F32, st1) for k in range(nkm)]
            ps_o = [p.ps("ps_o%d" % i, [128, CH], st1) for i in range(3)]
            xppt = [p.sb("xppt_%d" % i, [128, CH], F32, st1) for i in range(3)]

            evo = {0: 0, 1: 0, 2: 0}
            sto = {0: 0, 1: 0, 2: 0}
            mmo_prev = 0
            evo_prev = 0
            wrdy1 = p.sem("VC").v
            for (off, N) in chunks:
                def fy(e, off=off, N=N, mmp=mmo_prev, evp=evo_prev):
                    if mmp == 0:
                        W(e, TCs, VC)
                    else:
                        Wv(e, TCs, mmp)
                        Wv(e, VC, evp)
                    for k in range(nki):
                        ins = e.dma_start(out=yb[k][:, :N],
                                          in_=y_in[k * 128:(k + 1) * 128, off:off + N])
                        sLDy.inc(ins, 16)
                    for k in range(nkm):
                        ins = e.dma_start(out=xpb[k][:, :N],
                                          in_=xp_in[k * 128:(k + 1) * 128, off:off + N])
                        sLDy.inc(ins, 16)
                _mkbody(p, blk, "sync", fy)
                ldt1 = sLDy.v

                for m in range(nkm):
                    def ft(e, m=m, N=N, ldt=ldt1, wt=max(wrdy1, evo[m % 3])):
                        Wv(e, sLDy, ldt)
                        Wv(e, VC, wt)
                        for k in range(nki):
                            ins = e.matmul(ps_o[m % 3][:, :N],
                                           wob[k][:, m * 128:(m + 1) * 128],
                                           yb[k][:, :N],
                                           start=(k == 0), stop=(k == nki - 1))
                        TCs.inc(ins)
                    _mkbody(p, blk, "tensor", ft)
                    mmt = TCs.v

                    def fv(e, m=m, N=N, mmt=mmt, st=sto[m % 3], ldt=ldt1):
                        Wv(e, TCs, mmt)
                        Wv(e, sLDy, ldt)
                        Wv(e, sSTxppp[m % 3], st)
                        ins = e.tensor_tensor(out=xppt[m % 3][:, :N],
                                              in0=ps_o[m % 3][:, :N],
                                              in1=xpb[m][:, :N], op=ALU.add)
                        VC.inc(ins)
                    _mkbody(p, blk, "vector", fv)
                    evo[m % 3] = p.sem("VC").v

                    def fg(e, m=m, off=off, N=N, ev=evo[m % 3]):
                        Wv(e, VC, ev)
                        ins = e.dma_start(
                            out=xpp_sp[m * 128:(m + 1) * 128, off:off + N],
                            in_=xppt[m % 3][:, :N])
                        sSTxppp[m % 3].inc(ins, 16)
                    _mkbody(p, blk, "gpsimd", fg)
                    sto[m % 3] = sSTxppp[m % 3].v
                mmo_prev = TCs.v
                evo_prev = p.sem("VC").v

        def fybar(e):
            for s in sSTxppp:
                e.wait_ge(s.h, s.v)
        _mkbody(p, blk, "sync", fybar)

        emit_ffn(p, blk, cfg, xpp_sp, x3_out, h_sp, None, None, chunks,
                 consts, "f2", 0.5, wq_src=(w1q_in, w2q_in))

        def fgend(e):
            for nm in ("sSTof2_0", "sSTof2_1", "sSTof2_2",
                       "sSThf2_0", "sSThf2_1", "sSThf2_2",
                       "sSTxpp_0", "sSTxpp_1", "sSTxpp_2"):
                s = p.sem(nm)
                e.wait_ge(s.h, s.v)
        _mkbody(p, blk, "gpsimd", fgend)

    return p



def _t(a):
    return np.ascontiguousarray(np.asarray(a).T)


def stage_l1(cfg, inputs):
    x = np.asarray(inputs["L0_attn_hidden_states"])
    W1 = np.asarray(inputs["L0_ffn_W1"])
    W2 = np.asarray(inputs["L0_ffn_W2"])
    b1 = np.asarray(inputs["L0_ffn_b1"])
    b2 = np.asarray(inputs["L0_ffn_b2"])
    Win = np.asarray(inputs["L0_attn_W_in"])
    cw = np.asarray(inputs["L0_attn_conv_w"])[:, 0, :]
    cb = np.asarray(inputs["L0_attn_conv_b"])
    Wx = np.asarray(inputs["L0_attn_W_x"])
    Wdt = np.asarray(inputs["L0_attn_W_dt"])
    dtb = np.asarray(inputs["L0_attn_dt_bias"])

    dm, df, di = cfg.dm, cfg.df, cfg.di
    nkf, nkm, nki = df // 128, dm // 128, di // 128
    w1t, w2t, wint, wdtt = _t(W1), _t(W2), _t(Win), _t(Wdt)
    wxt_raw = _t(Wx)
    wxt = np.zeros((di, 128), np.float32)
    wxt[:, 0:cfg.dr] = wxt_raw[:, 0:cfg.dr]
    wxt[:, 64:64 + cfg.ds] = wxt_raw[:, cfg.dr:cfg.dr + cfg.ds]
    wxt[:, 96:96 + cfg.ds] = wxt_raw[:, cfg.dr + cfg.ds:cfg.dr + 2 * cfg.ds]
    b1m = np.ascontiguousarray(b1.reshape(nkf, 128).T)
    b2m = np.ascontiguousarray(b2.reshape(nkm, 128).T)
    cwm = np.ascontiguousarray(
        cw.reshape(nki, 128, 4).transpose(1, 2, 0).reshape(128, 4 * nki))
    cbm = np.ascontiguousarray(cb.reshape(nki, 128).T)
    dtbm = np.ascontiguousarray(dtb.reshape(nki, 128).T)
    eye = np.eye(128, dtype=np.float32)

    maps = []
    for c in range(cfg.nc):
        b, q = divmod(c, cfg.nq)
        xt = _t(x[b])
        qs = q * cfg.Lq
        slab = np.zeros((dm, cfg.Lq + 3), np.float32)
        slab[:, 3:] = xt[:, qs:qs + cfg.Lq]
        if q > 0:
            slab[:, 0:3] = xt[:, qs - 3:qs]
        hmask = np.full((128, 1), 0.0 if q == 0 else 1.0, np.float32)
        maps.append(dict(x_in=slab, hmask_in=hmask, w1t_in=w1t, w2t_in=w2t,
                         b1_in=b1m, b2_in=b2m, wint_in=wint, cw_in=cwm,
                         cb_in=cbm, wxt_in=wxt, wdtt_in=wdtt, dtb_in=dtbm,
                         eye_in=eye))
    return maps


def stage_l2(cfg, inputs, l1res):
    A_log = np.asarray(inputs["L0_attn_A_log"])
    Dv = np.asarray(inputs["L0_attn_D"])

    def seq(key):
        return [np.concatenate([np.asarray(l1res[b * cfg.nq + q][key])
                                for q in range(cfg.nq)], axis=1)
                for b in range(cfg.nb)]

    delta_b, u_b, sz_b = seq("delta_out"), seq("u_out"), seq("sz_out")
    B_b, C_b = seq("b_out"), seq("c_out")
    bmins = np.concatenate([np.asarray(l1res[c]["bmin_out"]).reshape(-1)
                            for c in range(cfg.nc)]).reshape(1, -1)
    sel_pat = np.zeros((128, cfg.ds * 128), np.float32)
    for n in range(cfg.ds):
        sel_pat[n, n * 128:(n + 1) * 128] = 1.0
    bmins = np.ascontiguousarray(np.broadcast_to(bmins, (128, bmins.shape[1])))
    b_cat = np.ascontiguousarray(np.concatenate(B_b, axis=0))
    c_cat = np.ascontiguousarray(np.concatenate(C_b, axis=0))

    maps = []
    for s in range(cfg.nc):
        r0 = s * cfg.dpc

        def sl(arrs):
            return np.ascontiguousarray(
                np.concatenate([arrs[b][r0:r0 + cfg.dpc, :]
                                for b in range(cfg.nb)], axis=0))

        alog_s = np.ascontiguousarray(A_log[r0:r0 + cfg.dpc, :])
        d_s = np.ascontiguousarray(Dv[r0:r0 + cfg.dpc].reshape(cfg.ndblk, 128).T)
        maps.append(dict(delta_in=sl(delta_b), u_in=sl(u_b), sz_in=sl(sz_b),
                         b_in=b_cat, c_in=c_cat, alog_in=alog_s, d_in=d_s,
                         bmins_in=bmins, sel_in=sel_pat))
    return maps


def stage_l3(cfg, inputs, l1res, l2res):
    Wout = np.asarray(inputs["L0_attn_W_out"])
    b1 = np.asarray(inputs["L0_ffn_b1"])
    b2 = np.asarray(inputs["L0_ffn_b2"])
    dm, df, di = cfg.dm, cfg.df, cfg.di
    nkf, nkm = df // 128, dm // 128
    woutt = _t(Wout)
    w1q = np.asarray(l1res[0]["w1q_out"])
    w2q = np.asarray(l1res[0]["w2q_out"])
    b1m = np.ascontiguousarray(b1.reshape(nkf, 128).T)
    b2m = np.ascontiguousarray(b2.reshape(nkm, 128).T)

    y_full = []
    for b in range(cfg.nb):
        yb = np.zeros((di, cfg.L), dtype=ml_dtypes.bfloat16)
        for s in range(cfg.nc):
            ys = np.asarray(l2res[s]["y_out"])
            for d in range(cfg.ndblk):
                plane = b * cfg.ndblk + d
                yb[s * cfg.dpc + d * 128: s * cfg.dpc + (d + 1) * 128, :] = \
                    ys[plane * 128:(plane + 1) * 128, :]
        y_full.append(yb)

    maps = []
    for c in range(cfg.nc):
        b, q = divmod(c, cfg.nq)
        qs = q * cfg.Lq
        y_c = np.ascontiguousarray(y_full[b][:, qs:qs + cfg.Lq])
        xp_c = np.ascontiguousarray(np.asarray(l1res[c]["xp_out"])[:, 3:])
        maps.append(dict(y_in=y_c, xp_in=xp_c, woutt_in=woutt, w1q_in=w1q,
                         w2q_in=w2q, b1_in=b1m, b2_in=b2m))
    return maps


_CACHE = {}


def _progs(cfg):
    key = (cfg.dm, cfg.Lq)
    if key not in _CACHE:
        _CACHE[key] = (build_l1(cfg), build_l2(cfg), build_l3(cfg))
    return _CACHE[key]


def kernel(**inputs):
    cfg = FULL
    p1, p2, p3 = _progs(cfg)
    cores = list(range(cfg.nc))

    m1 = stage_l1(cfg, inputs)
    r1 = run_bass_kernel_spmd(p1.nc, m1, core_ids=cores).results
    m2 = stage_l2(cfg, inputs, r1)
    r2 = run_bass_kernel_spmd(p2.nc, m2, core_ids=cores).results
    m3 = stage_l3(cfg, inputs, r1, r2)
    r3 = run_bass_kernel_spmd(p3.nc, m3, core_ids=cores).results

    out = np.empty((cfg.nb, cfg.L, cfg.dm), np.float32)
    for c in range(cfg.nc):
        b, q = divmod(c, cfg.nq)
        out[b, q * cfg.Lq:(q + 1) * cfg.Lq, :] = np.asarray(r3[c]["x3_out"]).T
    return out


# version 18 (final)
# speedup vs baseline: 1.1665x; 1.0107x over previous
import numpy as np
import ml_dtypes
import concourse.bass as bass
import concourse.mybir as mybir
from concourse.bass_utils import run_bass_kernel_spmd
from contextlib import ExitStack

F32 = mybir.dt.float32
BF16 = mybir.dt.bfloat16
AF = mybir.ActivationFunctionType
ALU = mybir.AluOpType
AX = mybir.AxisListType

LN_EPS = 1e-5
Q_EPS = 1e-5
RACE_CHECK = False


class CFG:
    def __init__(self, dm=1024, df=4096, di=2048, ds=16, dr=64, nb=2, nq=4,
                 Lq=4096, CH=512, TC=1024):
        self.dm, self.df, self.di, self.ds, self.dr = dm, df, di, ds, dr
        self.nb, self.nq = nb, nq
        self.nc = nb * nq
        self.Lq, self.CH, self.TC = Lq, CH, TC
        self.L = Lq * nq
        self.dpc = di // self.nc
        assert self.dpc % 128 == 0
        self.ndblk = self.dpc // 128


FULL = CFG()
MINI = CFG(dm=256, df=512, di=256, ds=16, dr=64, nb=1, nq=2, Lq=128, CH=64, TC=64)


class Sem:
    __slots__ = ("h", "v")

    def __init__(self, h):
        self.h = h
        self.v = 0

    def inc(self, instr, k=1):
        instr.then_inc(self.h, k)
        self.v += k


def W(eng, *sems):
    for s in sems:
        if s.v > 0:
            eng.wait_ge(s.h, s.v)


def chn(p, e, eng, ins):
    s = p.sem(_ENG_SEM[eng])
    s.inc(ins)
    e.wait_ge(s.h, s.v)


class Prog:
    def __init__(self, name):
        self.nc = bass.Bass("TRN2", target_bir_lowering=False,
                            detect_race_conditions=RACE_CHECK)
        self.ctx = ExitStack()
        self.name = name
        self.sems = {}

    def sem(self, name):
        if name not in self.sems:
            self.sems[name] = Sem(self.ctx.enter_context(self.nc.semaphore(name)))
        return self.sems[name]

    def sb(self, name, shape, dt, st=None):
        return (st or self.ctx).enter_context(
            self.nc.sbuf_tensor(name, list(shape), dt))

    def ps(self, name, shape, st=None):
        return (st or self.ctx).enter_context(
            self.nc.psum_tensor(name, list(shape), F32))


_ENG_SEM = {"vector": "VC", "scalar": "SC", "tensor": "TC", "gpsimd": "GC",
            "sync": None}


def _mkbody(p, blk, eng, fn):
    nm = _ENG_SEM[eng]
    s = p.sem(nm) if nm else None

    def wrapped(e):
        if s is not None and s.v > 0:
            e.wait_ge(s.h, s.v)
        fn(e)
    getattr(blk, eng)(wrapped)



_QCNT = 0


def emit_load_weight_cast(p, blk, dram_ap, dst_tiles, scratch, rows, cols, sLD):
    nk = rows // 128
    for k in range(nk):
        par = k % len(scratch)

        def fy(e, k=k, par=par):
            W(e, p.sem("VC"))
            ins = e.dma_start(out=scratch[par][:, :cols],
                              in_=dram_ap[k * 128:(k + 1) * 128, :])
            sLD.inc(ins, 16)
        _mkbody(p, blk, "sync", fy)

        def fv(e, k=k, par=par):
            W(e, sLD)
            ins = e.tensor_copy(out=dst_tiles[k][:, :cols], in_=scratch[par][:, :cols])
            p.sem("VC").inc(ins)
        _mkbody(p, blk, "vector", fv)


def emit_quantize(p, blk, dram_ap, dst_tiles, scratch, rows, cols, sLD, consts,
                  tag):
    nk = rows // 128
    
    redsum = consts["redsum"]
    gsb = consts["gsb"]
    ginv = consts["ginv_" + tag]
    onesall = consts["onesall"]
    ps_s = consts["ps_small"]
    global _QCNT
    _QCNT += 1
    qst = ExitStack()
    t1 = p.sb("qt1_%d" % _QCNT, [128, cols], F32, qst)

    def fvz(e):
        ins = e.memset(redsum[:, :], 0.0)
        p.sem("VC").inc(ins)
    _mkbody(p, blk, "vector", fvz)

    for k in range(nk):
        par = k % len(scratch)

        def fy(e, k=k, par=par):
            W(e, p.sem("VC"))
            ins = e.dma_start(out=scratch[par][:, :cols],
                              in_=dram_ap[k * 128:(k + 1) * 128, :])
            sLD.inc(ins, 16)
        _mkbody(p, blk, "sync", fy)

        def fv(e, k=k, par=par):
            W(e, sLD)
            ins = e.tensor_reduce(out=redsum[:, k:k + 1], in_=scratch[par][:, :cols],
                                  axis=AX.X, op=ALU.add, apply_absolute_value=True)
            p.sem("VC").inc(ins)
        _mkbody(p, blk, "vector", fv)

    ncols = redsum.shape[1]
    rhi = consts["x2"][0]
    rlo = consts["x2"][1]
    rdf = consts["vscr"]

    def fvhl(e):
        ins = e.tensor_copy(out=rhi[:, :ncols], in_=redsum[:, :ncols])
        chn(p, e, "vector", ins)
        ins = e.tensor_tensor(out=rdf[:, :ncols], in0=redsum[:, :ncols],
                              in1=rhi[:, :ncols], op=ALU.subtract)
        chn(p, e, "vector", ins)
        ins = e.tensor_copy(out=rlo[:, :ncols], in_=rdf[:, :ncols])
        p.sem("VC").inc(ins)
    _mkbody(p, blk, "vector", fvhl)

    def ft(e):
        W(e, p.sem("VC"))
        e.matmul(ps_s[:, :ncols], onesall[:, :], rhi[:, :ncols],
                 start=True, stop=False)
        ins = e.matmul(ps_s[:, :ncols], onesall[:, :], rlo[:, :ncols],
                       start=False, stop=True)
        p.sem("TC").inc(ins)
    _mkbody(p, blk, "tensor", ft)

    def fv4(e):
        W(e, p.sem("TC"))
        ins = e.tensor_reduce(out=gsb[:, :], in_=ps_s[:, :ncols], axis=AX.X,
                              op=ALU.add)
        chn(p, e, "vector", ins)
        ins = e.tensor_scalar(out=gsb[:, :], in0=gsb[:, :],
                              scalar1=1.0 / (rows * cols),
                              scalar2=Q_EPS, op0=ALU.mult, op1=ALU.add)
        chn(p, e, "vector", ins)
        ins = e.reciprocal(out=ginv[:, :], in_=gsb[:, :])
        p.sem("VC").inc(ins)
    _mkbody(p, blk, "vector", fv4)

    for k in range(nk):
        par = k % len(scratch)

        def fy(e, k=k, par=par):
            W(e, p.sem("VC"))
            ins = e.dma_start(out=scratch[par][:, :cols],
                              in_=dram_ap[k * 128:(k + 1) * 128, :])
            sLD.inc(ins, 16)
        _mkbody(p, blk, "sync", fy)

        def fv(e, k=k, par=par):
            W(e, sLD)
            ins = e.tensor_scalar(out=scratch[par][:, :cols],
                                  in0=scratch[par][:, :cols],
                                  scalar1=ginv[:, 0:1], scalar2=None, op0=ALU.mult)
            chn(p, e, "vector", ins)
            ins = e.tensor_scalar(out=t1[:, :cols], in0=scratch[par][:, :cols],
                                  scalar1=0.5, scalar2=None, op0=ALU.is_gt)
            chn(p, e, "vector", ins)
            ins = e.tensor_copy(out=dst_tiles[k][:, :cols], in_=t1[:, :cols])
            chn(p, e, "vector", ins)
            ins = e.tensor_scalar(out=t1[:, :cols], in0=scratch[par][:, :cols],
                                  scalar1=-0.5, scalar2=None, op0=ALU.is_lt)
            chn(p, e, "vector", ins)
            ins = e.tensor_tensor(out=dst_tiles[k][:, :cols],
                                  in0=dst_tiles[k][:, :cols],
                                  in1=t1[:, :cols], op=ALU.subtract)
            p.sem("VC").inc(ins)
        _mkbody(p, blk, "vector", fv)
    qst.close()


def Wv(e, sem, val):
    if val > 0:
        e.wait_ge(sem.h, val)


def mk_qstream(p, blk, dram_ap, dst_tiles, scratch, rows, cols, consts, tag,
               sta):
    nk = rows // 128
    redsum = consts["redsum"]
    gsb = consts["gsb"]
    ginv = consts["ginv_" + tag]
    onesall = consts["onesall"]
    ps_s = consts["ps_small"]
    VC, TCq = p.sem("VC"), p.sem("TC")
    sLDqp = [p.sem("sLDq%s_%d" % (tag, i)) for i in range(len(scratch))]
    global _QCNT
    _QCNT += 1
    t1 = p.sb("qs1_%d" % _QCNT, [128, cols], F32, sta)
    rhi = p.sb("qrh_%d" % _QCNT, [128, nk], BF16, sta)
    rlo = p.sb("qrl_%d" % _QCNT, [128, nk], BF16, sta)
    rdf = p.sb("qrd_%d" % _QCNT, [128, nk], F32, sta)
    base = VC.v
    state = {"red": {0: base, 1: base}, "q2": 0}

    def s_init():
        def fvz(e):
            ins = e.memset(redsum[:, :nk], 0.0)
            VC.inc(ins)
        _mkbody(p, blk, "vector", fvz)

    def s_p1(k):
        par = k % len(scratch)

        def fy(e, k=k, par=par, wt=state["red"][par]):
            Wv(e, VC, wt)
            ins = e.dma_start(out=scratch[par][:, :cols],
                              in_=dram_ap[k * 128:(k + 1) * 128, :])
            sLDqp[par].inc(ins, 16)
        _mkbody(p, blk, "sync", fy)
        ldt = sLDqp[par].v

        def fv(e, k=k, par=par, ldt=ldt):
            Wv(e, sLDqp[par], ldt)
            ins = e.tensor_reduce(out=redsum[:, k:k + 1],
                                  in_=scratch[par][:, :cols],
                                  axis=AX.X, op=ALU.add,
                                  apply_absolute_value=True)
            VC.inc(ins)
        _mkbody(p, blk, "vector", fv)
        state["red"][par] = VC.v

    def s_gsb():
        def fvhl(e):
            ins = e.tensor_copy(out=rhi[:, :], in_=redsum[:, :nk])
            chn(p, e, "vector", ins)
            ins = e.tensor_tensor(out=rdf[:, :], in0=redsum[:, :nk],
                                  in1=rhi[:, :], op=ALU.subtract)
            chn(p, e, "vector", ins)
            ins = e.tensor_copy(out=rlo[:, :], in_=rdf[:, :])
            VC.inc(ins)
        _mkbody(p, blk, "vector", fvhl)
        hlt = VC.v

        def ft(e, wt=hlt):
            Wv(e, VC, wt)
            e.matmul(ps_s[:, :nk], onesall[:, :], rhi[:, :],
                     start=True, stop=False)
            ins = e.matmul(ps_s[:, :nk], onesall[:, :], rlo[:, :],
                           start=False, stop=True)
            TCq.inc(ins)
        _mkbody(p, blk, "tensor", ft)
        mmt = TCq.v

        def fv4(e, wt=mmt):
            Wv(e, TCq, wt)
            ins = e.tensor_reduce(out=gsb[:, :], in_=ps_s[:, :nk], axis=AX.X,
                                  op=ALU.add)
            chn(p, e, "vector", ins)
            ins = e.tensor_scalar(out=gsb[:, :], in0=gsb[:, :],
                                  scalar1=1.0 / (rows * cols),
                                  scalar2=Q_EPS, op0=ALU.mult, op1=ALU.add)
            chn(p, e, "vector", ins)
            ins = e.reciprocal(out=ginv[:, :], in_=gsb[:, :])
            VC.inc(ins)
        _mkbody(p, blk, "vector", fv4)
        state["q2"] = VC.v

    def s_p2(k):
        par = k % len(scratch)

        def fy(e, k=k, par=par, wt=state["q2"]):
            Wv(e, VC, wt)
            ins = e.dma_start(out=scratch[par][:, :cols],
                              in_=dram_ap[k * 128:(k + 1) * 128, :])
            sLDqp[par].inc(ins, 16)
        _mkbody(p, blk, "sync", fy)
        ldt = sLDqp[par].v

        def fv(e, k=k, par=par, ldt=ldt):
            Wv(e, sLDqp[par], ldt)
            ins = e.tensor_scalar(out=scratch[par][:, :cols],
                                  in0=scratch[par][:, :cols],
                                  scalar1=ginv[:, 0:1], scalar2=None,
                                  op0=ALU.mult)
            chn(p, e, "vector", ins)
            ins = e.tensor_scalar(out=t1[:, :cols], in0=scratch[par][:, :cols],
                                  scalar1=0.5, scalar2=None, op0=ALU.is_gt)
            chn(p, e, "vector", ins)
            ins = e.tensor_copy(out=dst_tiles[k][:, :cols], in_=t1[:, :cols])
            chn(p, e, "vector", ins)
            ins = e.tensor_scalar(out=t1[:, :cols], in0=scratch[par][:, :cols],
                                  scalar1=-0.5, scalar2=None, op0=ALU.is_lt)
            chn(p, e, "vector", ins)
            ins = e.tensor_tensor(out=dst_tiles[k][:, :cols],
                                  in0=dst_tiles[k][:, :cols],
                                  in1=t1[:, :cols], op=ALU.subtract)
            VC.inc(ins)
        _mkbody(p, blk, "vector", fv)
        state["q2"] = VC.v

    steps = [s_init]
    for k in range(nk):
        steps.append(lambda k=k: s_p1(k))
    steps.append(s_gsb)
    for k in range(nk):
        steps.append(lambda k=k: s_p2(k))
    return steps, state


def emit_ln(p, blk, cfg, xb_tiles, out_tiles, N, consts, ready=()):
    nk = cfg.dm // 128
    VC, SC, TCs = p.sem("VC"), p.sem("SC"), p.sem("TC")
    ps_s1 = consts["lnps_s1"]
    ps_s2 = consts["lnps_s2"]
    onesall = consts["onesall"]
    x2 = consts["x2"]
    xbb = consts["xbb"]
    m_rep = consts["m_rep"]
    r_rep = consts["r_rep"]
    ckb = consts["ckb"]

    def fvc(e):
        for s, v in ready:
            Wv(e, s, v)
        for k in range(nk):
            ins = e.tensor_copy(out=xbb[k][:, :N], in_=xb_tiles[k][:, :N])
        VC.inc(ins)
    _mkbody(p, blk, "vector", fvc)
    tick = {"cast": p.sem("VC").v}

    lntmp = consts["lntmp"]
    vscr = consts["vscr"]

    def fsq(e):
        for s, v in ready:
            Wv(e, s, v)
        for k in range(nk):
            ins = e.activation(out=x2[k][:, :N], in_=xb_tiles[k][:, :N],
                               func=AF.Square)
        SC.inc(ins)
    _mkbody(p, blk, "scalar", fsq)
    tick["sq"] = SC.v

    def ft0(e, tc=tick["cast"], ts=tick["sq"]):
        Wv(e, VC, tc)
        Wv(e, SC, ts)
        for k in range(nk):
            e.matmul(ps_s1[:, :N], onesall[:, :], xbb[k][:, :N],
                     start=(k == 0), stop=(k == nk - 1))
        for k in range(nk):
            ins = e.matmul(ps_s2[:, :N], onesall[:, :], x2[k][:, :N],
                           start=(k == 0), stop=(k == nk - 1))
        TCs.inc(ins)
    _mkbody(p, blk, "tensor", ft0)

    def fv1(e):
        W(e, TCs)
        e.tensor_scalar(out=m_rep[:, :N], in0=ps_s1[:, :N], scalar1=1.0 / cfg.dm,
                        scalar2=None, op0=ALU.mult)
        ins = e.tensor_scalar(out=lntmp[:, :N], in0=ps_s2[:, :N],
                              scalar1=1.0 / cfg.dm, scalar2=None, op0=ALU.mult)
        chn(p, e, "vector", ins)
        ins = e.tensor_tensor(out=vscr[:, :N], in0=m_rep[:, :N],
                              in1=m_rep[:, :N], op=ALU.mult)
        chn(p, e, "vector", ins)
        ins = e.tensor_tensor(out=lntmp[:, :N], in0=lntmp[:, :N],
                              in1=vscr[:, :N], op=ALU.subtract)
        chn(p, e, "vector", ins)
        ins = e.tensor_scalar(out=lntmp[:, :N], in0=lntmp[:, :N], scalar1=0.0,
                              scalar2=None, op0=ALU.max)
        VC.inc(ins)
    _mkbody(p, blk, "vector", fv1)

    def fs2(e):
        W(e, VC)
        ins = e.activation(out=r_rep[:, :N], in_=lntmp[:, :N], func=AF.Ln,
                            bias=ckb[:, CK_EPS:CK_EPS + 1])
        chn(p, e, "scalar", ins)
        ins = e.activation(out=r_rep[:, :N], in_=r_rep[:, :N], func=AF.Exp,
                           scale=ckb[:, CK_NHALF:CK_NHALF + 1])
        SC.inc(ins)
    _mkbody(p, blk, "scalar", fs2)

    def fv3(e):
        W(e, SC)
        tmp = consts["lntmp"]
        for k in range(nk):
            ins = e.tensor_tensor(out=tmp[:, :N], in0=xb_tiles[k][:, :N],
                                  in1=m_rep[:, :N], op=ALU.subtract)
            chn(p, e, "vector", ins)
            ins = e.tensor_tensor(out=out_tiles[k][:, :N], in0=tmp[:, :N],
                                  in1=r_rep[:, :N], op=ALU.mult)
            chn(p, e, "vector", ins)
    _mkbody(p, blk, "vector", fv3)
    tick["apply"] = p.sem("VC").v
    return tick


def emit_ffn(p, blk, cfg, x_dram, out_dram, h_dram, w1t_in, w2t_in, chunks,
             consts, tag, resid_scale=0.5, wq_src=None, wq_out=None):
    nc = p.nc
    dm, df = cfg.dm, cfg.df
    CH = cfg.CH
    nkm, nkf = dm // 128, df // 128
    VC, SC, TCs = p.sem("VC"), p.sem("SC"), p.sem("TC")
    sLDx = p.sem("sLDx" + tag)
    sLDh = p.sem("sLDh" + tag)
    sSThp = [p.sem("sSTh%s_%d" % (tag, i)) for i in range(3)]
    sSTop = [p.sem("sSTo%s_%d" % (tag, i)) for i in range(3)]
    sLDw = p.sem("sLDw")
    b1sb, b2h = consts["b1sb"], consts["b2h"]

    sSTq1 = p.sem("sSTq1" + tag) if wq_out is not None else None
    sSTq2 = p.sem("sSTq2" + tag) if wq_out is not None else None
    stf = ExitStack()
    w2q = [p.sb("w2qB%s_%d" % (tag, k), [128, dm], BF16, stf) for k in range(nkf)]
    qstate = None
    with ExitStack() as sta:
        w1q = [p.sb("w1qA%s_%d" % (tag, k), [128, df], BF16, sta) for k in range(nkm)]
        if wq_src is None:
            stw = ExitStack()
            wscr = [p.sb("wscA%s_%d" % (tag, i), [128, df], F32, stw)
                    for i in range(2)]
            emit_quantize(p, blk, w1t_in, w1q, wscr, dm, df, sLDw, consts, "w1")
            stw.close()
        else:
            def fyq(e):
                W(e, VC)
                for k in range(nkm):
                    ins = e.dma_start(out=w1q[k][:, :],
                                      in_=wq_src[0][k * 128:(k + 1) * 128, :])
                    sLDw.inc(ins, 16)
                for k in range(nkf):
                    ins = e.dma_start(out=w2q[k][:, :],
                                      in_=wq_src[1][k * 128:(k + 1) * 128, :])
                    sLDw.inc(ins, 16)
            _mkbody(p, blk, "sync", fyq)

            def fvq(e):
                W(e, sLDw)
                VC.inc(e.nop())
            _mkbody(p, blk, "vector", fvq)
        qdA = VC.v
        if wq_out is not None:
            def fgq(e, qd=qdA):
                Wv(e, VC, qd)
                for k in range(nkm):
                    ins = e.dma_start(out=wq_out[0][k * 128:(k + 1) * 128, :],
                                      in_=w1q[k][:, :])
                    sSTq1.inc(ins, 16)
            _mkbody(p, blk, "gpsimd", fgq)
        xb = [p.sb("xbA%s_%d" % (tag, k), [128, CH], F32, sta) for k in range(nkm)]
        lnb = [p.sb("lnA%s_%d" % (tag, k), [128, CH], BF16, sta) for k in range(nkm)]
        hb = [p.sb("hbA%s_%d" % (tag, i), [128, CH], BF16, sta) for i in range(3)]
        ps_f = [p.ps("psA%s_%d" % (tag, i), [128, CH], sta) for i in range(3)]
        if wq_src is None:
            qscr = [p.sb("qsc%s_%d" % (tag, i), [128, dm], F32, sta)
                    for i in range(2)]
            qsteps, qstate = mk_qstream(p, blk, w2t_in, w2q, qscr, df, dm,
                                        consts, "w2", sta)
        else:
            qsteps = []

        evbufA = {0: 0, 1: 0, 2: 0}
        stbufA = {0: 0, 1: 0, 2: 0}
        prev = None
        for (off, N) in chunks:
            def fy(e, off=off, N=N, prev=prev):
                if prev is not None:
                    Wv(e, VC, prev["apply"])
                    Wv(e, SC, prev["sq"])
                else:
                    Wv(e, VC, qdA)
                for k in range(nkm):
                    ins = e.dma_start(out=xb[k][:, :N],
                                      in_=x_dram[k * 128:(k + 1) * 128, off:off + N])
                    sLDx.inc(ins, 16)
            _mkbody(p, blk, "sync", fy)
            ldt = sLDx.v

            prev = emit_ln(p, blk, cfg, xb, lnb, N, consts,
                           ready=((sLDx, ldt),))

            for m in range(nkf):
                def ft(e, m=m, N=N, at=prev["apply"], ev=evbufA[m % 3]):
                    Wv(e, VC, max(at, ev))
                    for k in range(nkm):
                        ins = e.matmul(ps_f[m % 3][:, :N],
                                       w1q[k][:, m * 128:(m + 1) * 128],
                                       lnb[k][:, :N], start=(k == 0),
                                       stop=(k == nkm - 1))
                    TCs.inc(ins)
                _mkbody(p, blk, "tensor", ft)
                mmt = TCs.v

                def fv(e, m=m, N=N, mmt=mmt, st=stbufA[m % 3]):
                    Wv(e, TCs, mmt)
                    Wv(e, sSThp[m % 3], st)
                    ins = e.tensor_scalar(out=hb[m % 3][:, :N],
                                          in0=ps_f[m % 3][:, :N],
                                          scalar1=b1sb[:, m:m + 1], scalar2=0.0,
                                          op0=ALU.add, op1=ALU.max)
                    VC.inc(ins)
                _mkbody(p, blk, "vector", fv)
                evbufA[m % 3] = VC.v

                def fg(e, m=m, off=off, N=N, ev=evbufA[m % 3]):
                    Wv(e, VC, ev)
                    ins = e.dma_start(out=h_dram[m * 128:(m + 1) * 128, off:off + N],
                                      in_=hb[m % 3][:, :N])
                    sSThp[m % 3].inc(ins, 16)
                _mkbody(p, blk, "gpsimd", fg)
                stbufA[m % 3] = sSThp[m % 3].v
                if m % 3 == 2 and qsteps:
                    qsteps.pop(0)()
        while qsteps:
            qsteps.pop(0)()

    def fvhsync(e):
        W(e, *sSThp)
        if sSTq1 is not None:
            W(e, sSTq1)
        VC.inc(e.nop())
    _mkbody(p, blk, "vector", fvhsync)
    hsync_t = VC.v

    with ExitStack() as stb:
        qdB = qstate["q2"] if qstate is not None else qdA
        if wq_out is not None:
            def fgq2(e, qd=qdB):
                Wv(e, VC, qd)
                for k in range(nkf):
                    ins = e.dma_start(out=wq_out[1][k * 128:(k + 1) * 128, :],
                                      in_=w2q[k][:, :])
                    sSTq2.inc(ins, 16)
            _mkbody(p, blk, "gpsimd", fgq2)
        hB = [[p.sb("hBB%s_%d_%d" % (tag, j, k), [128, CH], BF16, stb)
               for k in range(nkf)] for j in range(2)]
        xb = [p.sb("xbB%s_%d" % (tag, k), [128, CH], F32, stb) for k in range(nkm)]
        ps_f = [p.ps("psB%s_%d" % (tag, i), [128, CH], stb) for i in range(3)]
        ot = [p.sb("otB%s_%d" % (tag, i), [128, CH], F32, stb) for i in range(3)]
        sLDxB = p.sem("sLDxB" + tag)
        sLDhp = [p.sem("sLDh%s_%d" % (tag, i)) for i in range(2)]

        evbufB = {0: 0, 1: 0, 2: 0}
        stbufB = {0: 0, 1: 0, 2: 0}
        mm2last = {0: 0, 1: 0}
        evlast = 0
        for ci, (off, N) in enumerate(chunks):
            cpar = ci % 2

            def fy(e, off=off, N=N, ci=ci, cpar=cpar, mml=mm2last[cpar]):
                if ci == 0:
                    Wv(e, VC, hsync_t)
                    for s in sSThp:
                        e.wait_ge(s.h, s.v)
                else:
                    Wv(e, TCs, mml)
                for k in range(nkf):
                    ins = e.dma_start(out=hB[cpar][k][:, :N],
                                      in_=h_dram[k * 128:(k + 1) * 128, off:off + N])
                    sLDhp[cpar].inc(ins, 16)
            _mkbody(p, blk, "sync", fy)
            ldt = sLDhp[cpar].v

            def fyx(e, off=off, N=N, ci=ci, evlast=evlast):
                if ci == 0:
                    Wv(e, VC, hsync_t)
                else:
                    Wv(e, VC, evlast)
                for k in range(nkm):
                    ins = e.dma_start(out=xb[k][:, :N],
                                      in_=x_dram[k * 128:(k + 1) * 128, off:off + N])
                    sLDxB.inc(ins, 16)
            _mkbody(p, blk, "sync", fyx)
            ldtx = sLDxB.v

            for m in range(nkm):
                def ft(e, m=m, N=N, ldt=ldt, cpar=cpar, ev=evbufB[m % 3]):
                    Wv(e, sLDhp[cpar], ldt)
                    Wv(e, VC, max(ev, qdB))
                    for k in range(nkf):
                        ins = e.matmul(ps_f[m % 3][:, :N],
                                       w2q[k][:, m * 128:(m + 1) * 128],
                                       hB[cpar][k][:, :N], start=(k == 0),
                                       stop=(k == nkf - 1))
                    TCs.inc(ins)
                _mkbody(p, blk, "tensor", ft)
                mmt = TCs.v

                def fv(e, m=m, N=N, mmt=mmt, ldt=ldtx, st=stbufB[m % 3]):
                    Wv(e, TCs, mmt)
                    Wv(e, sLDxB, ldt)
                    Wv(e, sSTop[m % 3], st)
                    ins = e.tensor_scalar(out=ot[m % 3][:, :N],
                                          in0=ps_f[m % 3][:, :N],
                                          scalar1=resid_scale,
                                          scalar2=b2h[:, m:m + 1],
                                          op0=ALU.mult, op1=ALU.add)
                    chn(p, e, "vector", ins)
                    ins = e.tensor_tensor(out=ot[m % 3][:, :N], in0=ot[m % 3][:, :N],
                                          in1=xb[m][:, :N], op=ALU.add)
                    VC.inc(ins)
                _mkbody(p, blk, "vector", fv)
                evbufB[m % 3] = VC.v

                def fg(e, m=m, off=off, N=N, ev=evbufB[m % 3]):
                    Wv(e, VC, ev)
                    ins = e.dma_start(out=out_dram[m * 128:(m + 1) * 128, off:off + N],
                                      in_=ot[m % 3][:, :N])
                    sSTop[m % 3].inc(ins, 16)
                _mkbody(p, blk, "gpsimd", fg)
                stbufB[m % 3] = sSTop[m % 3].v
            mm2last[cpar] = TCs.v
            evlast = VC.v

    stf.close()
    if wq_out is not None:
        def fvqend(e):
            W(e, sSTq1, sSTq2)
            VC.inc(e.nop())
        _mkbody(p, blk, "vector", fvqend)


CK_EPS, CK_NHALF, CK_HALF, CK_NONE, CK_ONE = 0, 1, 2, 3, 4


def mk_consts(p, cfg, nkm, nkf):
    consts = {}
    CH = cfg.CH
    consts["ckb"] = p.sb("ckb", [128, 8], F32)
    consts["onesall"] = p.sb("onesall", [128, 128], BF16)
    consts["redsum"] = p.sb("redsum", [128, max(nkm, nkf)], F32)
    consts["gsb"] = p.sb("gsb", [128, 1], F32)
    consts["ginv_w1"] = p.sb("ginv_w1", [128, 1], F32)
    consts["ginv_w2"] = p.sb("ginv_w2", [128, 1], F32)
    consts["ps_small"] = p.ps("ps_small", [128, 128])
    consts["b1sb"] = p.sb("b1sb", [128, nkf], F32)
    consts["b2h"] = p.sb("b2h", [128, nkm], F32)
    consts["x2"] = [p.sb("x2_%d" % i, [128, CH], BF16)
                    for i in range(cfg.dm // 128)]
    consts["xbb"] = [p.sb("xbb_%d" % k, [128, CH], BF16) for k in range(nkm)]
    consts["vscr"] = p.sb("vscr", [128, CH], F32)
    consts["m_rep"] = p.sb("m_rep", [128, CH], F32)
    consts["r_rep"] = p.sb("r_rep", [128, CH], F32)
    consts["lntmp"] = p.sb("lntmp", [128, CH], F32)
    consts["lnps_s1"] = p.ps("lnps_s1", [128, CH])
    consts["lnps_s2"] = p.ps("lnps_s2", [128, CH])
    consts["CHmax"] = CH
    return consts


def emit_const_init(p, blk, consts):
    VC, SC, TCs = p.sem("VC"), p.sem("SC"), p.sem("TC")
    ckb = consts["ckb"]

    def f_init(e):
        e.memset(consts["onesall"][:, :], 1.0)
        e.memset(ckb[:, CK_EPS:CK_EPS + 1], LN_EPS)
        e.memset(ckb[:, CK_NHALF:CK_NHALF + 1], -0.5)
        e.memset(ckb[:, CK_HALF:CK_HALF + 1], 0.5)
        e.memset(ckb[:, CK_NONE:CK_NONE + 1], -1.0)
        ins = e.memset(ckb[:, CK_ONE:CK_ONE + 1], 1.0)
        VC.inc(ins)
    _mkbody(p, blk, "vector", f_init)

    def f_s(e):
        W(e, VC)
        SC.inc(e.nop())
    _mkbody(p, blk, "scalar", f_s)

    def f_t(e):
        W(e, VC)
        TCs.inc(e.nop())
    _mkbody(p, blk, "tensor", f_t)



def build_l1(cfg):
    p = Prog("l1")
    nc = p.nc
    dm, df, di, ds, dr = cfg.dm, cfg.df, cfg.di, cfg.ds, cfg.dr
    CH, Lq = cfg.CH, cfg.Lq
    Ls = Lq + 3
    nkm, nkf, nki = dm // 128, df // 128, di // 128
    ncH = Lq // CH

    x_in = nc.dram_tensor("x_in", [dm, Ls], F32, kind="ExternalInput").ap()
    hmask_in = nc.dram_tensor("hmask_in", [128, 1], F32, kind="ExternalInput").ap()
    w1t_in = nc.dram_tensor("w1t_in", [dm, df], F32, kind="ExternalInput").ap()
    w2t_in = nc.dram_tensor("w2t_in", [df, dm], F32, kind="ExternalInput").ap()
    b1_in = nc.dram_tensor("b1_in", [128, nkf], F32, kind="ExternalInput").ap()
    b2_in = nc.dram_tensor("b2_in", [128, nkm], F32, kind="ExternalInput").ap()
    wint_in = nc.dram_tensor("wint_in", [dm, 2 * di], F32, kind="ExternalInput").ap()
    cw_in = nc.dram_tensor("cw_in", [128, 4 * nki], F32, kind="ExternalInput").ap()
    cb_in = nc.dram_tensor("cb_in", [128, nki], F32, kind="ExternalInput").ap()
    wxt_in = nc.dram_tensor("wxt_in", [di, 128], F32, kind="ExternalInput").ap()
    wdtt_in = nc.dram_tensor("wdtt_in", [dr, di], F32, kind="ExternalInput").ap()
    dtb_in = nc.dram_tensor("dtb_in", [128, nki], F32, kind="ExternalInput").ap()
    eye_in = nc.dram_tensor("eye_in", [128, 128], F32, kind="ExternalInput").ap()

    xp_out = nc.dram_tensor("xp_out", [dm, Ls], F32, kind="ExternalOutput").ap()
    u_out = nc.dram_tensor("u_out", [di, Lq], BF16, kind="ExternalOutput").ap()
    sz_out = nc.dram_tensor("sz_out", [di, Lq], BF16, kind="ExternalOutput").ap()
    delta_out = nc.dram_tensor("delta_out", [di, Lq], F32, kind="ExternalOutput").ap()
    b_out = nc.dram_tensor("b_out", [ds, Lq], F32, kind="ExternalOutput").ap()
    c_out = nc.dram_tensor("c_out", [ds, Lq], F32, kind="ExternalOutput").ap()
    bmin_out = nc.dram_tensor("bmin_out", [ds, 1], F32, kind="ExternalOutput").ap()
    w1q_out = nc.dram_tensor("w1q_out", [dm, df], BF16, kind="ExternalOutput").ap()
    w2q_out = nc.dram_tensor("w2q_out", [df, dm], BF16, kind="ExternalOutput").ap()

    h_sp = nc.dram_tensor("h_sp", [df, Ls], BF16, kind="Internal").ap()
    x_sp = nc.dram_tensor("x_sp", [di, Ls], BF16, kind="Internal").ap()

    chunks = [(0, 3)] + [(3 + i * CH, CH) for i in range(ncH)]

    with p.ctx:
        blk = p.ctx.enter_context(nc.Block())
        VC, SC, TCs = p.sem("VC"), p.sem("SC"), p.sem("TC")
        sLDw = p.sem("sLDw")
        sLDc = p.sem("sLDc")

        consts = mk_consts(p, cfg, nkm, nkf)
        cwsb = p.sb("cwsb", [128, 4 * nki], F32)
        cbsb = p.sb("cbsb", [128, nki], F32)
        cbh = p.sb("cbh", [128, nki], F32)
        dtbsb = p.sb("dtbsb", [128, nki], F32)
        eyesb = p.sb("eyesb", [128, 128], F32)
        hmsb = p.sb("hmsb", [128, 1], F32)

        emit_const_init(p, blk, consts)

        def f_ldc(e):
            for src, dst in ((b1_in, consts["b1sb"]), (b2_in, consts["b2h"]),
                             (cw_in, cwsb), (cb_in, cbsb), (dtb_in, dtbsb),
                             (eye_in, eyesb), (hmask_in, hmsb)):
                ins = e.dma_start(out=dst[:, :], in_=src)
                sLDc.inc(ins, 16)
        _mkbody(p, blk, "sync", f_ldc)

        def f_small(e):
            W(e, sLDc)
            e.tensor_scalar(out=consts["b2h"][:, :], in0=consts["b2h"][:, :],
                            scalar1=0.5, scalar2=None, op0=ALU.mult)
            ins = e.tensor_scalar(out=cbh[:, :], in0=cbsb[:, :], scalar1=0.5,
                                  scalar2=None, op0=ALU.mult)
            VC.inc(ins)
        _mkbody(p, blk, "vector", f_small)

        emit_ffn(p, blk, cfg, x_in, xp_out, h_sp, w1t_in, w2t_in, chunks,
                 consts, "f1", 0.5, wq_out=(w1q_out, w2q_out))

        sSTof1p = [p.sem("sSTof1_%d" % i) for i in range(3)]

        def fvt12(e):
            W(e, *sSTof1p)
            VC.inc(e.nop())
        _mkbody(p, blk, "vector", fvt12)

        sLDx2 = p.sem("sLDx2")
        sSTx_p = [p.sem("sSTx__%d" % i) for i in range(3)]
        sSTsz = p.sem("sSTsz")
        with ExitStack() as st2:
            winb = [p.sb("winb_%d" % k, [128, 2 * di], BF16, st2) for k in range(nkm)]
            wscr = [p.sb("w2sc_0", [128, 2 * di], F32, st2)]
            emit_load_weight_cast(p, blk, wint_in, winb, wscr, dm, 2 * di, sLDw)
            xb = [p.sb("xb2_%d" % k, [128, CH], F32, st2) for k in range(nkm)]
            lnb = [p.sb("ln2_%d" % k, [128, CH], BF16, st2) for k in range(nkm)]
            ps_f = [p.ps("ps_w%d" % i, [128, CH], st2) for i in range(3)]
            xt = [p.sb("xt_%d" % i, [128, CH], BF16, st2) for i in range(3)]
            nki_l = cfg.di // 128
            zal = p.sb("zal", [128, nki_l * CH], BF16, st2)
            thal = p.sb("thal", [128, nki_l * CH], BF16, st2)
            szal = p.sb("szal", [128, nki_l * CH], BF16, st2)

            evb2 = {0: 0, 1: 0, 2: 0}
            stx2 = {0: 0, 1: 0, 2: 0}
            ztanh_prev = 0
            stsz_prev = 0
            prev2 = None
            for ci, (off, N) in enumerate(chunks):
                def fy(e, off=off, N=N, ci=ci, prev=prev2):
                    if prev is not None:
                        Wv(e, VC, prev["apply"])
                        Wv(e, SC, prev["sq"])
                    else:
                        W(e, SC, VC)
                        for s in sSTof1p:
                            e.wait_ge(s.h, s.v)
                    for k in range(nkm):
                        ins = e.dma_start(out=xb[k][:, :N],
                                          in_=xp_out[k * 128:(k + 1) * 128, off:off + N])
                        sLDx2.inc(ins, 16)
                _mkbody(p, blk, "sync", fy)
                ldt2 = sLDx2.v

                prev2 = emit_ln(p, blk, cfg, xb, lnb, N, consts,
                                ready=((sLDx2, ldt2),))

                m_tiles = nki if off == 0 else 2 * nki
                for m in range(m_tiles):
                    def ft(e, m=m, N=N, at=prev2["apply"], ev=evb2[m % 3]):
                        Wv(e, VC, max(at, ev))
                        for k in range(nkm):
                            ins = e.matmul(ps_f[m % 3][:, :N],
                                           winb[k][:, m * 128:(m + 1) * 128],
                                           lnb[k][:, :N],
                                           start=(k == 0), stop=(k == nkm - 1))
                        TCs.inc(ins)
                    _mkbody(p, blk, "tensor", ft)
                    mmt = TCs.v

                    if m < nki:
                        def fv(e, m=m, N=N, mmt=mmt, st=stx2[m % 3]):
                            Wv(e, TCs, mmt)
                            Wv(e, sSTx_p[m % 3], st)
                            ins = e.tensor_copy(out=xt[m % 3][:, :N],
                                                in_=ps_f[m % 3][:, :N])
                            VC.inc(ins)
                        _mkbody(p, blk, "vector", fv)
                        evb2[m % 3] = VC.v

                        def fg(e, m=m, off=off, N=N, ev=VC.v):
                            Wv(e, VC, ev)
                            ins = e.dma_start(
                                out=x_sp[m * 128:(m + 1) * 128, off:off + N],
                                in_=xt[m % 3][:, :N])
                            sSTx_p[m % 3].inc(ins, 16)
                        _mkbody(p, blk, "gpsimd", fg)
                        stx2[m % 3] = sSTx_p[m % 3].v
                    else:
                        def fv(e, m=m, N=N, mmt=mmt, zt=ztanh_prev):
                            Wv(e, TCs, mmt)
                            Wv(e, SC, zt)
                            mi = m - nki
                            ins = e.tensor_copy(out=zal[:, mi * CH:mi * CH + N],
                                                in_=ps_f[m % 3][:, :N])
                            VC.inc(ins)
                        _mkbody(p, blk, "vector", fv)
                        evb2[m % 3] = VC.v

                if m_tiles > nki:
                    def fs(e, N=N, ev=VC.v):
                        Wv(e, VC, ev)
                        ins = e.activation(out=thal[:, :], in_=zal[:, :],
                                           func=AF.Tanh,
                                           scale=consts["ckb"][:, CK_HALF:CK_HALF + 1])
                        SC.inc(ins)
                    _mkbody(p, blk, "scalar", fs)
                    ztanh_prev = SC.v

                    def fvz(e, N=N, sct=ztanh_prev, st=stsz_prev):
                        Wv(e, SC, sct)
                        Wv(e, sSTsz, st)
                        ins = e.tensor_scalar(out=thal[:, :], in0=thal[:, :],
                                              scalar1=0.5, scalar2=0.5,
                                              op0=ALU.mult, op1=ALU.add)
                        chn(p, e, "vector", ins)
                        ins = e.tensor_tensor(out=szal[:, :], in0=zal[:, :],
                                              in1=thal[:, :], op=ALU.mult)
                        VC.inc(ins)
                    _mkbody(p, blk, "vector", fvz)

                    def fgz(e, off=off, N=N, ev=VC.v):
                        Wv(e, VC, ev)
                        for mi in range(nki):
                            ins = e.dma_start(
                                out=sz_out[mi * 128:(mi + 1) * 128,
                                           off - 3:off - 3 + N],
                                in_=szal[:, mi * CH:mi * CH + N])
                            sSTsz.inc(ins, 16)
                    _mkbody(p, blk, "gpsimd", fgz)
                    stsz_prev = sSTsz.v

        sLDxw = p.sem("sLDxw")
        sSTu = p.sem("sSTu")
        with ExitStack() as st3:
            diag = [[p.sb("dg_%d_%d" % (j, k), [128, 128], BF16, st3)
                     for k in range(nki)] for j in range(4)]
            xw = [p.sb("xw_%d" % k, [128, CH + 3], BF16, st3) for k in range(nki)]
            ps_c = [p.ps("ps_c%d" % i, [128, CH], st3) for i in range(2)]
            ual = p.sb("ual", [128, nki * CH], BF16, st3)
            thal3 = p.sb("thal3", [128, nki * CH], BF16, st3)
            utal = p.sb("utal", [128, nki * CH], BF16, st3)

            def fdg(e):
                W(e, sLDc)
                for j in range(4):
                    for k in range(nki):
                        ins = e.tensor_scalar(
                            out=diag[j][k][:, :], in0=eyesb[:, :],
                            scalar1=cwsb[:, j * nki + k:j * nki + k + 1],
                            scalar2=None, op0=ALU.mult)
                VC.inc(ins)
            _mkbody(p, blk, "vector", fdg)

            evc3 = {0: 0, 1: 0}
            mm3_prev = 0
            tanh3_prev = 0
            stu_prev = 0
            dgt = VC.v
            for c in range(ncH):
                off = c * CH

                def fy(e, off=off, c=c, mmp=mm3_prev):
                    if c == 0:
                        W(e, TCs, VC)
                        for s in sSTx_p:
                            e.wait_ge(s.h, s.v)
                    else:
                        Wv(e, TCs, mmp)
                    for k in range(nki):
                        ins = e.dma_start(
                            out=xw[k][:, :],
                            in_=x_sp[k * 128:(k + 1) * 128, off:off + CH + 3])
                        sLDxw.inc(ins, 16)
                _mkbody(p, blk, "sync", fy)
                ldt3 = sLDxw.v

                if c == 0:
                    def fvm(e):
                        W(e, sLDxw)
                        for k in range(nki):
                            ins = e.tensor_scalar(out=xw[k][:, 0:3], in0=xw[k][:, 0:3],
                                                  scalar1=hmsb[:, 0:1], scalar2=None,
                                                  op0=ALU.mult)
                        VC.inc(ins)
                    _mkbody(p, blk, "vector", fvm)
                    dgt = VC.v

                for k in range(nki):
                    def ft(e, k=k, ldt=ldt3, wt=max(dgt, evc3[k % 2])):
                        Wv(e, sLDxw, ldt)
                        Wv(e, VC, wt)
                        for j in range(4):
                            ins = e.matmul(ps_c[k % 2][:, :CH], diag[j][k][:, :],
                                           xw[k][:, j:j + CH],
                                           start=(j == 0), stop=(j == 3))
                        TCs.inc(ins)
                    _mkbody(p, blk, "tensor", ft)
                    mmt = TCs.v

                    def fv(e, k=k, mmt=mmt, zt=tanh3_prev):
                        Wv(e, TCs, mmt)
                        Wv(e, SC, zt)
                        ins = e.tensor_scalar(out=ual[:, k * CH:(k + 1) * CH],
                                              in0=ps_c[k % 2][:, :CH],
                                              scalar1=cbsb[:, k:k + 1],
                                              scalar2=None, op0=ALU.add)
                        VC.inc(ins)
                    _mkbody(p, blk, "vector", fv)
                    evc3[k % 2] = VC.v
                mm3_prev = TCs.v

                def fs(e, ev=VC.v):
                    Wv(e, VC, ev)
                    ins = e.activation(out=thal3[:, :], in_=ual[:, :],
                                       func=AF.Tanh,
                                       scale=consts["ckb"][:, CK_HALF:CK_HALF + 1])
                    SC.inc(ins)
                _mkbody(p, blk, "scalar", fs)
                tanh3_prev = SC.v

                def fvu(e, sct=tanh3_prev, st=stu_prev):
                    Wv(e, SC, sct)
                    Wv(e, sSTu, st)
                    ins = e.tensor_scalar(out=thal3[:, :], in0=thal3[:, :],
                                          scalar1=0.5, scalar2=0.5,
                                          op0=ALU.mult, op1=ALU.add)
                    chn(p, e, "vector", ins)
                    ins = e.tensor_tensor(out=utal[:, :], in0=ual[:, :],
                                          in1=thal3[:, :], op=ALU.mult)
                    VC.inc(ins)
                _mkbody(p, blk, "vector", fvu)

                def fg(e, off=off, ev=VC.v):
                    Wv(e, VC, ev)
                    for k in range(nki):
                        ins = e.dma_start(
                            out=u_out[k * 128:(k + 1) * 128, off:off + CH],
                            in_=utal[:, k * CH:(k + 1) * CH])
                        sSTu.inc(ins, 16)
                _mkbody(p, blk, "gpsimd", fg)
                stu_prev = sSTu.v

        sLDu = p.sem("sLDu")
        sSTd = p.sem("sSTd")
        sSTbc = p.sem("sSTbc")
        dxw = 128
        with ExitStack() as st4:
            wxb = [p.sb("wxb_%d" % k, [128, dxw], BF16, st4) for k in range(nki)]
            wdtb = p.sb("wdtb", [128, di], BF16, st4)
            wscr4 = [p.sb("w4sc_%d" % i, [128, max(dxw, di)], F32, st4)
                     for i in range(2)]
            emit_load_weight_cast(p, blk, wxt_in, wxb, wscr4, di, dxw, sLDw)

        
            def fyw(e):
                W(e, p.sem("VC"))
                ins = e.dma_start(out=wscr4[0][0:dr, :di], in_=wdtt_in)
                sLDw.inc(ins, 16)
            _mkbody(p, blk, "sync", fyw)

            def fvw(e):
                W(e, sLDw)
                ins = e.memset(wdtb[:, :], 0.0)
                chn(p, e, "vector", ins)
                ins = e.tensor_copy(out=wdtb[0:dr, :], in_=wscr4[0][0:dr, :di])
                p.sem("VC").inc(ins)
            _mkbody(p, blk, "vector", fvw)

            ub = [p.sb("ub_%d" % k, [128, CH], BF16, st4) for k in range(nki)]
            ps_x = p.ps("ps_x", [dxw, CH], st4)
            ps_d = [p.ps("ps_d%d" % i, [128, CH], st4) for i in range(2)]
            dtr = p.sb("dtr", [128, CH], BF16, st4)
            Bt = p.sb("Bt", [ds, CH], F32, st4)
            Ct = p.sb("Ct", [ds, CH], F32, st4)
            chmin = p.sb("chmin", [ds, 1], F32, st4)
            bmin_run = p.sb("bmin_run", [ds, 1], F32, st4)
            dtal = p.sb("dtal", [128, nki * CH], F32, st4)
            spl = p.sb("spl", [128, nki * CH], F32, st4)

            def fv0(e):
                e.memset(bmin_run[:, :], 3.0e38)
                ins = e.memset(dtr[:, :], 0.0)
                VC.inc(ins)
            _mkbody(p, blk, "vector", fv0)

            mm4_prev = 0
            dtr_prev = 0
            bc_prev = 0
            stbc_prev = 0
            spl_prev = 0
            std_prev = 0
            evd4 = {0: 0, 1: 0}
            wrdy = VC.v
            for c in range(ncH):
                off = c * CH

                def fy(e, off=off, c=c, mmp=mm4_prev):
                    if c == 0:
                        W(e, TCs)
                        e.wait_ge(sSTu.h, sSTu.v)
                    else:
                        Wv(e, TCs, mmp)
                    for k in range(nki):
                        ins = e.dma_start(out=ub[k][:, :],
                                          in_=u_out[k * 128:(k + 1) * 128, off:off + CH])
                        sLDu.inc(ins, 16)
                _mkbody(p, blk, "sync", fy)
                ldt4 = sLDu.v

                def ft(e, ldt=ldt4, sct=dtr_prev, vct=max(wrdy, bc_prev)):
                    Wv(e, sLDu, ldt)
                    Wv(e, SC, sct)
                    Wv(e, VC, vct)
                    for k in range(nki):
                        ins = e.matmul(ps_x[:, :CH], wxb[k][:, :], ub[k][:, :],
                                       start=(k == 0), stop=(k == nki - 1))
                    TCs.inc(ins)
                _mkbody(p, blk, "tensor", ft)
                mmx = TCs.v
                mm4_prev = mmx

                def fs(e, mmt=mmx):
                    Wv(e, TCs, mmt)
                    ins = e.activation(out=dtr[0:dr, :], in_=ps_x[0:dr, :CH],
                                       func=AF.Identity)
                    SC.inc(ins)
                _mkbody(p, blk, "scalar", fs)
                dtr_prev = SC.v

                def fv(e, mmt=mmx, st=stbc_prev):
                    Wv(e, TCs, mmt)
                    Wv(e, sSTbc, st)
                    e.tensor_copy(out=Bt[:, :], in_=ps_x[64:64 + ds, :CH])
                    ins = e.tensor_copy(out=Ct[:, :], in_=ps_x[96:96 + ds, :CH])
                    chn(p, e, "vector", ins)
                    ins = e.tensor_reduce(out=chmin[:, :], in_=Bt[:, :], axis=AX.X,
                                          op=ALU.min)
                    chn(p, e, "vector", ins)
                    ins = e.tensor_tensor(out=bmin_run[:, :], in0=bmin_run[:, :],
                                          in1=chmin[:, :], op=ALU.min)
                    VC.inc(ins)
                _mkbody(p, blk, "vector", fv)
                bc_prev = VC.v

                def fg(e, off=off, ev=VC.v):
                    Wv(e, VC, ev)
                    ins = e.dma_start(out=b_out[:, off:off + CH], in_=Bt[:, :])
                    sSTbc.inc(ins, 16)
                    ins = e.dma_start(out=c_out[:, off:off + CH], in_=Ct[:, :])
                    sSTbc.inc(ins, 16)
                _mkbody(p, blk, "gpsimd", fg)
                stbc_prev = sSTbc.v

                for m in range(nki):
                    def ft2(e, m=m, sct=dtr_prev, ev=evd4[m % 2]):
                        Wv(e, SC, sct)
                        Wv(e, VC, max(ev, wrdy))
                        ins = e.matmul(ps_d[m % 2][:, :CH],
                                       wdtb[:, m * 128:(m + 1) * 128], dtr[:, :],
                                       start=True, stop=True)
                        TCs.inc(ins)
                    _mkbody(p, blk, "tensor", ft2)
                    mmt2 = TCs.v

                    def fv2(e, m=m, mmt=mmt2, sct=spl_prev, st=std_prev):
                        Wv(e, TCs, mmt)
                        Wv(e, SC, sct)
                        Wv(e, sSTd, st)
                        ins = e.tensor_scalar(out=dtal[:, m * CH:(m + 1) * CH],
                                              in0=ps_d[m % 2][:, :CH],
                                              scalar1=dtbsb[:, m:m + 1],
                                              scalar2=None, op0=ALU.add)
                        VC.inc(ins)
                    _mkbody(p, blk, "vector", fv2)
                    evd4[m % 2] = VC.v

                def fs2(e, ev=VC.v):
                    Wv(e, VC, ev)
                    ckb = consts["ckb"]
                    ins = e.activation(out=spl[:, :], in_=dtal[:, :], func=AF.Abs)
                    chn(p, e, "scalar", ins)
                    ins = e.activation(out=spl[:, :], in_=spl[:, :], func=AF.Exp,
                                       scale=ckb[:, CK_NONE:CK_NONE + 1])
                    chn(p, e, "scalar", ins)
                    ins = e.activation(out=spl[:, :], in_=spl[:, :], func=AF.Ln,
                                       bias=ckb[:, CK_ONE:CK_ONE + 1])
                    chn(p, e, "scalar", ins)
                    ins = e.activation(out=dtal[:, :], in_=dtal[:, :], func=AF.Relu)
                    SC.inc(ins)
                _mkbody(p, blk, "scalar", fs2)
                spl_prev = SC.v

                def fv3(e, sct=spl_prev, st=std_prev):
                    Wv(e, SC, sct)
                    Wv(e, sSTd, st)
                    ins = e.tensor_tensor(out=dtal[:, :], in0=dtal[:, :],
                                          in1=spl[:, :], op=ALU.add)
                    chn(p, e, "vector", ins)
                    ins = e.tensor_scalar(out=dtal[:, :], in0=dtal[:, :],
                                          scalar1=1e-4, scalar2=20.0,
                                          op0=ALU.max, op1=ALU.min)
                    VC.inc(ins)
                _mkbody(p, blk, "vector", fv3)

                def fg2(e, off=off, ev=VC.v):
                    Wv(e, VC, ev)
                    for m in range(nki):
                        ins = e.dma_start(
                            out=delta_out[m * 128:(m + 1) * 128, off:off + CH],
                            in_=dtal[:, m * CH:(m + 1) * CH])
                        sSTd.inc(ins, 16)
                _mkbody(p, blk, "gpsimd", fg2)
                std_prev = sSTd.v

            def fgend(e):
                W(e, VC)
                ins = e.dma_start(out=bmin_out, in_=bmin_run[:, :])
                sSTd.inc(ins, 16)
                for nm in ("sSTd", "sSTbc", "sSTu", "sSTsz", "sSTx_",
                           "sSTof1_0", "sSTof1_1", "sSTof1_2",
                           "sSThf1_0", "sSThf1_1", "sSThf1_2"):
                    s = p.sem(nm)
                    e.wait_ge(s.h, s.v)
            _mkbody(p, blk, "gpsimd", fgend)

    return p



def build_l2(cfg):
    p = Prog("l2")
    nc = p.nc
    ds, TC_, L = cfg.ds, cfg.TC, cfg.L
    nmc = L // TC_
    SUB = min(512, TC_)
    nsub = TC_ // SUB
    ndblk, nbat = cfg.ndblk, cfg.nb
    rows = nbat * ndblk * 128

    delta_in = nc.dram_tensor("delta_in", [rows, L], F32, kind="ExternalInput").ap()
    u_in = nc.dram_tensor("u_in", [rows, L], BF16, kind="ExternalInput").ap()
    sz_in = nc.dram_tensor("sz_in", [rows, L], BF16, kind="ExternalInput").ap()
    b_in = nc.dram_tensor("b_in", [nbat * ds, L], F32, kind="ExternalInput").ap()
    c_in = nc.dram_tensor("c_in", [nbat * ds, L], F32, kind="ExternalInput").ap()
    alog_in = nc.dram_tensor("alog_in", [ndblk * 128, ds], F32, kind="ExternalInput").ap()
    d_in = nc.dram_tensor("d_in", [128, ndblk], F32, kind="ExternalInput").ap()
    bmins_in = nc.dram_tensor("bmins_in", [128, cfg.nc * ds], F32, kind="ExternalInput").ap()
    sel_in = nc.dram_tensor("sel_in", [128, ds * 128], F32, kind="ExternalInput").ap()
    y_out = nc.dram_tensor("y_out", [rows, L], BF16, kind="ExternalOutput").ap()

    with p.ctx:
        blk = p.ctx.enter_context(nc.Block())
        VC, SC, TCe, GC = p.sem("VC"), p.sem("SC"), p.sem("TC"), p.sem("GC")
        sLD = p.sem("sLD")
        sLDc = p.sem("sLDc")
        sSTy = p.sem("sSTy")

        negA = [p.sb("negA_%d" % d, [128, ds], F32) for d in range(ndblk)]
        Dsb = p.sb("Dsb", [128, ndblk], F32)
        bmrow = p.sb("bmrow", [128, cfg.nc * ds], F32)
        beta_sb = p.sb("beta_sb", [128, 1], F32)
        sel = p.sb("sel", [128, ds * 128], BF16)
        self_f = p.sb("self_f", [128, ds * 128], F32)
        ps_bc = [p.ps("ps_bc%d" % i, [128, 2 * TC_]) for i in range(2)]

        dw = [[p.sb("dw_%d_%d" % (d, i), [128, TC_], F32) for i in range(2)]
              for d in range(ndblk)]
        uw = [[p.sb("uw_%d_%d" % (d, i), [128, TC_], BF16) for i in range(2)]
              for d in range(ndblk)]
        szw = [[p.sb("szw_%d_%d" % (d, i), [128, TC_], BF16) for i in range(2)]
               for d in range(ndblk)]
        Bwf = p.sb("Bwf", [ds, TC_], F32)
        Cwf = p.sb("Cwf", [ds, TC_], F32)
        Bw = [p.sb("Bw_%d" % i, [128, TC_], BF16) for i in range(2)]
        Cw = [p.sb("Cw_%d" % i, [128, TC_], BF16) for i in range(2)]
        du = [p.sb("du_%d" % d, [128, TC_], F32) for d in range(ndblk)]
        AB = 8
        a_b2 = [p.sb("a_b%d" % i, [128, AB * TC_], F32) for i in range(2)]
        b_b = p.sb("b_b", [128, TC_], F32)
        bcD = [p.sb("bcD_%d" % d, [128, TC_], F32) for d in range(ndblk)]
        uds = p.sb("uds", [128, TC_], F32)
        xc = p.sb("xc", [128, TC_], F32)
        wvb = [p.sb("wvb_%d" % i, [128, ndblk * TC_], BF16) for i in range(2)]
        yac = [p.sb("yac_%d" % d, [128, TC_], F32) for d in range(ndblk)]
        ygt = [p.sb("ygt_%d" % i, [128, TC_], BF16) for i in range(2)]
        state = p.sb("state", [128, nbat * ndblk * ds], F32)

        def fy0(e):
            for d in range(ndblk):
                ins = e.dma_start(out=negA[d][:, :],
                                  in_=alog_in[d * 128:(d + 1) * 128, :])
                sLDc.inc(ins, 16)
            ins = e.dma_start(out=Dsb[:, :], in_=d_in)
            sLDc.inc(ins, 16)
            ins = e.dma_start(out=bmrow[:, :], in_=bmins_in)
            sLDc.inc(ins, 16)
            ins = e.dma_start(out=self_f[:, :], in_=sel_in)
            sLDc.inc(ins, 16)
        _mkbody(p, blk, "sync", fy0)

        def fs0(e):
            W(e, sLDc)
            for d in range(ndblk):
                ins = e.activation(out=negA[d][:, :], in_=negA[d][:, :], func=AF.Exp)
            SC.inc(ins)
        _mkbody(p, blk, "scalar", fs0)

        def fv0(e):
            W(e, SC, sLDc)
            for d in range(ndblk):
                e.tensor_scalar(out=negA[d][:, :], in0=negA[d][:, :],
                                scalar1=-1.0, scalar2=None, op0=ALU.mult)
            e.tensor_copy(out=sel[:, :], in_=self_f[:, :])
            e.memset(Bw[0][:, :], 0.0)
            e.memset(Bw[1][:, :], 0.0)
            e.memset(Cw[0][:, :], 0.0)
            e.memset(Cw[1][:, :], 0.0)
            ins = e.tensor_reduce(out=beta_sb[:, :], in_=bmrow[:, :], axis=AX.X,
                                  op=ALU.min)
            chn(p, e, "vector", ins)
            e.tensor_scalar(out=beta_sb[:, :], in0=beta_sb[:, :], scalar1=-1.0,
                            scalar2=0.0, op0=ALU.mult, op1=ALU.max)
            for d in range(ndblk):
                e.memset(yac[d][:, :], 0.0)
            ins = e.memset(b_b[:, :], 0.0)
            chn(p, e, "vector", ins)
            for d in range(ndblk):
                e.tensor_scalar(out=bcD[d][:, :], in0=b_b[:, :],
                                scalar1=Dsb[:, d:d + 1], scalar2=None,
                                op0=ALU.add)
            ins = e.memset(state[:, :], 0.0)
            VC.inc(ins)
        _mkbody(p, blk, "vector", fv0)

        fv0t = VC.v
        gsz = 8 // ndblk
        sSTyp = [p.sem("sSTy_%d" % i) for i in range(2)]
        sc_last = {0: 0, 1: 0}
        vc2_last = {0: 0, 1: 0}
        tc_last = {0: 0, 1: 0}
        fv1_prev = 0
        gc_prev = 0
        ab_tick = {0: fv0t, 1: fv0t}
        psbc_tick = {0: fv0t, 1: fv0t}
        gc_tick = {0: 0, 1: 0}
        sty_tick = {0: 0, 1: 0}

        for b in range(nbat):
            for mc in range(nmc):
                par = mc % 2
                t0 = mc * TC_

                def fy(e, b=b, par=par, t0=t0, sct=sc_last[par],
                       vct=fv1_prev, gct=vc2_last[par]):
                    Wv(e, SC, sct)
                    Wv(e, VC, vct)
                    Wv(e, GC, gct)
                    for d in range(ndblk):
                        r0 = (b * ndblk + d) * 128
                        ins = e.dma_start(out=dw[d][par][:, :],
                                          in_=delta_in[r0:r0 + 128, t0:t0 + TC_])
                        sLD.inc(ins, 16)
                        ins = e.dma_start(out=uw[d][par][:, :],
                                          in_=u_in[r0:r0 + 128, t0:t0 + TC_])
                        sLD.inc(ins, 16)
                        ins = e.dma_start(out=szw[d][par][:, :],
                                          in_=sz_in[r0:r0 + 128, t0:t0 + TC_])
                        sLD.inc(ins, 16)
                    ins = e.dma_start(out=Bwf[:, :],
                                      in_=b_in[b * ds:(b + 1) * ds, t0:t0 + TC_])
                    sLD.inc(ins, 16)
                    ins = e.dma_start(out=Cwf[:, :],
                                      in_=c_in[b * ds:(b + 1) * ds, t0:t0 + TC_])
                    sLD.inc(ins, 16)
                _mkbody(p, blk, "sync", fy)
                ldt = sLD.v

                def fv1(e, par=par, ldt=ldt, tct=tc_last[par]):
                    Wv(e, sLD, ldt)
                    Wv(e, TCe, tct)
                    e.tensor_scalar(out=Bw[par][0:ds, :], in0=Bwf[:, :],
                                    scalar1=beta_sb[0:ds, 0:1], scalar2=None,
                                    op0=ALU.add)
                    e.tensor_copy(out=Cw[par][0:ds, :], in_=Cwf[:, :])
                    for d in range(ndblk):
                        ins = e.tensor_tensor(out=du[d][:, :], in0=dw[d][par][:, :],
                                              in1=uw[d][par][:, :], op=ALU.mult)
                    VC.inc(ins)
                _mkbody(p, blk, "vector", fv1)
                fv1t = VC.v

                for n in range(ds):
                    if n % gsz == 0:
                        g = n // gsz
                        ab = a_b2[g % 2]

                        def fsb(e, n=n, par=par, ab=ab, wt=ab_tick[g % 2],
                                ldt=ldt):
                            Wv(e, VC, wt)
                            Wv(e, sLD, ldt)
                            i0 = n * ndblk
                            for j in range(8):
                                i = i0 + j
                                nn, dd = divmod(i, ndblk)
                                ins = e.activation(
                                    out=ab[:, j * TC_:(j + 1) * TC_],
                                    in_=dw[dd][par][:, :], func=AF.Exp,
                                    scale=negA[dd][:, nn:nn + 1])
                            SC.inc(ins)
                        _mkbody(p, blk, "scalar", fsb)
                        sbt = SC.v

                    def ft(e, n=n, par=par, wt=max(fv1t, psbc_tick[n % 2])):
                        Wv(e, VC, wt)
                        pb = ps_bc[n % 2]
                        for s in range(nsub):
                            e.matmul(pb[:, s * SUB:(s + 1) * SUB],
                                     sel[:, n * 128:(n + 1) * 128],
                                     Bw[par][:, s * SUB:(s + 1) * SUB],
                                     start=True, stop=True)
                            ins = e.matmul(
                                pb[:, TC_ + s * SUB:TC_ + (s + 1) * SUB],
                                sel[:, n * 128:(n + 1) * 128],
                                Cw[par][:, s * SUB:(s + 1) * SUB],
                                start=True, stop=True)
                        TCe.inc(ins)
                    _mkbody(p, blk, "tensor", ft)
                    mmt = TCe.v

                    ab = a_b2[(n // gsz) % 2]

                    def fv(e, n=n, b=b, ab=ab, mmt=mmt, sbt=sbt,
                           gct=gc_tick[n % 2]):
                        Wv(e, TCe, mmt)
                        Wv(e, SC, sbt)
                        Wv(e, GC, gct)
                        pb = ps_bc[n % 2]
                        for d in range(ndblk):
                            ji = (n * ndblk + d) % 8
                            ins = e.tensor_tensor(out=b_b[:, :], in0=pb[:, :TC_],
                                                  in1=du[d][:, :], op=ALU.mult)
                            chn(p, e, "vector", ins)
                            sidx = (b * ndblk + d) * ds + n
                            ins = e.tensor_tensor_scan(out=xc[:, :],
                                                       data0=ab[:, ji * TC_:(ji + 1) * TC_],
                                                       data1=b_b[:, :],
                                                       initial=state[:, sidx:sidx + 1],
                                                       op0=ALU.mult, op1=ALU.add)
                            chn(p, e, "vector", ins)
                            e.tensor_copy(out=state[:, sidx:sidx + 1],
                                          in_=xc[:, TC_ - 1:TC_])
                            ins = e.tensor_tensor(
                                out=wvb[n % 2][:, d * TC_:(d + 1) * TC_],
                                in0=xc[:, :], in1=pb[:, TC_:], op=ALU.mult)
                            if d == ndblk - 1:
                                VC.inc(ins)
                            else:
                                chn(p, e, "vector", ins)
                    _mkbody(p, blk, "vector", fv)
                    psbc_tick[n % 2] = VC.v
                    if (n + 1) % gsz == 0:
                        ab_tick[(n // gsz) % 2] = VC.v

                    def fg(e, n=n, ev=VC.v):
                        Wv(e, VC, ev)
                        for d in range(ndblk):
                            ins = e.tensor_tensor(
                                out=yac[d][:, :], in0=yac[d][:, :],
                                in1=wvb[n % 2][:, d * TC_:(d + 1) * TC_],
                                op=ALU.add)
                        GC.inc(ins)
                    _mkbody(p, blk, "gpsimd", fg)
                    gc_tick[n % 2] = GC.v

                sc_last[par] = SC.v
                tc_last[par] = TCe.v
                fv1_prev = fv1t
                gc_prev = GC.v

                for d in range(ndblk):
                    def fg2(e, d=d, b=b, t0=t0, par=par, st=sty_tick[d % 2]):
                        Wv(e, sSTyp[d % 2], st)
                        ins = e.tensor_tensor(out=uds[:, :], in0=uw[d][par][:, :],
                                              in1=bcD[d][:, :], op=ALU.mult)
                        chn(p, e, "gpsimd", ins)
                        ins = e.tensor_tensor(out=yac[d][:, :], in0=yac[d][:, :],
                                              in1=uds[:, :], op=ALU.add)
                        chn(p, e, "gpsimd", ins)
                        ins = e.tensor_tensor(out=ygt[d % 2][:, :], in0=yac[d][:, :],
                                              in1=szw[d][par][:, :], op=ALU.mult)
                        chn(p, e, "gpsimd", ins)
                        r0 = (b * ndblk + d) * 128
                        ins = e.dma_start(out=y_out[r0:r0 + 128, t0:t0 + TC_],
                                          in_=ygt[d % 2][:, :])
                        sSTyp[d % 2].inc(ins, 16)
                        ins = e.memset(yac[d][:, :], 0.0)
                        GC.inc(ins)
                    _mkbody(p, blk, "gpsimd", fg2)
                    sty_tick[d % 2] = sSTyp[d % 2].v
                vc2_last[par] = GC.v

        def fgend(e):
            for s in sSTyp:
                e.wait_ge(s.h, s.v)
        _mkbody(p, blk, "gpsimd", fgend)

    return p



def build_l3(cfg):
    p = Prog("l3")
    nc = p.nc
    dm, df, di = cfg.dm, cfg.df, cfg.di
    CH, Lq = cfg.CH, cfg.Lq
    nkm, nkf, nki = dm // 128, df // 128, di // 128
    ncH = Lq // CH

    y_in = nc.dram_tensor("y_in", [di, Lq], BF16, kind="ExternalInput").ap()
    xp_in = nc.dram_tensor("xp_in", [dm, Lq], F32, kind="ExternalInput").ap()
    woutt_in = nc.dram_tensor("woutt_in", [di, dm], F32, kind="ExternalInput").ap()
    w1q_in = nc.dram_tensor("w1q_in", [dm, df], BF16, kind="ExternalInput").ap()
    w2q_in = nc.dram_tensor("w2q_in", [df, dm], BF16, kind="ExternalInput").ap()
    b1_in = nc.dram_tensor("b1_in", [128, nkf], F32, kind="ExternalInput").ap()
    b2_in = nc.dram_tensor("b2_in", [128, nkm], F32, kind="ExternalInput").ap()

    x3_out = nc.dram_tensor("x3_out", [dm, Lq], F32, kind="ExternalOutput").ap()
    xpp_sp = nc.dram_tensor("xpp_sp", [dm, Lq], F32, kind="Internal").ap()
    h_sp = nc.dram_tensor("h_sp3", [df, Lq], BF16, kind="Internal").ap()

    chunks = [(i * CH, CH) for i in range(ncH)]

    with p.ctx:
        blk = p.ctx.enter_context(nc.Block())
        VC, SC, TCs = p.sem("VC"), p.sem("SC"), p.sem("TC")
        sLDw, sLDc = p.sem("sLDw"), p.sem("sLDc")
        sLDy = p.sem("sLDy")

        consts = mk_consts(p, cfg, nkm, nkf)
        emit_const_init(p, blk, consts)

        def f_ldc(e):
            ins = e.dma_start(out=consts["b1sb"][:, :], in_=b1_in)
            sLDc.inc(ins, 16)
            ins = e.dma_start(out=consts["b2h"][:, :], in_=b2_in)
            sLDc.inc(ins, 16)
        _mkbody(p, blk, "sync", f_ldc)

        def f_small(e):
            W(e, sLDc)
            ins = e.tensor_scalar(out=consts["b2h"][:, :], in0=consts["b2h"][:, :],
                                  scalar1=0.5, scalar2=None, op0=ALU.mult)
            VC.inc(ins)
        _mkbody(p, blk, "vector", f_small)

        sSTxppp = [p.sem("sSTxpp_%d" % i) for i in range(3)]
        with ExitStack() as st1:
            wob = [p.sb("wob_%d" % k, [128, dm], BF16, st1) for k in range(nki)]
            wscr = [p.sb("wsc1_%d" % i, [128, dm], F32, st1) for i in range(2)]
            emit_load_weight_cast(p, blk, woutt_in, wob, wscr, di, dm, sLDw)
            yb = [p.sb("yb_%d" % k, [128, CH], BF16, st1) for k in range(nki)]
            xpb = [p.sb("xpb_%d" % k, [128, CH], F32, st1) for k in range(nkm)]
            ps_o = [p.ps("ps_o%d" % i, [128, CH], st1) for i in range(3)]
            xppt = [p.sb("xppt_%d" % i, [128, CH], F32, st1) for i in range(3)]

            evo = {0: 0, 1: 0, 2: 0}
            sto = {0: 0, 1: 0, 2: 0}
            mmo_prev = 0
            evo_prev = 0
            wrdy1 = p.sem("VC").v
            for (off, N) in chunks:
                def fy(e, off=off, N=N, mmp=mmo_prev, evp=evo_prev):
                    if mmp == 0:
                        W(e, TCs, VC)
                    else:
                        Wv(e, TCs, mmp)
                        Wv(e, VC, evp)
                    for k in range(nki):
                        ins = e.dma_start(out=yb[k][:, :N],
                                          in_=y_in[k * 128:(k + 1) * 128, off:off + N])
                        sLDy.inc(ins, 16)
                    for k in range(nkm):
                        ins = e.dma_start(out=xpb[k][:, :N],
                                          in_=xp_in[k * 128:(k + 1) * 128, off:off + N])
                        sLDy.inc(ins, 16)
                _mkbody(p, blk, "sync", fy)
                ldt1 = sLDy.v

                for m in range(nkm):
                    def ft(e, m=m, N=N, ldt=ldt1, wt=max(wrdy1, evo[m % 3])):
                        Wv(e, sLDy, ldt)
                        Wv(e, VC, wt)
                        for k in range(nki):
                            ins = e.matmul(ps_o[m % 3][:, :N],
                                           wob[k][:, m * 128:(m + 1) * 128],
                                           yb[k][:, :N],
                                           start=(k == 0), stop=(k == nki - 1))
                        TCs.inc(ins)
                    _mkbody(p, blk, "tensor", ft)
                    mmt = TCs.v

                    def fv(e, m=m, N=N, mmt=mmt, st=sto[m % 3], ldt=ldt1):
                        Wv(e, TCs, mmt)
                        Wv(e, sLDy, ldt)
                        Wv(e, sSTxppp[m % 3], st)
                        ins = e.tensor_tensor(out=xppt[m % 3][:, :N],
                                              in0=ps_o[m % 3][:, :N],
                                              in1=xpb[m][:, :N], op=ALU.add)
                        VC.inc(ins)
                    _mkbody(p, blk, "vector", fv)
                    evo[m % 3] = p.sem("VC").v

                    def fg(e, m=m, off=off, N=N, ev=evo[m % 3]):
                        Wv(e, VC, ev)
                        ins = e.dma_start(
                            out=xpp_sp[m * 128:(m + 1) * 128, off:off + N],
                            in_=xppt[m % 3][:, :N])
                        sSTxppp[m % 3].inc(ins, 16)
                    _mkbody(p, blk, "gpsimd", fg)
                    sto[m % 3] = sSTxppp[m % 3].v
                mmo_prev = TCs.v
                evo_prev = p.sem("VC").v

        def fybar(e):
            for s in sSTxppp:
                e.wait_ge(s.h, s.v)
        _mkbody(p, blk, "sync", fybar)

        emit_ffn(p, blk, cfg, xpp_sp, x3_out, h_sp, None, None, chunks,
                 consts, "f2", 0.5, wq_src=(w1q_in, w2q_in))

        def fgend(e):
            for nm in ("sSTof2_0", "sSTof2_1", "sSTof2_2",
                       "sSThf2_0", "sSThf2_1", "sSThf2_2",
                       "sSTxpp_0", "sSTxpp_1", "sSTxpp_2"):
                s = p.sem(nm)
                e.wait_ge(s.h, s.v)
        _mkbody(p, blk, "gpsimd", fgend)

    return p



def _t(a):
    return np.ascontiguousarray(np.asarray(a).T)


def stage_l1(cfg, inputs):
    x = np.asarray(inputs["L0_attn_hidden_states"])
    W1 = np.asarray(inputs["L0_ffn_W1"])
    W2 = np.asarray(inputs["L0_ffn_W2"])
    b1 = np.asarray(inputs["L0_ffn_b1"])
    b2 = np.asarray(inputs["L0_ffn_b2"])
    Win = np.asarray(inputs["L0_attn_W_in"])
    cw = np.asarray(inputs["L0_attn_conv_w"])[:, 0, :]
    cb = np.asarray(inputs["L0_attn_conv_b"])
    Wx = np.asarray(inputs["L0_attn_W_x"])
    Wdt = np.asarray(inputs["L0_attn_W_dt"])
    dtb = np.asarray(inputs["L0_attn_dt_bias"])

    dm, df, di = cfg.dm, cfg.df, cfg.di
    nkf, nkm, nki = df // 128, dm // 128, di // 128
    w1t, w2t, wint, wdtt = _t(W1), _t(W2), _t(Win), _t(Wdt)
    wxt_raw = _t(Wx)
    wxt = np.zeros((di, 128), np.float32)
    wxt[:, 0:cfg.dr] = wxt_raw[:, 0:cfg.dr]
    wxt[:, 64:64 + cfg.ds] = wxt_raw[:, cfg.dr:cfg.dr + cfg.ds]
    wxt[:, 96:96 + cfg.ds] = wxt_raw[:, cfg.dr + cfg.ds:cfg.dr + 2 * cfg.ds]
    b1m = np.ascontiguousarray(b1.reshape(nkf, 128).T)
    b2m = np.ascontiguousarray(b2.reshape(nkm, 128).T)
    cwm = np.ascontiguousarray(
        cw.reshape(nki, 128, 4).transpose(1, 2, 0).reshape(128, 4 * nki))
    cbm = np.ascontiguousarray(cb.reshape(nki, 128).T)
    dtbm = np.ascontiguousarray(dtb.reshape(nki, 128).T)
    eye = np.eye(128, dtype=np.float32)

    maps = []
    for c in range(cfg.nc):
        b, q = divmod(c, cfg.nq)
        xt = _t(x[b])
        qs = q * cfg.Lq
        slab = np.zeros((dm, cfg.Lq + 3), np.float32)
        slab[:, 3:] = xt[:, qs:qs + cfg.Lq]
        if q > 0:
            slab[:, 0:3] = xt[:, qs - 3:qs]
        hmask = np.full((128, 1), 0.0 if q == 0 else 1.0, np.float32)
        maps.append(dict(x_in=slab, hmask_in=hmask, w1t_in=w1t, w2t_in=w2t,
                         b1_in=b1m, b2_in=b2m, wint_in=wint, cw_in=cwm,
                         cb_in=cbm, wxt_in=wxt, wdtt_in=wdtt, dtb_in=dtbm,
                         eye_in=eye))
    return maps


def stage_l2(cfg, inputs, l1res):
    A_log = np.asarray(inputs["L0_attn_A_log"])
    Dv = np.asarray(inputs["L0_attn_D"])

    def seq(key):
        return [np.concatenate([np.asarray(l1res[b * cfg.nq + q][key])
                                for q in range(cfg.nq)], axis=1)
                for b in range(cfg.nb)]

    delta_b, u_b, sz_b = seq("delta_out"), seq("u_out"), seq("sz_out")
    B_b, C_b = seq("b_out"), seq("c_out")
    bmins = np.concatenate([np.asarray(l1res[c]["bmin_out"]).reshape(-1)
                            for c in range(cfg.nc)]).reshape(1, -1)
    sel_pat = np.zeros((128, cfg.ds * 128), np.float32)
    for n in range(cfg.ds):
        sel_pat[n, n * 128:(n + 1) * 128] = 1.0
    bmins = np.ascontiguousarray(np.broadcast_to(bmins, (128, bmins.shape[1])))
    b_cat = np.ascontiguousarray(np.concatenate(B_b, axis=0))
    c_cat = np.ascontiguousarray(np.concatenate(C_b, axis=0))

    maps = []
    for s in range(cfg.nc):
        r0 = s * cfg.dpc

        def sl(arrs):
            return np.ascontiguousarray(
                np.concatenate([arrs[b][r0:r0 + cfg.dpc, :]
                                for b in range(cfg.nb)], axis=0))

        alog_s = np.ascontiguousarray(A_log[r0:r0 + cfg.dpc, :])
        d_s = np.ascontiguousarray(Dv[r0:r0 + cfg.dpc].reshape(cfg.ndblk, 128).T)
        maps.append(dict(delta_in=sl(delta_b), u_in=sl(u_b), sz_in=sl(sz_b),
                         b_in=b_cat, c_in=c_cat, alog_in=alog_s, d_in=d_s,
                         bmins_in=bmins, sel_in=sel_pat))
    return maps


def stage_l3(cfg, inputs, l1res, l2res):
    Wout = np.asarray(inputs["L0_attn_W_out"])
    b1 = np.asarray(inputs["L0_ffn_b1"])
    b2 = np.asarray(inputs["L0_ffn_b2"])
    dm, df, di = cfg.dm, cfg.df, cfg.di
    nkf, nkm = df // 128, dm // 128
    woutt = _t(Wout)
    w1q = np.asarray(l1res[0]["w1q_out"])
    w2q = np.asarray(l1res[0]["w2q_out"])
    b1m = np.ascontiguousarray(b1.reshape(nkf, 128).T)
    b2m = np.ascontiguousarray(b2.reshape(nkm, 128).T)

    y_full = []
    for b in range(cfg.nb):
        yb = np.zeros((di, cfg.L), dtype=ml_dtypes.bfloat16)
        for s in range(cfg.nc):
            ys = np.asarray(l2res[s]["y_out"])
            for d in range(cfg.ndblk):
                plane = b * cfg.ndblk + d
                yb[s * cfg.dpc + d * 128: s * cfg.dpc + (d + 1) * 128, :] = \
                    ys[plane * 128:(plane + 1) * 128, :]
        y_full.append(yb)

    maps = []
    for c in range(cfg.nc):
        b, q = divmod(c, cfg.nq)
        qs = q * cfg.Lq
        y_c = np.ascontiguousarray(y_full[b][:, qs:qs + cfg.Lq])
        xp_c = np.ascontiguousarray(np.asarray(l1res[c]["xp_out"])[:, 3:])
        maps.append(dict(y_in=y_c, xp_in=xp_c, woutt_in=woutt, w1q_in=w1q,
                         w2q_in=w2q, b1_in=b1m, b2_in=b2m))
    return maps


_CACHE = {}


def _progs(cfg):
    key = (cfg.dm, cfg.Lq)
    if key not in _CACHE:
        _CACHE[key] = (build_l1(cfg), build_l2(cfg), build_l3(cfg))
    return _CACHE[key]


def kernel(**inputs):
    cfg = FULL
    p1, p2, p3 = _progs(cfg)
    cores = list(range(cfg.nc))

    m1 = stage_l1(cfg, inputs)
    r1 = run_bass_kernel_spmd(p1.nc, m1, core_ids=cores).results
    m2 = stage_l2(cfg, inputs, r1)
    r2 = run_bass_kernel_spmd(p2.nc, m2, core_ids=cores).results
    m3 = stage_l3(cfg, inputs, r1, r2)
    r3 = run_bass_kernel_spmd(p3.nc, m3, core_ids=cores).results

    out = np.empty((cfg.nb, cfg.L, cfg.dm), np.float32)
    for c in range(cfg.nc):
        b, q = divmod(c, cfg.nq)
        out[b, q * cfg.Lq:(q + 1) * cfg.Lq, :] = np.asarray(r3[c]["x3_out"]).T
    return out
